# version 08
# baseline (speedup 1.0000x reference)
import numpy as np
import ml_dtypes
import concourse.bass as bass
import concourse.mybir as mybir
from concourse.bass_utils import run_bass_kernel_spmd

F32 = mybir.dt.float32
BF16 = mybir.dt.bfloat16
ALU = mybir.AluOpType
ACT = mybir.ActivationFunctionType
AXX = mybir.AxisListType.X

B_, L_, D_ = 4, 4096, 1024
T = 2048
NTC = 16
Q = 128
H, P, NS = 32, 64, 64
E = 2048
CD = 2176
DIP = 4256
NEX = 36
EPS = 1e-5
EPSI = 1e-6
TILE = 256
NTILE = T // TILE
NEC = 17


class Sems:
    def __init__(self, nc, names):
        self.h, self.n, self._g = {}, {}, []
        for nm in names:
            g = nc.semaphore(nm)
            self.h[nm] = g.__enter__()
            self._g.append(g)
            self.n[nm] = 0

    def inc(self, instr, nm, k=1):
        instr.then_inc(self.h[nm], k)
        self.n[nm] += k
        return instr

    def wait(self, eng, nm, val=None):
        eng.wait_ge(self.h[nm], self.n[nm] if val is None else val)


def build_program(debug=()):
    nc = bass.Bass("TRN2", target_bir_lowering=False)
    d = {}

    def din(name, shape):
        d[name] = nc.dram_tensor(name, list(shape), F32, kind="ExternalInput").ap()

    din("xin", (T, D_)); din("xaux", (T, D_))
    din("winT", (D_, DIP)); din("winTa", (D_, CD + H))
    din("lnrows", (1, 4 * D_))
    din("cw", (128, NEC * 4)); din("cb", (128, NEC))
    din("cwa", (128, NEC * 4)); din("cba", (128, NEC))
    din("dtb", (H, 1)); din("alog", (H, 1)); din("dtba", (H, 1)); din("aloga", (H, 1))
    din("dexp", (128, 16)); din("nwp", (128, 16))
    din("woutT", (E, D_)); din("gwT", (D_, H)); din("ebias", (1, H))
    din("upT", (NEX, D_, 512)); din("dnT", (NEX, 512, D_))
    yout = nc.dram_tensor("yout", [T, D_], F32, kind="ExternalOutput").ap()

    x_sp = nc.dram_tensor("x_sp", [16, 128, T], BF16, kind="Internal").ap()
    xa_sp = nc.dram_tensor("xa_sp", [16, 128, T], BF16, kind="Internal").ap()
    sz_sp = nc.dram_tensor("sz_sp", [16, 128, T], BF16, kind="Internal").ap()
    x1_sp = nc.dram_tensor("x1_sp", [T, D_], F32, kind="Internal").ap()

    dbg = {}
    dbg_shapes = {"u": (128, 8 * T), "dtv": (H, T), "acinc": (H, T),
                  "Hinit": (NS, E), "Hend": (NS, E), "y0": (128, E),
                  "x1": (128, NTC * D_), "scores": (128, NTC * H),
                  "comb": (128, NTC * H), "macc": (128, NTC * D_)}
    for nm in debug:
        dbg[nm] = nc.dram_tensor("dbg_" + nm, list(dbg_shapes[nm]), F32,
                                 kind="ExternalOutput").ap()

    S = Sems(nc, ["TC", "VC", "SC", "GC", "SYD", "GPD", "ST"])
    guards = []

    def sbuf(name, shape, dt, **kw):
        g = nc.sbuf_tensor(name, shape, dt, **kw)
        t = g.__enter__()
        guards.append(g)
        return t

    c_strict = sbuf("c_strict", [128, 128], BF16)
    c_umask = sbuf("c_umask", [128, 128], BF16)
    c_idf = sbuf("c_idf", [128, 128], F32)
    identb = sbuf("identb", [128, 128], BF16)
    c_ind = sbuf("c_ind", [32, 2048], BF16)
    ones1b = sbuf("ones1b", [1, 128], BF16)
    ones1f = sbuf("ones1f", [1, 128], F32)
    onescol = sbuf("onescol", [128, 1], BF16)
    c_eb = sbuf("c_eb", [128, H], F32)
    iotaE = sbuf("iotaE", [128, H], F32)
    cw_s = sbuf("cw_s", [128, NEC * 4], F32)
    cb_s = sbuf("cb_s", [128, NEC], F32)
    cwa_s = sbuf("cwa_s", [128, NEC * 4], F32)
    cba_s = sbuf("cba_s", [128, NEC], F32)
    dexp_s = sbuf("dexp_s", [128, 16], F32)
    nwp_s = sbuf("nwp_s", [128, 16], F32)
    ebrow = sbuf("ebrow", [1, H], F32)
    gwf = sbuf("gwf", [128, 8 * H], F32)
    gwh = sbuf("gwh", [128, 8 * H], BF16)
    gwl = sbuf("gwl", [128, 8 * H], BF16)
    scorestk = sbuf("scorestk", [128, NTC * H], F32)
    combB = sbuf("combB", [128, NTC * H], F32)
    combTb = sbuf("combTb", [NEX, T], BF16)
    stat = sbuf("stat", [128, 8], F32)
    sqscr = sbuf("sqscr", [128, D_], F32)
    rmsrow = sbuf("rmsrow", [1, 256], F32)
    rscr = sbuf("rscr", [128, 40], F32)
    rscr2 = sbuf("rscr2", [128, 40], F32)
    cvals = sbuf("cvals", [128, 2], F32)

    psA = nc.psum_tensor("psA", [128, 2048], F32).__enter__()
    psB = nc.psum_tensor("psB", [128, 2048], F32).__enter__()


    arena = []

    def push(name, shape, dt, **kw):
        g = nc.sbuf_tensor(name, shape, dt, **kw)
        t = g.__enter__()
        arena.append(g)
        return t

    def mark():
        return len(arena)

    def pop_to(m):
        while len(arena) > m:
            arena.pop().__exit__(None, None, None)

    m_ab = mark()
    pk1 = push("pk1", [64, T], F32)
    pk2 = push("pk2", [64, T], F32)
    pk3 = push("pk3", [64, T], F32)
    pk4 = push("pk4", [64, T], F32)
    pk5 = push("pk5", [64, T], F32)
    dtraw, dtrawA = pk1[0:32, :], pk1[32:64, :]
    dtv, dtvA = pk2[0:32, :], pk2[32:64, :]
    acinc, acincA = pk3[0:32, :], pk3[32:64, :]
    w2m, w2a = pk4[0:32, :], pk4[32:64, :]
    scr32, scr32A = pk5[0:32, :], pk5[32:64, :]
    apack = push("apack", [64, 1], F32)
    dtpack = push("dtpack", [64, 1], F32)
    acol, acola = apack[0:32, :], apack[32:64, :]
    dtb_s, dtba_s = dtpack[0:32, :], dtpack[32:64, :]
    erow = push("erow", [H, T], BF16)
    lam64 = push("lam64", [64, NTC * H], F32)
    lam64A = push("lam64A", [64, NTC * H], F32)
    dtvT = push("dtvT", [128, NTC * H], F32)
    dtvTA = push("dtvTA", [128, NTC * H], F32)
    w2T = push("w2T", [128, NTC * H], F32)
    w2TA = push("w2TA", [128, NTC * H], F32)
    bcm = push("bcm", [128, T], BF16)
    bcma = push("bcma", [128, T], BF16)
    h0 = push("h0", [NS, E], F32)
    h1 = push("h1", [NS, E], F32)
    HB = [h0, h1]

    blk = nc.Block().__enter__()
    TT, VV, SS, GG, SY = blk.tensor, blk.vector, blk.scalar, blk.gpsimd, blk.sync

    def consts_g(g):
        for nm, dst in [("cw", cw_s), ("cb", cb_s), ("cwa", cwa_s), ("cba", cba_s),
                        ("dexp", dexp_s), ("nwp", nwp_s), ("dtb", dtb_s),
                        ("dtba", dtba_s), ("ebias", ebrow), ("alog", acol),
                        ("aloga", acola)]:
            S.inc(g.dma_start(dst[:, :], d[nm]), "GPD", 16)
        S.inc(g.dma_start(gwf[:, :].rearrange("p (k h) -> p k h", k=8),
                          d["gwT"].rearrange("(k p) h -> p k h", p=128)), "GPD", 16)
        g.memset(ones1b[:, :], 1.0)
        g.memset(ones1f[:, :], 1.0)
        g.memset(onescol[:, :], 1.0)
        g.memset(c_strict[:, :], 1.0)
        g.memset(c_umask[:, :], 1.0)
        g.memset(c_idf[:, :], 1.0)
        g.memset(identb[:, :], 1.0)
        g.memset(combTb[32:36, :], 1.0)
        g.drain()
        g.affine_select(c_strict[:, :], c_strict[:, :], [[-1, 128]], ALU.is_ge, 0.0,
                        base=-1, channel_multiplier=1)
        g.affine_select(c_umask[:, :], c_umask[:, :], [[1, 128]], ALU.is_ge, 0.0,
                        channel_multiplier=-1)
        g.affine_select(c_idf[:, :], c_idf[:, :], [[-1, 128]], ALU.is_equal, 0.0,
                        channel_multiplier=1)
        g.drain()
        g.affine_select(identb[:, :], identb[:, :], [[-1, 128]], ALU.is_equal, 0.0,
                        channel_multiplier=1)
        g.memset(cvals[:, 0:1], EPS)
        g.memset(cvals[:, 1:2], 1.0)
        g.memset(c_ind[:, :], 1.0)
        g.drain()
        g.affine_select(c_ind[:, :], c_ind[:, :], [[1, 2048]], ALU.is_ge, 0.0,
                        channel_multiplier=-64)
        g.drain()
        g.affine_select(c_ind[:, :], c_ind[:, :], [[-1, 2048]], ALU.is_ge, 0.0,
                        base=63, channel_multiplier=64)
        g.iota(iotaE[:, :], [[1, H]], base=0, channel_multiplier=0,
               allow_small_or_imprecise_dtypes=True)
        S.inc(g.memset(h0[:, :], 0.0), "GC")
    GG(consts_g)

    def consts_v(v):
        S.wait(v, "GC")
        S.wait(v, "GPD")
        v.tensor_copy(gwh[:, :], gwf[:, :])
        v.drain()
        S.inc(v.tensor_tensor(gwl[:, :], gwf[:, :], gwh[:, :], ALU.subtract), "VC")
    VV(consts_v)

    def consts_s(sc):
        S.wait(sc, "GPD")
        sc.activation(acol[:, :], acol[:, :], ACT.Exp)
        S.inc(sc.activation(acola[:, :], acola[:, :], ACT.Exp), "SC")
    SS(consts_s)

    def consts_v2(v):
        S.wait(v, "SC")
        v.tensor_scalar_mul(acol[:, :], acol[:, :], -1.0)
        S.inc(v.tensor_scalar_mul(acola[:, :], acola[:, :], -1.0), "VC")
    VV(consts_v2)

    def eb_t(t):
        S.wait(t, "VC")
        S.inc(t.matmul(psA[:, 0:H], ones1f[0:1, :], ebrow[0:1, :], start=True,
                       stop=True), "TC")
    TT(eb_t)

    def eb_v(v):
        S.wait(v, "TC")
        S.inc(v.scalar_tensor_tensor(c_eb[:, :], iotaE[:, :], -EPSI, psA[:, 0:H],
                                     ALU.mult, ALU.add), "VC")
    VV(eb_v)

    def fence():
        for body in [TT, VV, SS, GG, SY]:
            def f(e):
                for nm in ["TC", "VC", "SC", "GC", "SYD", "GPD"]:
                    S.wait(e, nm)
            body(f)

    def ln_chunk(src_ap, dst_ap):
        def v1(v):
            S.wait(v, "SC")
            v.reduce_sum(stat[:, 1:2], src_ap, axis=AXX)
            v.drain()
            S.inc(v.tensor_scalar_mul(stat[:, 2:3], stat[:, 1:2], 1.0 / D_), "VC")
        VV(v1)

        def s1(sc):
            S.wait(sc, "VC")
            S.inc(sc.activation(sqscr[:, :], src_ap, ACT.Square,
                                accum_out=stat[:, 0:1]), "SC")
        SS(s1)

        def v2(v):
            S.wait(v, "SC")
            v.tensor_scalar_mul(stat[:, 3:4], stat[:, 0:1], 1.0 / D_)
            v.drain()
            v.tensor_tensor(stat[:, 4:5], stat[:, 2:3], stat[:, 2:3], ALU.mult)
            v.drain()
            S.inc(v.tensor_tensor(stat[:, 4:5], stat[:, 3:4], stat[:, 4:5],
                                  ALU.subtract), "VC")
        VV(v2)

        def s2(sc):
            S.wait(sc, "VC")
            sc.activation(stat[:, 5:6], stat[:, 4:5], ACT.Ln, bias=cvals[:, 0:1])
            S.inc(sc.activation(stat[:, 6:7], stat[:, 5:6], ACT.Exp, scale=-0.5),
                  "SC")
        SS(s2)

        def v3(v):
            S.wait(v, "SC")
            S.inc(v.tensor_scalar(dst_ap, src_ap, stat[:, 2:3], stat[:, 6:7],
                                  ALU.subtract, ALU.mult), "VC")
        VV(v3)

    m_a = mark()
    ufm = push("ufm", [128, 8 * T], BF16, is_transpose_dest=True)
    wins = push("wins", [128, 3 * 8 * 128], BF16)
    gbB = push("gbB", [128, 4 * D_], F32)
    gbrow = push("gbrow", [1, D_], F32)
    xtile = push("xtile", [128, D_], F32)
    utile = push("utile", [128, D_], F32)
    ubtile = push("ubtile", [128, D_], BF16)
    fbuf0 = push("fbuf0", [128, T + 3], F32)
    fbuf1 = push("fbuf1", [128, T + 3], F32)
    cacc0 = push("cacc0", [128, T], F32)
    ubz0 = push("ubz0", [128, T], BF16)
    ubz1 = push("ubz1", [128, T], BF16)
    if True:

        for i in range(4):
            def gb_load(g, i=i):
                S.wait(g, "TC")
                S.inc(g.dma_start(gbrow[:, :], d["lnrows"][0:1, i * D_:(i + 1) * D_]),
                      "GPD", 16)
            GG(gb_load)
            gpd_r = S.n["GPD"]

            def bc_t(t, i=i, gpd_r=gpd_r):
                S.wait(t, "GPD", gpd_r)
                S.wait(t, "VC")
                t.matmul(psA[:, 0:512], ones1f[0:1, :],
                         gbrow[0:1, 0:512], start=True, stop=True)
                S.inc(t.matmul(psA[:, 512:1024], ones1f[0:1, :],
                               gbrow[0:1, 512:1024], start=True,
                               stop=True), "TC")
            TT(bc_t)

            def bc_v(v, i=i):
                S.wait(v, "TC")
                S.inc(v.tensor_copy(gbB[:, i * D_:(i + 1) * D_], psA[:, 0:1024]),
                      "VC")
            VV(bc_v)

        def phase_ln1(src_dram, goff):
            for tc in range(NTC):
                def ld(y, tc=tc):
                    S.wait(y, "VC")
                    S.wait(y, "SC")
                    S.inc(y.dma_start(xtile[:, :],
                                      src_dram[tc * 128:(tc + 1) * 128, :]),
                          "SYD", 16)
                SY(ld)

                ln_chunk(xtile[:, :], utile[:, :])

                def v3(v, goff=goff):
                    S.wait(v, "SYD")
                    v.tensor_tensor(utile[:, :], utile[:, :],
                                    gbB[:, goff * D_:(goff + 1) * D_], ALU.mult)
                    S.inc(v.tensor_tensor(ubtile[:, :], utile[:, :],
                                          gbB[:, (goff + 1) * D_:(goff + 2) * D_],
                                          ALU.add), "VC")
                VV(v3)

                def tr(y, tc=tc):
                    S.wait(y, "VC")
                    for kd in range(8):
                        S.inc(y.dma_start(
                            ufm[:, kd * T + tc * 128: kd * T + (tc + 1) * 128],
                            ubtile[:, kd * 128:(kd + 1) * 128], transpose=True),
                            "SYD", 16)
                SY(tr)

        def phase_ln1_fixed(src_dram, goff):
            for tc in range(NTC):
                def ld(y, tc=tc):
                    S.wait(y, "VC")
                    S.wait(y, "SC")
                    S.inc(y.dma_start(xtile[:, :],
                                      src_dram[tc * 128:(tc + 1) * 128, :]),
                          "SYD", 16)
                SY(ld)
                syd_x = S.n["SYD"]

                def s1(sc, syd_x=syd_x):
                    S.wait(sc, "SYD", syd_x)
                    S.inc(sc.activation(sqscr[:, :], xtile[:, :], ACT.Square,
                                        accum_out=stat[:, 0:1]), "SC")
                SS(s1)

                def v1(v, syd_x=syd_x):
                    S.wait(v, "SYD", syd_x)
                    v.reduce_sum(stat[:, 1:2], xtile[:, :], axis=AXX)
                    v.drain()
                    S.wait(v, "SC")
                    v.tensor_scalar_mul(stat[:, 2:3], stat[:, 1:2], 1.0 / D_)
                    v.tensor_scalar_mul(stat[:, 3:4], stat[:, 0:1], 1.0 / D_)
                    v.drain()
                    v.tensor_tensor(stat[:, 4:5], stat[:, 2:3], stat[:, 2:3], ALU.mult)
                    v.drain()
                    S.inc(v.tensor_tensor(stat[:, 4:5], stat[:, 3:4], stat[:, 4:5],
                                          ALU.subtract), "VC")
                VV(v1)

                def s2(sc):
                    S.wait(sc, "VC")
                    sc.activation(stat[:, 5:6], stat[:, 4:5], ACT.Ln, bias=cvals[:, 0:1])
                    S.inc(sc.activation(stat[:, 6:7], stat[:, 5:6], ACT.Exp,
                                        scale=-0.5), "SC")
                SS(s2)

                def v2(v, goff=goff):
                    S.wait(v, "SC")
                    S.wait(v, "SYD")
                    v.tensor_scalar(utile[:, :], xtile[:, :], stat[:, 2:3],
                                    stat[:, 6:7], ALU.subtract, ALU.mult)
                    v.drain()
                    v.tensor_tensor(utile[:, :], utile[:, :],
                                    gbB[:, goff * D_:(goff + 1) * D_], ALU.mult)
                    v.drain()
                    S.inc(v.tensor_tensor(ubtile[:, :], utile[:, :],
                                          gbB[:, (goff + 1) * D_:(goff + 2) * D_],
                                          ALU.add), "VC")
                VV(v2)

                def tr(y, tc=tc):
                    S.wait(y, "VC")
                    for kd in range(8):
                        S.inc(y.dma_start(
                            ufm[:, kd * T + tc * 128: kd * T + (tc + 1) * 128],
                            ubtile[:, kd * 128:(kd + 1) * 128], transpose=True),
                            "SYD", 16)
                SY(tr)

        def phase_a2(winT_dram, necols, spill, bc_dst, dtraw_dst, zch, cw_t, cb_t):
            dt_bp = dtraw_dst.base_partition()
            nec_all = necols // 128
            rem = necols - nec_all * 128
            n_ec = nec_all + (1 if rem else 0)
            src3 = winT_dram.rearrange("(k p) e -> p k e", p=128)

            def zt(v):
                v.memset(fbuf0[:, 0:3], 0.0)
                S.inc(v.memset(fbuf1[:, 0:3], 0.0), "VC")
            VV(zt)
            ufm_done = S.n["SYD"]

            wslab = [wins[:, s * 1024:(s + 1) * 1024]
                         .rearrange("p (k e) -> p k e", e=128) for s in range(3)]
            slab_gpd = {}

            def wl0(g):
                for s in range(min(3, n_ec)):
                    rows = 128 if s < nec_all else rem
                    S.inc(g.dma_start(wslab[s][:, :, 0:rows],
                                      src3[:, :, s * 128:s * 128 + rows]),
                          "GPD", 16)
                    slab_gpd[s] = S.n["GPD"]
            GG(wl0)

            last_evac = {psl: ("VC", S.n["VC"]) for psl in range(4)}
            fb_free = [None, None]
            cc_free = [None]
            ub_free = [None, None]

            for ec in range(n_ec):
                mm_rows = 128 if ec < nec_all else rem
                slot = ec % 3
                gpd_here = slab_gpd[ec]
                par = ec % 2
                fb = [fbuf0, fbuf1][par]
                cc = cacc0
                ub = [ubz0, ubz1][par]
                evac_cnts = []

                for ti in range(NTILE):
                    psl = ti % 4
                    if ec < nec_all:
                        sl = psA[0:mm_rows, psl * 512:psl * 512 + TILE]
                    else:
                        sl = psA[dt_bp:dt_bp + mm_rows, psl * 512:psl * 512 + TILE]

                    def mm(t, ec=ec, ti=ti, sl=sl, mm_rows=mm_rows, slot=slot,
                           gpd_here=gpd_here, ufm_done=ufm_done, psl=psl):
                        S.wait(t, "GPD", gpd_here)
                        S.wait(t, "SYD", ufm_done)
                        if psl in last_evac and last_evac[psl] is not None:
                            S.wait(t, last_evac[psl][0], last_evac[psl][1])
                        ins = None
                        for kd in range(8):
                            ins = t.matmul(
                                sl, wslab[slot][:, kd, 0:mm_rows],
                                ufm[:, kd * T + ti * TILE: kd * T + (ti + 1) * TILE],
                                start=(kd == 0), stop=(kd == 7))
                        S.inc(ins, "TC")
                    TT(mm)
                    tc_here = S.n["TC"]
                    if ti == NTILE - 1 and ec + 3 < n_ec:
                        def wln(g, ec=ec, slot=slot, tc_here=tc_here):
                            S.wait(g, "TC", tc_here)
                            rows = 128 if ec + 3 < nec_all else rem
                            S.inc(g.dma_start(
                                wslab[slot][:, :, 0:rows],
                                src3[:, :, (ec + 3) * 128:(ec + 3) * 128 + rows]),
                                "GPD", 16)
                        GG(wln)
                        slab_gpd[ec + 3] = S.n["GPD"]

                    if ec < nec_all:
                        def ev(v, ti=ti, sl=sl, tc_here=tc_here, fb=fb, par=par):
                            S.wait(v, "TC", tc_here)
                            if ti == 0 and fb_free[par] is not None:
                                S.wait(v, fb_free[par][0], fb_free[par][1])
                            S.inc(v.tensor_copy(
                                fb[:, 3 + ti * TILE: 3 + (ti + 1) * TILE], sl),
                                "VC")
                        VV(ev)
                    else:
                        def ev(v, ti=ti, sl=sl, tc_here=tc_here,
                               dtraw_dst=dtraw_dst):
                            S.wait(v, "TC", tc_here)
                            S.inc(v.tensor_copy(
                                dtraw_dst[:, ti * TILE:(ti + 1) * TILE], sl), "VC")
                        VV(ev)
                    last_evac[psl] = ("VC", S.n["VC"])

                if ec >= nec_all:
                    continue

                if ec < zch:
                    def zsil(sc, fb=fb, ub=ub, par=par):
                        S.wait(sc, "VC")
                        if ub_free[par] is not None:
                            S.wait(sc, ub_free[par][0], ub_free[par][1])
                        S.inc(sc.activation(ub[:, :], fb[:, 3:3 + T], ACT.Silu),
                              "SC")
                    SS(zsil)
                    fb_free[par] = ("SC", S.n["SC"])

                    def zsp(y, ec=ec, ub=ub):
                        S.wait(y, "SC")
                        S.inc(y.dma_start(sz_sp[ec, :, :], ub[:, :]), "SYD", 16)
                    SY(zsp)
                    ub_free[par] = ("SYD", S.n["SYD"])
                else:
                    xc = ec - zch
                    def cv(v, fb=fb, cc=cc, xc=xc, par=par, cw_t=cw_t):
                        if cc_free[0] is not None:
                            S.wait(v, cc_free[0][0], cc_free[0][1])
                        v.tensor_scalar(cc[:, :], fb[:, 0:T],
                                        cw_t[:, xc * 4:xc * 4 + 1], None, ALU.mult)
                        ins = None
                        for j in range(1, 4):
                            v.drain()
                            ins = v.scalar_tensor_tensor(
                                cc[:, :], fb[:, j:j + T],
                                cw_t[:, xc * 4 + j:xc * 4 + j + 1], cc[:, :],
                                ALU.mult, ALU.add)
                        S.inc(ins, "VC")
                    VV(cv)
                    fb_free[par] = ("VC", S.n["VC"])

                    def csil(sc, cc=cc, ub=ub, par=par, xc=xc, cb_t=cb_t):
                        S.wait(sc, "VC")
                        if ub_free[par] is not None:
                            S.wait(sc, ub_free[par][0], ub_free[par][1])
                        S.inc(sc.activation(ub[:, :], cc[:, :], ACT.Silu,
                                            bias=cb_t[:, xc:xc + 1]), "SC")
                    SS(csil)
                    cc_free[0] = ("SC", S.n["SC"])

                    if xc < 16:
                        def xsp(y, xc=xc, ub=ub, spill=spill):
                            S.wait(y, "SC")
                            S.inc(y.dma_start(spill[xc, :, :], ub[:, :]),
                                  "SYD", 16)
                        SY(xsp)
                        ub_free[par] = ("SYD", S.n["SYD"])
                    else:
                        def bcp(v, ub=ub, bc_dst=bc_dst):
                            S.wait(v, "SC")
                            S.inc(v.tensor_copy(bc_dst[:, :], ub[:, :]), "VC")
                        VV(bcp)
                        ub_free[par] = ("VC", S.n["VC"])

        def phase_a3(dtr, dtb_t, a_t, dv, ac, w2_, l64, dvT, w2T_, er, scr):
            bp3 = dtr.base_partition()

            def sp1(sc):
                S.wait(sc, "VC")
                sc.activation(scr[:, :], dtr[:, :], ACT.Abs, bias=dtb_t[:, 0:1])
                sc.activation(scr[:, :], scr[:, :], ACT.Exp, scale=-1.0)
                S.inc(sc.activation(scr[:, :], scr[:, :], ACT.Ln,
                                    bias=cvals[bp3:bp3 + 32, 1:2]), "SC")
            SS(sp1)

            def sp2(v):
                S.wait(v, "SC")
                v.tensor_scalar(dv[:, :], dtr[:, :], dtb_t[:, 0:1], 0.0,
                                ALU.add, ALU.max)
                v.drain()
                v.tensor_tensor(dv[:, :], dv[:, :], scr[:, :], ALU.add)
                v.drain()
                ins = None
                v.tensor_scalar(scr[:, :], dv[:, :], a_t[:, 0:1], None, ALU.mult)
                v.drain()
                for c in range(NTC):
                    ins = v.tensor_tensor_scan(ac[:, c * Q:(c + 1) * Q],
                                               scr[:, c * Q:(c + 1) * Q],
                                               scr[:, c * Q:(c + 1) * Q], 0.0,
                                               ALU.add, ALU.bypass)
                S.inc(ins, "VC")
            VV(sp2)

            if er is not None:
                def sp4(sc):
                    S.wait(sc, "VC")
                    S.inc(sc.activation(er[:, :], ac[:, :], ACT.Exp), "SC")
                SS(sp4)

            for c in range(NTC):
                def trp(t, c=c):
                    S.wait(t, "VC")
                    S.wait(t, "SC")
                    bp = ac.base_partition()
                    t.transpose(psB[0:1, 64:96], ac[:, (c + 1) * Q - 1:(c + 1) * Q],
                                c_idf[bp:bp + 32, bp:bp + 32])
                    bpd = dv.base_partition()
                    S.inc(t.transpose(psB[:, 32:64], dv[:, c * Q:(c + 1) * Q],
                                      c_idf[bpd:bpd + 32, bpd:bpd + 32]), "TC")
                TT(trp)

                def tev(v, c=c, dvT=dvT):
                    S.wait(v, "TC")
                    v.tensor_copy(dvT[:, c * H:(c + 1) * H], psB[:, 32:64])
                    S.inc(v.tensor_copy(rmsrow[0:1, 0:32], psB[0:1, 64:96]), "VC")
                VV(tev)

                def lse(sc, c=c):
                    S.wait(sc, "VC")
                    S.inc(sc.activation(rmsrow[0:1, 32:64], rmsrow[0:1, 0:32],
                                        ACT.Exp), "SC")
                SS(lse)

                def lmm(t, c=c):
                    S.wait(t, "SC")
                    S.inc(t.matmul(psB[0:64, 64:96], ones1f[0:1, 0:64],
                                   rmsrow[0:1, 32:64], start=True, stop=True), "TC")
                TT(lmm)

                def lev(v, c=c, l64=l64):
                    S.wait(v, "TC")
                    S.inc(v.tensor_copy(l64[:, c * H:(c + 1) * H], psB[0:64, 64:96]),
                          "VC")
                VV(lev)

                def w2v(v, c=c, ac=ac):
                    S.inc(v.tensor_scalar(scr[:, c * Q:(c + 1) * Q],
                                          ac[:, c * Q:(c + 1) * Q],
                                          ac[:, (c + 1) * Q - 1:(c + 1) * Q], None,
                                          ALU.subtract), "VC")
                VV(w2v)

                def w2e(sc, c=c):
                    S.wait(sc, "VC")
                    S.inc(sc.activation(scr[:, c * Q:(c + 1) * Q],
                                        scr[:, c * Q:(c + 1) * Q], ACT.Exp,
                                        scale=-1.0), "SC")
                SS(w2e)

                def w2f(v, c=c, dv=dv, w2_=w2_):
                    S.wait(v, "SC")
                    S.inc(v.tensor_tensor(w2_[:, c * Q:(c + 1) * Q],
                                          dv[:, c * Q:(c + 1) * Q],
                                          scr[:, c * Q:(c + 1) * Q], ALU.mult),
                          "VC")
                VV(w2f)

                def w2t(t, c=c, w2_=w2_):
                    S.wait(t, "VC")
                    bpw = w2_.base_partition()
                    S.inc(t.transpose(psB[:, 96:128], w2_[:, c * Q:(c + 1) * Q],
                                      c_idf[bpw:bpw + 32, bpw:bpw + 32]), "TC")
                TT(w2t)

                def w2tv(v, c=c, w2T_=w2T_):
                    S.wait(v, "TC")
                    S.inc(v.tensor_copy(w2T_[:, c * H:(c + 1) * H], psB[:, 96:128]),
                          "VC")
                VV(w2tv)

        phase_ln1_fixed(d["xaux"], 2)
        phase_a2(d["winTa"], CD + H, xa_sp, bcma, dtrawA, 0, cwa_s, cba_s)
        fence()
        phase_ln1_fixed(d["xin"], 0)
        phase_a2(d["winT"], DIP, x_sp, bcm, dtraw, 16, cw_s, cb_s)
        fence()
        phase_a3(dtrawA, dtba_s, acola, dtvA, acincA, w2a, lam64A, dtvTA, w2TA,
                 None, scr32A)
        phase_a3(dtraw, dtb_s, acol, dtv, acinc, w2m, lam64, dtvT, w2T, erow, scr32)
        fence()
        if "u" in dbg:
            def du(g):
                S.inc(g.dma_start(dbg["u"], ufm[:, :]), "ST", 16)
            GG(du)
        if "dtv" in dbg:
            def dd(y):
                S.inc(y.dma_start(dbg["dtv"], dtv[:, :]), "ST", 16)
            SY(dd)
        if "acinc" in dbg:
            def da(y):
                S.inc(y.dma_start(dbg["acinc"], acinc[:, :]), "ST", 16)
            SY(da)

    pop_to(m_a)
    m_bc = mark()
    wout2 = push("wout2", [128, 16 * D_], BF16)
    xt0 = push("xt0", [128, E], BF16, is_transpose_dest=True)
    xt1 = push("xt1", [128, E], BF16, is_transpose_dest=True)
    wq = push("wq", [128, 2 * E], BF16)
    e2sb = push("e2sb", [128, E], BF16)
    gpmb = push("gpmb", [128, 128], BF16)
    zal = push("zal", [128, E], BF16)
    btp = push("btp", [128, 64], BF16)
    ybuf = push("ybuf", [128, E], F32)
    gbuf = push("gbuf", [128, E], F32)
    g2b = push("g2b", [128, E], BF16)
    xck = push("xck", [128, E], BF16)
    szt = push("szt", [128, E], BF16)
    ynb = push("ynb", [128, E], BF16)
    rib = push("rib", [128, 128], F32)
    dab = push("dab", [128, 32], F32)
    vbuf = push("vbuf", [128, D_], F32)
    xres = push("xres", [128, D_], F32)
    x1t = push("x1t", [128, D_], F32)
    cmt = push("cmt", [64, Q], BF16)
    if True:

        XT = [xt0, xt1]

        def w2l(g):
            S.inc(g.dma_start(wout2[:, :].rearrange("p (k dd) -> p k dd", k=16),
                              d["woutT"].rearrange("(k p) dd -> p k dd", p=128)),
                  "GPD", 16)
        GG(w2l)
        gpd_wout = S.n["GPD"]

        def w2sc(v):
            S.wait(v, "GPD", gpd_wout)
            ins = None
            for ec in range(16):
                ins = v.tensor_scalar(wout2[:, ec * D_:(ec + 1) * D_],
                                      wout2[:, ec * D_:(ec + 1) * D_],
                                      nwp_s[:, ec:ec + 1], None, ALU.mult)
            S.inc(ins, "VC")
        VV(w2sc)

        for c in range(NTC):
            par = c % 2
            def xtl(y, c=c, par=par):
                S.wait(y, "VC")
                for ec in range(16):
                    S.inc(y.dma_start(XT[par][:, ec * 128:(ec + 1) * 128],
                                      xa_sp[ec, :, c * Q:(c + 1) * Q],
                                      transpose=True), "SYD", 16)
            SY(xtl)
            syd_xt = S.n["SYD"]

            def btr(t, c=c):
                S.wait(t, "VC")
                S.inc(t.transpose(psB[:, 0:32].bitcast(BF16),
                                  bcma[0:64, c * Q:(c + 1) * Q],
                                  identb[0:64, 0:64]), "TC")
            TT(btr)

            def bev(v, c=c):
                S.wait(v, "TC")
                S.inc(v.tensor_copy(btp[:, :], psB[:, 0:32].bitcast(BF16)), "VC")
            VV(bev)

            def zb(v, c=c, par=par, syd_xt=syd_xt):
                S.wait(v, "SYD", syd_xt)
                S.inc(v.tensor_tensor(
                    zal[:, :].rearrange("p (h q) -> p h q", q=P),
                    XT[par][:, :].rearrange("p (h q) -> p h q", q=P),
                    w2TA[:, c * H:(c + 1) * H].unsqueeze(2)
                        .broadcast_to([128, H, P]), ALU.mult), "VC")
            VV(zb)

            def scm(t, c=c):
                S.wait(t, "VC")
                ins = None
                for i in range(4):
                    ins = t.matmul(psA[0:64, i * 512:(i + 1) * 512], btp[:, :],
                                   zal[:, i * 512:(i + 1) * 512], start=True,
                                   stop=True)
                S.inc(ins, "TC")
            TT(scm)

            def hup(v, c=c):
                S.wait(v, "TC")
                src, dst = HB[c % 2], HB[1 - c % 2]
                v.tensor_tensor(dst[:, :].rearrange("n (h q) -> n h q", q=P),
                                src[:, :].rearrange("n (h q) -> n h q", q=P),
                                lam64A[:, c * H:(c + 1) * H].unsqueeze(2)
                                    .broadcast_to([64, H, P]), ALU.mult)
                v.drain()
                S.inc(v.tensor_tensor(dst[:, :], dst[:, :], psA[0:64, :], ALU.add),
                      "VC")
            VV(hup)
        fence()
        if "Hinit" in dbg:
            def dh(y):
                S.inc(y.dma_start(dbg["Hinit"], HB[0][:, :]), "ST", 16)
            SY(dh)

        vc_hup = {}
        vc_resv = {}
        vc_yc = {}
        sc_a3 = S.n["SC"]
        for c in range(NTC):
            par = c % 2
            hsrc, hdst = HB[c % 2], HB[1 - c % 2]

            def xtl(y, c=c, par=par):
                if c - 1 in vc_resv:
                    S.wait(y, "VC", vc_resv[c - 1])
                for ec in range(16):
                    S.inc(y.dma_start(XT[par][:, ec * 128:(ec + 1) * 128],
                                      x_sp[ec, :, c * Q:(c + 1) * Q],
                                      transpose=True), "SYD", 16)
                S.inc(y.dma_start(xck[:, :].rearrange("p (ec q) -> p ec q", q=Q),
                                  x_sp[:, :, c * Q:(c + 1) * Q].transpose([1, 0, 2])),
                      "SYD", 16)
                S.inc(y.dma_start(szt[:, :].rearrange("p (ec q) -> p ec q", q=Q),
                                  sz_sp[:, :, c * Q:(c + 1) * Q].transpose([1, 0, 2])),
                      "SYD", 16)
                S.inc(y.dma_start(xres[:, :], d["xin"][c * 128:(c + 1) * 128, :]),
                      "SYD", 16)
                S.inc(y.dma_start(cmt[:, :], bcm[64:128, c * Q:(c + 1) * Q]),
                      "SYD", 16)
            SY(xtl)
            syd_xt = S.n["SYD"]

            def e2t(t, c=c):
                if c - 1 in vc_resv:
                    S.wait(t, "VC", vc_resv[c - 1])
                S.wait(t, "SC", sc_a3)
                ins = None
                for fc in range(16):
                    ins = t.matmul(psB[:, fc * 128:(fc + 1) * 128],
                                   c_ind[:, fc * 128:(fc + 1) * 128],
                                   erow[:, c * Q:(c + 1) * Q], start=True, stop=True)
                S.inc(ins, "TC")
            TT(e2t)

            def e2v(v, c=c):
                S.wait(v, "TC")
                S.inc(v.tensor_copy(e2sb[:, :], psB[:, :]), "VC")
            VV(e2v)
            vc_e2 = S.n["VC"]

            def dtar(v, c=c):
                S.inc(v.tensor_scalar(scr32[:, c * Q:(c + 1) * Q],
                                      dtv[:, c * Q:(c + 1) * Q], acol[:, 0:1], None,
                                      ALU.mult), "VC")
            VV(dtar)

            vc_dtar = S.n["VC"]

            def dtat(t, c=c, vc_dtar=vc_dtar):
                S.wait(t, "VC", vc_dtar)
                if c - 1 in vc_hup:
                    S.wait(t, "VC", vc_hup[c - 1])
                S.inc(t.transpose(psA[:, 0:32], scr32[:, c * Q:(c + 1) * Q],
                                  c_idf[0:32, 0:32]), "TC")
            TT(dtat)

            def dtae(v, c=c):
                S.wait(v, "TC")
                S.inc(v.tensor_copy(dab[:, :], psA[:, 0:32]), "VC")
            VV(dtae)

            def rhb(v, c=c):
                S.inc(v.tensor_tensor(
                    wq[:, :].rearrange("p (h q) -> p h q", q=Q),
                    c_umask[:, :].unsqueeze(1).broadcast_to([128, H, Q]),
                    dab[:, :].unsqueeze(2).broadcast_to([128, H, Q]),
                    ALU.mult), "VC")
            VV(rhb)
            vc_rh = S.n["VC"]

            sc_mde = [None]
            for hf in range(2):
                def cdf(t, c=c, hf=hf, vc_rh=vc_rh):
                    S.wait(t, "VC", vc_rh)
                    if hf == 1:
                        S.wait(t, "SC", sc_mde[0])
                    elif c - 1 in vc_hup:
                        S.wait(t, "VC", vc_hup[c - 1])
                    ins = None
                    for hh in range(16):
                        h = hf * 16 + hh
                        ins = t.matmul(psA[:, hh * 128:(hh + 1) * 128],
                                       c_strict[:, :], wq[:, h * 128:(h + 1) * 128],
                                       start=True, stop=True)
                    S.inc(ins, "TC")
                TT(cdf)

                def mde(sc, c=c, hf=hf):
                    S.wait(sc, "TC")
                    S.inc(sc.activation(wq[:, hf * E:(hf + 1) * E], psA[:, :],
                                        ACT.Exp), "SC")
                SS(mde)
                sc_mde[0] = S.n["SC"]

            def gpm(t, c=c, vc_e2=vc_e2, syd_xt=syd_xt):
                S.wait(t, "VC", vc_e2)
                S.wait(t, "SYD", syd_xt)
                S.inc(t.matmul(psB[:, 0:128], bcm[0:64, c * Q:(c + 1) * Q],
                               cmt[:, :], start=True, stop=True), "TC")
            TT(gpm)

            def gpe(v, c=c):
                S.wait(v, "TC")
                S.inc(v.tensor_tensor(gpmb[:, :], psB[:, 0:128], c_umask[:, :],
                                      ALU.mult), "VC")
            VV(gpe)

            def wpb(v, c=c):
                S.wait(v, "SC", sc_mde[0])
                ins = None
                for hf in range(2):
                    ins = v.tensor_tensor(
                        wq[:, hf * E:(hf + 1) * E].rearrange("p (h q) -> p h q", q=Q),
                        wq[:, hf * E:(hf + 1) * E].rearrange("p (h q) -> p h q", q=Q),
                        gpmb[:, :].unsqueeze(1).broadcast_to([128, 16, Q]),
                        ALU.mult)
                S.inc(ins, "VC")
            VV(wpb)

            def xdt(v, c=c, par=par, syd_xt=syd_xt):
                S.wait(v, "SYD", syd_xt)
                S.inc(v.tensor_tensor(
                    zal[:, :].rearrange("p (h q) -> p h q", q=P),
                    XT[par][:, :].rearrange("p (h q) -> p h q", q=P),
                    dtvT[:, c * H:(c + 1) * H].unsqueeze(2)
                        .broadcast_to([128, H, P]), ALU.mult), "VC")
            VV(xdt)

            def hcast(v, c=c, hsrc=hsrc):
                S.inc(v.tensor_copy(g2b[0:64, :], hsrc[:, :]), "VC")
            VV(hcast)
            vc_xdt = S.n["VC"]

            def yim(t, c=c, vc_xdt=vc_xdt):
                S.wait(t, "VC", vc_xdt)
                ins = None
                for h in range(H):
                    fc, odd = h // 2, h % 2
                    ins = t.matmul(psB[odd * 64:odd * 64 + 64,
                                       fc * 128:(fc + 1) * 128],
                                   zal[:, h * P:(h + 1) * P],
                                   wq[:, h * 128:(h + 1) * 128],
                                   start=True, stop=True)
                S.inc(ins, "TC")
            TT(yim)

            def ysm(t, c=c):
                S.wait(t, "SC", sc_mde[0])
                ins = None
                for fc in range(16):
                    ins = t.matmul(psA[:, fc * 128:(fc + 1) * 128],
                                   g2b[0:64, fc * 128:(fc + 1) * 128],
                                   cmt[:, :], start=True, stop=True)
                S.inc(ins, "TC")
            TT(ysm)
            tc_y = S.n["TC"]

            def yc(v, c=c, tc_y=tc_y, syd_xt=syd_xt):
                S.wait(v, "TC", tc_y)
                S.wait(v, "SYD", syd_xt)
                v.tensor_tensor(ybuf[:, :], psA[:, :], e2sb[:, :], ALU.mult)
                v.drain()
                v.tensor_tensor(ybuf[:, :], ybuf[:, :], psB[:, :], ALU.add)
                v.tensor_tensor(gbuf[:, :].rearrange("p (fc q) -> p fc q", q=Q),
                                xck[:, :].rearrange("p (fc q) -> p fc q", q=Q),
                                dexp_s[:, :].unsqueeze(2)
                                    .broadcast_to([128, 16, Q]), ALU.mult)
                v.drain()
                S.inc(v.tensor_tensor(ybuf[:, :], ybuf[:, :], gbuf[:, :], ALU.add),
                      "VC")
            VV(yc)
            vc_y = S.n["VC"]
            vc_yc[c] = vc_y
            if c == 0 and "y0" in dbg:
                def dy(y, vc_y=vc_y):
                    S.wait(y, "VC", vc_y)
                    S.inc(y.dma_start(dbg["y0"], ybuf[:, :]), "ST", 16)
                SY(dy)

            def zb2(v, c=c, par=par):
                S.inc(v.tensor_tensor(
                    zal[:, :].rearrange("p (h q) -> p h q", q=P),
                    XT[par][:, :].rearrange("p (h q) -> p h q", q=P),
                    w2T[:, c * H:(c + 1) * H].unsqueeze(2)
                        .broadcast_to([128, H, P]), ALU.mult), "VC")
            VV(zb2)

            def btr2(t, c=c):
                S.wait(t, "VC")
                S.inc(t.transpose(psB[:, 0:32].bitcast(BF16),
                                  bcm[0:64, c * Q:(c + 1) * Q],
                                  identb[0:64, 0:64]), "TC")
            TT(btr2)

            def bev2(v, c=c):
                S.wait(v, "TC")
                S.inc(v.tensor_copy(btp[:, :], psB[:, 0:32].bitcast(BF16)), "VC")
            VV(bev2)

            def scm2(t, c=c):
                S.wait(t, "VC")
                ins = None
                for i in range(4):
                    ins = t.matmul(psA[0:64, i * 512:(i + 1) * 512], btp[:, :],
                                   zal[:, i * 512:(i + 1) * 512], start=True,
                                   stop=True)
                S.inc(ins, "TC")
            TT(scm2)

            def hup2(v, c=c, hsrc=hsrc, hdst=hdst):
                S.wait(v, "TC")
                v.tensor_tensor(hdst[:, :].rearrange("n (h q) -> n h q", q=P),
                                hsrc[:, :].rearrange("n (h q) -> n h q", q=P),
                                lam64[:, c * H:(c + 1) * H].unsqueeze(2)
                                    .broadcast_to([64, H, P]), ALU.mult)
                v.drain()
                S.inc(v.tensor_tensor(hdst[:, :], hdst[:, :], psA[0:64, :], ALU.add),
                      "VC")
            VV(hup2)
            vc_hup[c] = S.n["VC"]

            def gat(v, c=c):
                v.tensor_tensor(gbuf[:, :], ybuf[:, :], szt[:, :], ALU.mult)
                v.drain()
                S.inc(v.tensor_tensor(g2b[:, :], gbuf[:, :], gbuf[:, :], ALU.mult),
                      "VC")
            VV(gat)

            def rmsm(t, c=c):
                S.wait(t, "VC")
                ins = None
                for fc in range(16):
                    ins = t.matmul(psB[0:1, 0:128], onescol[:, 0:1],
                                   g2b[:, fc * 128:(fc + 1) * 128],
                                   start=(fc == 0), stop=(fc == 15))
                S.inc(ins, "TC")
            TT(rmsm)

            def rmss(sc, c=c):
                S.wait(sc, "TC")
                sc.activation(rmsrow[0:1, 0:128], psB[0:1, 0:128], ACT.Ln,
                              bias=cvals[0:1, 0:1], scale=1.0 / E)
                S.inc(sc.activation(rmsrow[0:1, 128:256], rmsrow[0:1, 0:128],
                                    ACT.Exp, scale=-0.5), "SC")
            SS(rmss)

            def rim(t, c=c):
                S.wait(t, "SC")
                S.inc(t.matmul(psB[:, 128:256], ones1f[0:1, :],
                               rmsrow[0:1, 128:256], start=True, stop=True), "TC")
            TT(rim)

            def rie(v, c=c):
                S.wait(v, "TC")
                S.inc(v.tensor_copy(rib[:, :], psB[:, 128:256]), "VC")
            VV(rie)

            def ynbv(v, c=c):
                S.inc(v.tensor_tensor(
                    ynb[:, :].rearrange("p (fc q) -> p fc q", q=Q),
                    gbuf[:, :].rearrange("p (fc q) -> p fc q", q=Q),
                    rib[:, :].unsqueeze(1).broadcast_to([128, 16, Q]),
                    ALU.mult), "VC")
            VV(ynbv)
            vc_ynb = S.n["VC"]

            def wom(t, c=c, vc_ynb=vc_ynb):
                S.wait(t, "VC", vc_ynb)
                ins = None
                for ek in range(16):
                    for hh in range(2):
                        ins = t.matmul(
                            psB[:, hh * 512:(hh + 1) * 512],
                            ynb[:, ek * 128:(ek + 1) * 128],
                            wout2[:, ek * D_ + hh * 512: ek * D_ + (hh + 1) * 512],
                            start=(ek == 0), stop=(ek == 15))
                S.inc(ins, "TC")
            TT(wom)

            def resv(v, c=c):
                S.wait(v, "TC")
                S.wait(v, "SYD")
                S.inc(v.scalar_tensor_tensor(vbuf[:, :], xres[:, :], 2.0,
                                             psB[:, 0:1024], ALU.mult, ALU.add),
                      "VC")
            VV(resv)
            vc_resv[c] = S.n["VC"]

            ln_chunk(vbuf[:, :], x1t[:, :])

            def x1s(y, c=c):
                S.wait(y, "VC")
                S.inc(y.dma_start(x1_sp[c * 128:(c + 1) * 128, :], x1t[:, :]),
                      "SYD", 16)
            SY(x1s)

        fence()
        if "Hend" in dbg:
            def dh2(y):
                S.inc(y.dma_start(dbg["Hend"], HB[0][:, :]), "ST", 16)
            SY(dh2)

    pop_to(m_bc)
    pop_to(m_ab)
    MACC = push("maccb", [128, NTC * D_], F32)
    x2bf = push("x2bf", [128, 8 * T], BF16, is_transpose_dest=True)
    m_d = mark()
    x2lo = push("x2lo", [128, 8 * T], BF16, is_transpose_dest=True)
    hib = push("hib", [128, D_], BF16)
    lob = push("lob", [128, D_], BF16)
    scoresF = push("scoresF", [32, T], F32)
    if True:
        def x1l(y):
            for tc in range(NTC):
                S.inc(y.dma_start(MACC[:, tc * D_:(tc + 1) * D_],
                                  x1_sp[tc * 128:(tc + 1) * 128, :]), "SYD", 16)
        SY(x1l)
        if "x1" in dbg:
            def dx1(y):
                S.wait(y, "SYD")
                S.inc(y.dma_start(dbg["x1"], MACC[:, :]), "ST", 16)
            SY(dx1)

        syd_tc = {}
        for tc in range(NTC):
            def hilo(v, tc=tc):
                S.wait(v, "SYD")
                v.tensor_copy(hib[:, :], MACC[:, tc * D_:(tc + 1) * D_])
                v.drain()
                S.inc(v.tensor_tensor(lob[:, :], MACC[:, tc * D_:(tc + 1) * D_],
                                      hib[:, :], ALU.subtract), "VC")
            VV(hilo)

            def htr(y, tc=tc):
                S.wait(y, "VC")
                for kd in range(4):
                    S.inc(y.dma_start(
                        x2bf[:, kd * T + tc * 128: kd * T + (tc + 1) * 128],
                        hib[:, kd * 128:(kd + 1) * 128], transpose=True),
                        "SYD", 16)
                    S.inc(y.dma_start(
                        x2lo[:, kd * T + tc * 128: kd * T + (tc + 1) * 128],
                        lob[:, kd * 128:(kd + 1) * 128], transpose=True),
                        "SYD", 16)
            SY(htr)

            def htr2(sc, tc=tc):
                S.wait(sc, "VC")
                for kd in range(4, 8):
                    S.inc(sc.dma_start(
                        x2bf[:, kd * T + tc * 128: kd * T + (tc + 1) * 128],
                        hib[:, kd * 128:(kd + 1) * 128], transpose=True),
                        "SYD", 16)
                    S.inc(sc.dma_start(
                        x2lo[:, kd * T + tc * 128: kd * T + (tc + 1) * 128],
                        lob[:, kd * 128:(kd + 1) * 128], transpose=True),
                        "SYD", 16)
            SS(htr2)
            syd_tc[tc] = S.n["SYD"]
        syd_hilo = S.n["SYD"]

        for tg in range(4):
            def rlm(t, tg=tg):
                S.wait(t, "SYD", syd_tc[4 * tg + 3])
                if tg > 0:
                    S.wait(t, "SC")
                mms = []
                for kd in range(8):
                    for a_, b_ in [(gwh, x2bf), (gwl, x2bf), (gwh, x2lo)]:
                        mms.append((a_, b_, kd))
                ins = None
                for i, (a_, b_, kd) in enumerate(mms):
                    ins = t.matmul(psB[0:32, 0:512],
                                   a_[:, kd * H:(kd + 1) * H],
                                   b_[:, kd * T + tg * 512: kd * T + (tg + 1) * 512],
                                   start=(i == 0), stop=(i == len(mms) - 1))
                S.inc(ins, "TC")
            TT(rlm)

            def sig(sc, tg=tg):
                S.wait(sc, "TC")
                S.inc(sc.activation(scoresF[:, tg * 512:(tg + 1) * 512],
                                    psB[0:32, 0:512], ACT.Sigmoid), "SC")
            SS(sig)

        for tc in range(NTC):
            def str_(t, tc=tc):
                S.wait(t, "SC")
                S.wait(t, "VC")
                S.inc(t.transpose(psA[:, 0:32],
                                  scoresF[:, tc * 128:(tc + 1) * 128],
                                  c_idf[0:32, 0:32]), "TC")
            TT(str_)

            def rmath(v, tc=tc):
                S.wait(v, "TC")
                sco = scorestk[:, tc * H:(tc + 1) * H]
                v.tensor_copy(sco, psA[:, 0:32])
                sc_ = rscr[:, 0:32]
                t2s = rscr[:, 32:40]
                t2b = rscr2[:, 32:40]
                msk = rscr2[:, 0:32]
                v.drain()
                v.tensor_tensor(sc_, sco, c_eb[:, :], ALU.add)
                v.drain()
                pairs = [(0, 1), (0, 2), (0, 3), (1, 2), (1, 3), (2, 3)]
                for k_, (i_, j_) in enumerate(pairs):
                    dstp = t2s if k_ == 0 else t2b
                    v.tensor_tensor(dstp, sc_[:, i_:32:4], sc_[:, j_:32:4],
                                    ALU.add)
                    v.drain()
                    if k_ > 0:
                        v.tensor_tensor(t2s, t2s, t2b, ALU.max)
                        v.drain()
                v.tensor_copy(t2b, t2s)
                v.drain()
                for _ in range(3):
                    v.reduce_max(stat[:, 0:1], t2b, axis=AXX)
                    v.drain()
                    v.tensor_scalar(sqscr[:, 0:8], t2b, stat[:, 0:1], -1e30,
                                    ALU.is_ge, ALU.mult)
                    v.drain()
                    v.tensor_tensor(t2b, t2b, sqscr[:, 0:8], ALU.add)
                    v.drain()
                v.reduce_max(stat[:, 0:1], t2b, axis=AXX)
                v.drain()
                v.tensor_scalar(t2b, t2s, stat[:, 0:1], None, ALU.is_ge)
                v.drain()
                v.tensor_tensor(msk.rearrange("p (g j) -> p g j", j=4),
                                sc_.rearrange("p (g j) -> p g j", j=4),
                                t2b.unsqueeze(2)
                                   .broadcast_to([128, 8, 4]), ALU.mult)
                v.drain()
                v.tensor_copy(sqscr[:, 0:32], msk)
                v.drain()
                for _ in range(5):
                    v.reduce_max(stat[:, 0:1], sqscr[:, 0:32], axis=AXX)
                    v.drain()
                    v.tensor_scalar(sqscr[:, 32:64], sqscr[:, 0:32],
                                    stat[:, 0:1], -1e30, ALU.is_ge, ALU.mult)
                    v.drain()
                    v.tensor_tensor(sqscr[:, 0:32], sqscr[:, 0:32],
                                    sqscr[:, 32:64], ALU.add)
                    v.drain()
                v.reduce_max(stat[:, 0:1], sqscr[:, 0:32], axis=AXX)
                v.drain()
                v.tensor_scalar(msk, msk, stat[:, 0:1], None, ALU.is_ge)
                v.drain()
                v.tensor_tensor(msk, msk, sco, ALU.mult)
                v.drain()
                v.reduce_sum(stat[:, 1:2], msk, axis=AXX)
                v.drain()
                v.tensor_scalar(stat[:, 1:2], stat[:, 1:2], 1e-20, None, ALU.add)
                v.drain()
                v.reciprocal(stat[:, 2:3], stat[:, 1:2])
                v.drain()
                S.inc(v.tensor_scalar(combB[:, tc * H:(tc + 1) * H], msk,
                                      stat[:, 2:3], 2.5, ALU.mult, ALU.mult),
                      "VC")
            VV(rmath)

            def ctr(t, tc=tc):
                S.wait(t, "VC")
                S.inc(t.transpose(psA[0:32, 128:256], combB[:, tc * H:(tc + 1) * H],
                                  c_idf[:, :]), "TC")
            TT(ctr)

            def cte(v, tc=tc):
                S.wait(v, "TC")
                S.inc(v.tensor_copy(combTb[0:32, tc * 128:(tc + 1) * 128],
                                    psA[0:32, 128:256]), "VC")
            VV(cte)
        fence()
        if "scores" in dbg:
            def ds(y):
                S.inc(y.dma_start(dbg["scores"], scorestk[:, :]), "ST", 16)
            SY(ds)
        if "comb" in dbg:
            def dc(y):
                S.inc(y.dma_start(dbg["comb"], combB[:, :]), "ST", 16)
            SY(dc)

    pop_to(m_d)
    m_e = mark()
    wtu = push("wtu", [128, 4 * 8 * 512], BF16)
    wtd = push("wtd", [128, 4 * 4 * D_], BF16)
    heb = push("heb", [128, 2048], BF16)
    cbc4 = push("cbc4", [128, 1024], BF16)
    rbE = push("rbE", [128, 1024], BF16)
    crow = push("crow", [1, 4 * T], BF16)
    if True:

        groups = [[0, 1, 2, 3], [4, 5, 6, 7], [8, 9, 10, 11], [12, 13, 14, 15],
                  [16, 17, 18, 19], [20, 21, 22, 23], [24, 25, 26, 27],
                  [28, 29, 30, 31], [32, 33, 34, 35]]

        def pf_expert(g, e, slot):
            S.inc(g.dma_start(
                wtu[:, slot * 4096:(slot + 1) * 4096]
                    .rearrange("p (k i) -> p k i", k=8),
                d["upT"][e].rearrange("(k p) i -> p k i", p=128)), "GPD", 16)
            S.inc(g.dma_start(
                wtd[:, slot * 4 * D_:(slot + 1) * 4 * D_]
                    .rearrange("p (k dd) -> p k dd", k=4),
                d["dnT"][e].rearrange("(k p) dd -> p k dd", p=128)), "GPD", 16)

        def pf0(g):
            for slot, e in enumerate(groups[0]):
                pf_expert(g, e, slot)
        GG(pf0)

        gpd_exp = {}
        for slot, e in enumerate(groups[0]):
            gpd_exp[(0, slot)] = None
        gpd_pf0 = S.n["GPD"]
        vc_ep = {}
        tc_dn = {}
        sc_sq = [None]
        vc_mev = {}
        tc_cbm = {}

        for gi, grp in enumerate(groups):
            def crl(y, grp=grp, gi=gi):
                S.wait(y, "TC")
                S.wait(y, "VC")
                for si, e in enumerate(grp):
                    S.inc(y.dma_start(crow[0:1, si * T:(si + 1) * T],
                                      combTb[e:e + 1, :]), "SYD", 16)
            SY(crl)
            syd_crow = S.n["SYD"]

            for tt in range(8):
                def cbm4(t, gi=gi, tt=tt, grp=grp, syd_crow=syd_crow):
                    S.wait(t, "SYD", syd_crow)
                    if (gi, tt - 1, 3) in vc_ep:
                        S.wait(t, "VC", vc_ep[(gi, tt - 1, 3)])
                    elif gi > 0 and (gi - 1, 7, 3) in vc_ep:
                        S.wait(t, "VC", vc_ep[(gi - 1, 7, 3)])
                    ins = None
                    for si in range(len(grp)):
                        ins = t.matmul(psB[:, si * 256:(si + 1) * 256],
                                       ones1b[0:1, :],
                                       crow[0:1, si * T + tt * 256:
                                            si * T + (tt + 1) * 256],
                                       start=True, stop=True)
                    S.inc(ins, "TC")
                TT(cbm4)
                tc_cbm[(gi, tt)] = S.n["TC"]

                def cbe4(v, gi=gi, tt=tt):
                    S.wait(v, "TC", tc_cbm[(gi, tt)])
                    S.inc(v.tensor_copy(cbc4[:, :], psB[:, 0:1024]), "VC")
                VV(cbe4)
                vc_cbe = S.n["VC"]

                for si, e in enumerate(grp):
                    half = si % 2
                    def upm(t, e=e, tt=tt, si=si, gi=gi, half=half,
                            vc_cbe=vc_cbe):
                        g_ = gpd_exp.get((gi, si))
                        S.wait(t, "GPD", gpd_pf0 if g_ is None else g_)
                        if si < 2:
                            S.wait(t, "VC", vc_cbe)
                        if (gi, tt, si - 2) in vc_ep:
                            S.wait(t, "VC", vc_ep[(gi, tt, si - 2)])
                        elif (gi, tt - 1, si) in vc_ep:
                            S.wait(t, "VC", vc_ep[(gi, tt - 1, si)])
                        elif gi > 0 and (gi - 1, 7, si) in vc_ep:
                            S.wait(t, "VC", vc_ep[(gi - 1, 7, si)])
                        ins = None
                        for ic in range(4):
                            for kd in range(8):
                                ins = t.matmul(
                                    psB[:, half * 1024 + ic * 256:
                                        half * 1024 + (ic + 1) * 256],
                                    wtu[:, si * 4096 + kd * 512 + ic * 128:
                                        si * 4096 + kd * 512 + (ic + 1) * 128],
                                    x2bf[:, kd * T + tt * 256:
                                         kd * T + (tt + 1) * 256],
                                    start=(kd == 0), stop=(kd == 7))
                        S.inc(ins, "TC")
                    TT(upm)
                    tc_up = S.n["TC"]

                    def ep1v(v, si=si, half=half, tc_up=tc_up, gi=gi, tt=tt):
                        S.wait(v, "TC", tc_up)
                        if sc_sq[0] is not None:
                            S.wait(v, "SC", sc_sq[0])
                        S.inc(v.tensor_scalar(rbE[:, :],
                                              psB[:, half * 1024:(half + 1) * 1024],
                                              0.0, None, ALU.max), "VC")
                    VV(ep1v)
                    vc_r = S.n["VC"]

                    def ep2s(sc, si=si, half=half, vc_r=vc_r, gi=gi, tt=tt):
                        S.wait(sc, "VC", vc_r)
                        key = (gi, tt, si - 2)
                        if key in tc_dn:
                            S.wait(sc, "TC", tc_dn[key])
                        elif (gi, tt - 1, si) in tc_dn:
                            S.wait(sc, "TC", tc_dn[(gi, tt - 1, si)])
                        elif gi > 0 and (gi - 1, 7, si) in tc_dn:
                            S.wait(sc, "TC", tc_dn[(gi - 1, 7, si)])
                        S.inc(sc.activation(heb[:, half * 1024:(half + 1) * 1024],
                                            rbE[:, :], ACT.Square), "SC")
                    SS(ep2s)
                    sc_sq[0] = S.n["SC"]

                    def ep3v(v, si=si, half=half):
                        S.wait(v, "SC", sc_sq[0])
                        S.inc(v.tensor_tensor(
                            heb[:, half * 1024:(half + 1) * 1024]
                                .rearrange("p (ic q) -> p ic q", q=256),
                            heb[:, half * 1024:(half + 1) * 1024]
                                .rearrange("p (ic q) -> p ic q", q=256),
                            cbc4[:, si * 256:(si + 1) * 256].unsqueeze(1)
                                .broadcast_to([128, 4, 256]), ALU.mult), "VC")
                    VV(ep3v)
                    vc_ep[(gi, tt, si)] = S.n["VC"]

                    first = (si == 0)
                    last = (si == len(grp) - 1)
                    def dnm(t, e=e, tt=tt, si=si, gi=gi, half=half, first=first,
                            last=last):
                        S.wait(t, "VC", vc_ep[(gi, tt, si)])
                        if first and (gi, tt - 1) in vc_mev:
                            S.wait(t, "VC", vc_mev[(gi, tt - 1)])
                        elif first and gi > 0 and (gi - 1, 7) in vc_mev:
                            S.wait(t, "VC", vc_mev[(gi - 1, 7)])
                        ins = None
                        for tc2 in range(2):
                            for hh in range(2):
                                for ik in range(4):
                                    ins = t.matmul(
                                        psA[:, tc2 * 1024 + hh * 512:
                                            tc2 * 1024 + (hh + 1) * 512],
                                        heb[:, half * 1024 + ik * 256 + tc2 * 128:
                                            half * 1024 + ik * 256 + tc2 * 128 + 128],
                                        wtd[:, si * 4 * D_ + ik * D_ + hh * 512:
                                            si * 4 * D_ + ik * D_ + (hh + 1) * 512],
                                        start=(first and ik == 0),
                                        stop=(last and ik == 3))
                        S.inc(ins, "TC")
                    TT(dnm)
                    tc_dn[(gi, tt, si)] = S.n["TC"]

                    if tt == 7 and gi + 1 < len(groups) and si < len(groups[gi + 1]):
                        def pfn(g, gi=gi, si=si):
                            S.wait(g, "TC", tc_dn[(gi, 7, si)])
                            pf_expert(g, groups[gi + 1][si], si)
                        GG(pfn)
                        gpd_exp[(gi + 1, si)] = S.n["GPD"]

                def mev(v, tt=tt, gi=gi):
                    S.wait(v, "TC", tc_dn[(gi, tt, len(grp) - 1)])
                    ins = None
                    for tc2 in range(2):
                        tcg = tt * 2 + tc2
                        ins = v.tensor_tensor(
                            MACC[:, tcg * D_:(tcg + 1) * D_],
                            MACC[:, tcg * D_:(tcg + 1) * D_],
                            psA[:, tc2 * 1024:(tc2 + 1) * 1024], ALU.add)
                    S.inc(ins, "VC")
                VV(mev)
                vc_mev[(gi, tt)] = S.n["VC"]
        fence()
        if "macc" in dbg:
            def dm(y):
                S.inc(y.dma_start(dbg["macc"], MACC[:, :]), "ST", 16)
            SY(dm)

        for tc in range(NTC):
            ln_chunk(MACC[:, tc * D_:(tc + 1) * D_], MACC[:, tc * D_:(tc + 1) * D_])

            def out_dma(y, tc=tc):
                S.wait(y, "VC")
                S.inc(y.dma_start(yout[tc * 128:(tc + 1) * 128, :],
                                  MACC[:, tc * D_:(tc + 1) * D_]), "ST", 16)
            SY(out_dma)
            st_now = S.n["ST"]

            def stw(sc, st_now=st_now):
                S.wait(sc, "ST", st_now)
            SS(stw)

            def vtw(v, st_now=st_now):
                S.wait(v, "ST", st_now)
            VV(vtw)

        def final_wait(y):
            S.wait(y, "ST")
        SY(final_wait)

    blk.__exit__(None, None, None)
    pop_to(0)
    for g in reversed(guards):
        g.__exit__(None, None, None)
    return nc, dbg


def _prep_core_inputs(inputs, core):
    b, half = core >> 1, core & 1
    f32 = lambda a: np.ascontiguousarray(np.asarray(a, dtype=np.float32))
    x = f32(inputs["L0_attn_x"])
    W_in = f32(inputs["L0_attn_W_in"])
    conv_w = f32(inputs["L0_attn_conv_w"])
    conv_b = f32(inputs["L0_attn_conv_b"])
    lnw = f32(inputs["L0_attn_ln_w"]); lnb = f32(inputs["L0_attn_ln_b"])
    dtb = f32(inputs["L0_attn_dt_bias"]); alog = f32(inputs["L0_attn_A_log"])
    Dv = f32(inputs["L0_attn_D"]); nw = f32(inputs["L0_attn_norm_w"])
    W_out = f32(inputs["L0_attn_W_out"])
    gw = f32(inputs["L0_ffn_gate_w"]); eb = f32(inputs["L0_ffn_e_bias"])
    up = f32(inputs["L0_ffn_up_w"]); dn = f32(inputs["L0_ffn_down_w"])
    sup = f32(inputs["L0_ffn_shared_up_w"]); sdn = f32(inputs["L0_ffn_shared_down_w"])

    def convlay(w):
        return np.ascontiguousarray(w.reshape(NEC, 128, 4).transpose(1, 0, 2)
                                    .reshape(128, NEC * 4))

    def convblay(w):
        return np.ascontiguousarray(w.reshape(NEC, 128).T)

    upT = np.ascontiguousarray(np.concatenate(
        [up.transpose(0, 2, 1),
         np.ascontiguousarray(sup.T).reshape(D_, 4, 512).transpose(1, 0, 2)],
        axis=0))
    dnT = np.ascontiguousarray(np.concatenate(
        [dn.transpose(0, 2, 1),
         np.ascontiguousarray(sdn.T).reshape(4, 512, D_)], axis=0))

    if half == 1:
        xaux = np.ascontiguousarray(x[b, 0:T])
        winTa = np.ascontiguousarray(W_in[E:].T)
        lnwa, lnba = lnw, lnb
        cwa, cba = convlay(conv_w), convblay(conv_b)
        dtba, aloga = dtb, alog
    else:
        xaux = np.zeros((T, D_), np.float32)
        winTa = np.zeros((D_, CD + H), np.float32)
        lnwa = np.zeros(D_, np.float32); lnba = np.zeros(D_, np.float32)
        cwa = np.zeros((128, NEC * 4), np.float32)
        cba = np.zeros((128, NEC), np.float32)
        dtba = np.zeros(H, np.float32); aloga = np.zeros(H, np.float32)

    return {
        "xin": np.ascontiguousarray(x[b, half * T:(half + 1) * T]),
        "xaux": xaux,
        "winT": np.ascontiguousarray(W_in.T),
        "winTa": winTa,
        "lnrows": np.ascontiguousarray(
            np.concatenate([lnw, lnb, lnwa, lnba]).reshape(1, 4 * D_)),
        "cw": convlay(conv_w), "cb": convblay(conv_b),
        "cwa": cwa, "cba": cba,
        "dtb": dtb.reshape(H, 1), "alog": alog.reshape(H, 1),
        "dtba": dtba.reshape(H, 1), "aloga": aloga.reshape(H, 1),
        "dexp": np.ascontiguousarray(np.repeat(Dv, P).reshape(16, 128).T),
        "nwp": np.ascontiguousarray(nw.reshape(16, 128).T),
        "woutT": np.ascontiguousarray(W_out.T),
        "gwT": np.ascontiguousarray(gw.T),
        "ebias": eb.reshape(1, H),
        "upT": upT, "dnT": dnT,
    }


_CACHE = {}


def kernel(**inputs):
    if "prog" not in _CACHE:
        _CACHE["prog"] = build_program()
    nc, _ = _CACHE["prog"]
    in_maps = [_prep_core_inputs(inputs, c) for c in range(8)]
    res = run_bass_kernel_spmd(nc, in_maps, core_ids=list(range(8)))
    out = np.zeros((B_, L_, D_), np.float32)
    for c in range(8):
        b, half = c >> 1, c & 1
        out[b, half * T:(half + 1) * T] = np.asarray(res.results[c]["yout"])
    return out


if __name__ == "__main__":
    nc, _ = build_program()
    print("build ok, instructions:", nc.n_instructions())


# version 09
# speedup vs baseline: 1.2090x; 1.2090x over previous
import numpy as np
import ml_dtypes
import concourse.bass as bass
import concourse.mybir as mybir
from concourse.bass_utils import run_bass_kernel_spmd

F32 = mybir.dt.float32
BF16 = mybir.dt.bfloat16
ALU = mybir.AluOpType
ACT = mybir.ActivationFunctionType
AXX = mybir.AxisListType.X

B_, L_, D_ = 4, 4096, 1024
T = 2048
NTC = 16
Q = 128
H, P, NS = 32, 64, 64
E = 2048
CD = 2176
DIP = 4256
NEX = 36
EPS = 1e-5
EPSI = 1e-6
TILE = 256
NTILE = T // TILE
NEC = 17


class Sems:
    def __init__(self, nc, names):
        self.h, self.n, self._g = {}, {}, []
        for nm in names:
            g = nc.semaphore(nm)
            self.h[nm] = g.__enter__()
            self._g.append(g)
            self.n[nm] = 0

    def inc(self, instr, nm, k=1):
        instr.then_inc(self.h[nm], k)
        self.n[nm] += k
        return instr

    def wait(self, eng, nm, val=None):
        eng.wait_ge(self.h[nm], self.n[nm] if val is None else val)


def build_program(debug=()):
    nc = bass.Bass("TRN2", target_bir_lowering=False)
    d = {}

    def din(name, shape):
        d[name] = nc.dram_tensor(name, list(shape), F32, kind="ExternalInput").ap()

    din("xin", (T, D_)); din("xaux", (T, D_))
    din("winT", (D_, DIP)); din("winTa", (D_, CD + H))
    din("lnrows", (1, 4 * D_))
    din("cw", (128, NEC * 4)); din("cb", (128, NEC))
    din("cwa", (128, NEC * 4)); din("cba", (128, NEC))
    din("dtb", (H, 1)); din("alog", (H, 1)); din("dtba", (H, 1)); din("aloga", (H, 1))
    din("dexp", (128, 16)); din("nwp", (128, 16))
    din("woutT", (E, D_)); din("gwT", (D_, H)); din("ebias", (1, H))
    din("upT", (NEX, D_, 512)); din("dnT", (NEX, 512, D_))
    yout = nc.dram_tensor("yout", [T, D_], F32, kind="ExternalOutput").ap()

    x_sp = nc.dram_tensor("x_sp", [16, 128, T], BF16, kind="Internal").ap()
    xa_sp = nc.dram_tensor("xa_sp", [16, 128, T], BF16, kind="Internal").ap()
    sz_sp = nc.dram_tensor("sz_sp", [16, 128, T], BF16, kind="Internal").ap()
    x1_sp = nc.dram_tensor("x1_sp", [T, D_], F32, kind="Internal").ap()

    dbg = {}
    dbg_shapes = {"u": (128, 8 * T), "dtv": (H, T), "acinc": (H, T),
                  "Hinit": (NS, E), "Hend": (NS, E), "y0": (128, E),
                  "x1": (128, NTC * D_), "scores": (128, NTC * H),
                  "comb": (128, NTC * H), "macc": (128, NTC * D_)}
    for nm in debug:
        dbg[nm] = nc.dram_tensor("dbg_" + nm, list(dbg_shapes[nm]), F32,
                                 kind="ExternalOutput").ap()

    S = Sems(nc, ["TC", "VC", "SC", "GC", "SYD", "GPD", "ST"])
    guards = []

    def sbuf(name, shape, dt, **kw):
        g = nc.sbuf_tensor(name, shape, dt, **kw)
        t = g.__enter__()
        guards.append(g)
        return t

    c_strict = sbuf("c_strict", [128, 128], BF16)
    c_umask = sbuf("c_umask", [128, 128], BF16)
    c_idf = sbuf("c_idf", [128, 128], F32)
    identb = sbuf("identb", [128, 128], BF16)
    c_ind = sbuf("c_ind", [32, 2048], BF16)
    ones1b = sbuf("ones1b", [1, 128], BF16)
    ones1f = sbuf("ones1f", [1, 128], F32)
    onescol = sbuf("onescol", [128, 1], BF16)
    c_eb = sbuf("c_eb", [128, H], F32)
    cw_s = sbuf("cw_s", [128, NEC * 4], F32)
    cb_s = sbuf("cb_s", [128, NEC], F32)
    cwa_s = sbuf("cwa_s", [128, NEC * 4], F32)
    cba_s = sbuf("cba_s", [128, NEC], F32)
    dexp_s = sbuf("dexp_s", [128, 16], F32)
    nwp_s = sbuf("nwp_s", [128, 16], F32)
    ebrow = sbuf("ebrow", [1, H], F32)
    gwh = sbuf("gwh", [128, 8 * H], BF16)
    gwl = sbuf("gwl", [128, 8 * H], BF16)
    scorestk = sbuf("scorestk", [128, NTC * H], F32)
    combB = sbuf("combB", [128, NTC * H], F32)
    combTb = sbuf("combTb", [NEX, T], BF16)
    stat = sbuf("stat", [128, 8], F32)
    sqscr = sbuf("sqscr", [128, D_], F32)
    rmsrow = sbuf("rmsrow", [1, 256], F32)
    rscr = sbuf("rscr", [128, 40], F32)
    rscr2 = sbuf("rscr2", [128, 40], F32)
    cvals = sbuf("cvals", [128, 2], F32)

    psA = nc.psum_tensor("psA", [128, 2048], F32).__enter__()
    psB = nc.psum_tensor("psB", [128, 2048], F32).__enter__()


    arena = []

    def push(name, shape, dt, **kw):
        g = nc.sbuf_tensor(name, shape, dt, **kw)
        t = g.__enter__()
        arena.append(g)
        return t

    def mark():
        return len(arena)

    def pop_to(m):
        while len(arena) > m:
            arena.pop().__exit__(None, None, None)

    m_ab = mark()
    pk1 = push("pk1", [64, T], F32)
    pk2 = push("pk2", [64, T], F32)
    pk3 = push("pk3", [64, T], F32)
    pk4 = push("pk4", [64, T], F32)
    pk5 = push("pk5", [64, T], F32)
    dtraw, dtrawA = pk1[0:32, :], pk1[32:64, :]
    dtv, dtvA = pk2[0:32, :], pk2[32:64, :]
    acinc, acincA = pk3[0:32, :], pk3[32:64, :]
    w2m, w2a = pk4[0:32, :], pk4[32:64, :]
    scr32, scr32A = pk5[0:32, :], pk5[32:64, :]
    apack = push("apack", [64, 1], F32)
    iotaE = push("iotaE", [128, H], F32)
    gwf = push("gwf", [128, 8 * H], F32)
    dtpack = push("dtpack", [64, 1], F32)
    acol, acola = apack[0:32, :], apack[32:64, :]
    dtb_s, dtba_s = dtpack[0:32, :], dtpack[32:64, :]
    erow = push("erow", [H, T], BF16)
    lam64 = push("lam64", [64, NTC * H], F32)
    lam64A = push("lam64A", [64, NTC * H], F32)
    dtvT = push("dtvT", [128, NTC * H], F32)
    dtvTA = push("dtvTA", [128, NTC * H], F32)
    w2T = push("w2T", [128, NTC * H], F32)
    w2TA = push("w2TA", [128, NTC * H], F32)
    bcm = push("bcm", [128, T], BF16)
    bcma = push("bcma", [128, T], BF16)
    h0 = push("h0", [NS, E], F32)
    h1 = push("h1", [NS, E], F32)
    HB = [h0, h1]

    blk = nc.Block().__enter__()
    TT, VV, SS, GG, SY = blk.tensor, blk.vector, blk.scalar, blk.gpsimd, blk.sync

    def consts_g(g):
        for nm, dst in [("cw", cw_s), ("cb", cb_s), ("cwa", cwa_s), ("cba", cba_s),
                        ("dexp", dexp_s), ("nwp", nwp_s), ("dtb", dtb_s),
                        ("dtba", dtba_s), ("ebias", ebrow), ("alog", acol),
                        ("aloga", acola)]:
            S.inc(g.dma_start(dst[:, :], d[nm]), "GPD", 16)
        S.inc(g.dma_start(gwf[:, :].rearrange("p (k h) -> p k h", k=8),
                          d["gwT"].rearrange("(k p) h -> p k h", p=128)), "GPD", 16)
        g.memset(ones1b[:, :], 1.0)
        g.memset(ones1f[:, :], 1.0)
        g.memset(onescol[:, :], 1.0)
        g.memset(c_strict[:, :], 1.0)
        g.memset(c_umask[:, :], 1.0)
        g.memset(c_idf[:, :], 1.0)
        g.memset(identb[:, :], 1.0)
        g.memset(combTb[32:36, :], 1.0)
        g.drain()
        g.affine_select(c_strict[:, :], c_strict[:, :], [[-1, 128]], ALU.is_ge, 0.0,
                        base=-1, channel_multiplier=1)
        g.affine_select(c_umask[:, :], c_umask[:, :], [[1, 128]], ALU.is_ge, 0.0,
                        channel_multiplier=-1)
        g.affine_select(c_idf[:, :], c_idf[:, :], [[-1, 128]], ALU.is_equal, 0.0,
                        channel_multiplier=1)
        g.drain()
        g.affine_select(identb[:, :], identb[:, :], [[-1, 128]], ALU.is_equal, 0.0,
                        channel_multiplier=1)
        g.memset(cvals[:, 0:1], EPS)
        g.memset(cvals[:, 1:2], 1.0)
        g.memset(c_ind[:, :], 1.0)
        g.drain()
        g.affine_select(c_ind[:, :], c_ind[:, :], [[1, 2048]], ALU.is_ge, 0.0,
                        channel_multiplier=-64)
        g.drain()
        g.affine_select(c_ind[:, :], c_ind[:, :], [[-1, 2048]], ALU.is_ge, 0.0,
                        base=63, channel_multiplier=64)
        g.iota(iotaE[:, :], [[1, H]], base=0, channel_multiplier=0,
               allow_small_or_imprecise_dtypes=True)
        S.inc(g.memset(h0[:, :], 0.0), "GC")
    GG(consts_g)

    def consts_v(v):
        S.wait(v, "GC")
        S.wait(v, "GPD")
        v.tensor_copy(gwh[:, :], gwf[:, :])
        v.drain()
        S.inc(v.tensor_tensor(gwl[:, :], gwf[:, :], gwh[:, :], ALU.subtract), "VC")
    VV(consts_v)

    def consts_s(sc):
        S.wait(sc, "GPD")
        sc.activation(acol[:, :], acol[:, :], ACT.Exp)
        S.inc(sc.activation(acola[:, :], acola[:, :], ACT.Exp), "SC")
    SS(consts_s)

    def consts_v2(v):
        S.wait(v, "SC")
        v.tensor_scalar_mul(acol[:, :], acol[:, :], -1.0)
        S.inc(v.tensor_scalar_mul(acola[:, :], acola[:, :], -1.0), "VC")
    VV(consts_v2)

    def eb_t(t):
        S.wait(t, "VC")
        S.inc(t.matmul(psA[:, 0:H], ones1f[0:1, :], ebrow[0:1, :], start=True,
                       stop=True), "TC")
    TT(eb_t)

    def eb_v(v):
        S.wait(v, "TC")
        S.inc(v.scalar_tensor_tensor(c_eb[:, :], iotaE[:, :], -EPSI, psA[:, 0:H],
                                     ALU.mult, ALU.add), "VC")
    VV(eb_v)

    def fence():
        for body in [TT, VV, SS, GG, SY]:
            def f(e):
                for nm in ["TC", "VC", "SC", "GC", "SYD", "GPD"]:
                    S.wait(e, nm)
            body(f)

    def ln_chunk(src_ap, dst_ap):
        def v1(v):
            S.wait(v, "SC")
            v.reduce_sum(stat[:, 1:2], src_ap, axis=AXX)
            v.drain()
            S.inc(v.tensor_scalar_mul(stat[:, 2:3], stat[:, 1:2], 1.0 / D_), "VC")
        VV(v1)

        def s1(sc):
            S.wait(sc, "VC")
            S.inc(sc.activation(sqscr[:, :], src_ap, ACT.Square,
                                accum_out=stat[:, 0:1]), "SC")
        SS(s1)

        def v2(v):
            S.wait(v, "SC")
            v.tensor_scalar_mul(stat[:, 3:4], stat[:, 0:1], 1.0 / D_)
            v.drain()
            v.tensor_tensor(stat[:, 4:5], stat[:, 2:3], stat[:, 2:3], ALU.mult)
            v.drain()
            S.inc(v.tensor_tensor(stat[:, 4:5], stat[:, 3:4], stat[:, 4:5],
                                  ALU.subtract), "VC")
        VV(v2)

        def s2(sc):
            S.wait(sc, "VC")
            sc.activation(stat[:, 5:6], stat[:, 4:5], ACT.Ln, bias=cvals[:, 0:1])
            S.inc(sc.activation(stat[:, 6:7], stat[:, 5:6], ACT.Exp, scale=-0.5),
                  "SC")
        SS(s2)

        def v3(v):
            S.wait(v, "SC")
            S.inc(v.tensor_scalar(dst_ap, src_ap, stat[:, 2:3], stat[:, 6:7],
                                  ALU.subtract, ALU.mult), "VC")
        VV(v3)

    m_a = mark()
    ufm = push("ufm", [128, 8 * T], BF16, is_transpose_dest=True)
    wins = push("wins", [128, 3 * 8 * 128], BF16)
    gbB = push("gbB", [128, 4 * D_], F32)
    gbrow = push("gbrow", [1, D_], F32)
    xtile = push("xtile", [128, D_], F32)
    utile = push("utile", [128, D_], F32)
    ubtile = push("ubtile", [128, D_], BF16)
    fbuf0 = push("fbuf0", [128, T + 3], F32)
    fbuf1 = push("fbuf1", [128, T + 3], F32)
    cacc0 = push("cacc0", [128, T], F32)
    ubz0 = push("ubz0", [128, T], BF16)
    ubz1 = push("ubz1", [128, T], BF16)
    if True:

        for i in range(4):
            def gb_load(g, i=i):
                S.wait(g, "TC")
                S.inc(g.dma_start(gbrow[:, :], d["lnrows"][0:1, i * D_:(i + 1) * D_]),
                      "GPD", 16)
            GG(gb_load)
            gpd_r = S.n["GPD"]

            def bc_t(t, i=i, gpd_r=gpd_r):
                S.wait(t, "GPD", gpd_r)
                S.wait(t, "VC")
                t.matmul(psA[:, 0:512], ones1f[0:1, :],
                         gbrow[0:1, 0:512], start=True, stop=True)
                S.inc(t.matmul(psA[:, 512:1024], ones1f[0:1, :],
                               gbrow[0:1, 512:1024], start=True,
                               stop=True), "TC")
            TT(bc_t)

            def bc_v(v, i=i):
                S.wait(v, "TC")
                S.inc(v.tensor_copy(gbB[:, i * D_:(i + 1) * D_], psA[:, 0:1024]),
                      "VC")
            VV(bc_v)

        def phase_ln1(src_dram, goff):
            for tc in range(NTC):
                def ld(y, tc=tc):
                    S.wait(y, "VC")
                    S.wait(y, "SC")
                    S.inc(y.dma_start(xtile[:, :],
                                      src_dram[tc * 128:(tc + 1) * 128, :]),
                          "SYD", 16)
                SY(ld)

                ln_chunk(xtile[:, :], utile[:, :])

                def v3(v, goff=goff):
                    S.wait(v, "SYD")
                    v.tensor_tensor(utile[:, :], utile[:, :],
                                    gbB[:, goff * D_:(goff + 1) * D_], ALU.mult)
                    S.inc(v.tensor_tensor(ubtile[:, :], utile[:, :],
                                          gbB[:, (goff + 1) * D_:(goff + 2) * D_],
                                          ALU.add), "VC")
                VV(v3)

                def tr(y, tc=tc):
                    S.wait(y, "VC")
                    for kd in range(8):
                        S.inc(y.dma_start(
                            ufm[:, kd * T + tc * 128: kd * T + (tc + 1) * 128],
                            ubtile[:, kd * 128:(kd + 1) * 128], transpose=True),
                            "SYD", 16)
                SY(tr)

        def phase_ln1_fixed(src_dram, goff):
            for tc in range(NTC):
                def ld(y, tc=tc):
                    S.wait(y, "VC")
                    S.wait(y, "SC")
                    S.inc(y.dma_start(xtile[:, :],
                                      src_dram[tc * 128:(tc + 1) * 128, :]),
                          "SYD", 16)
                SY(ld)
                syd_x = S.n["SYD"]

                def s1(sc, syd_x=syd_x):
                    S.wait(sc, "SYD", syd_x)
                    S.inc(sc.activation(sqscr[:, :], xtile[:, :], ACT.Square,
                                        accum_out=stat[:, 0:1]), "SC")
                SS(s1)

                def v1(v, syd_x=syd_x):
                    S.wait(v, "SYD", syd_x)
                    v.reduce_sum(stat[:, 1:2], xtile[:, :], axis=AXX)
                    v.drain()
                    S.wait(v, "SC")
                    v.tensor_scalar_mul(stat[:, 2:3], stat[:, 1:2], 1.0 / D_)
                    v.tensor_scalar_mul(stat[:, 3:4], stat[:, 0:1], 1.0 / D_)
                    v.drain()
                    v.tensor_tensor(stat[:, 4:5], stat[:, 2:3], stat[:, 2:3], ALU.mult)
                    v.drain()
                    S.inc(v.tensor_tensor(stat[:, 4:5], stat[:, 3:4], stat[:, 4:5],
                                          ALU.subtract), "VC")
                VV(v1)

                def s2(sc):
                    S.wait(sc, "VC")
                    sc.activation(stat[:, 5:6], stat[:, 4:5], ACT.Ln, bias=cvals[:, 0:1])
                    S.inc(sc.activation(stat[:, 6:7], stat[:, 5:6], ACT.Exp,
                                        scale=-0.5), "SC")
                SS(s2)

                def v2(v, goff=goff):
                    S.wait(v, "SC")
                    S.wait(v, "SYD")
                    v.tensor_scalar(utile[:, :], xtile[:, :], stat[:, 2:3],
                                    stat[:, 6:7], ALU.subtract, ALU.mult)
                    v.drain()
                    v.tensor_tensor(utile[:, :], utile[:, :],
                                    gbB[:, goff * D_:(goff + 1) * D_], ALU.mult)
                    v.drain()
                    S.inc(v.tensor_tensor(ubtile[:, :], utile[:, :],
                                          gbB[:, (goff + 1) * D_:(goff + 2) * D_],
                                          ALU.add), "VC")
                VV(v2)

                def tr(y, tc=tc):
                    S.wait(y, "VC")
                    for kd in range(8):
                        S.inc(y.dma_start(
                            ufm[:, kd * T + tc * 128: kd * T + (tc + 1) * 128],
                            ubtile[:, kd * 128:(kd + 1) * 128], transpose=True),
                            "SYD", 16)
                SY(tr)

        def phase_a2(winT_dram, necols, spill, bc_dst, dtraw_dst, zch, cw_t, cb_t):
            dt_bp = dtraw_dst.base_partition()
            nec_all = necols // 128
            rem = necols - nec_all * 128
            n_ec = nec_all + (1 if rem else 0)
            src3 = winT_dram.rearrange("(k p) e -> p k e", p=128)

            def zt(v):
                v.memset(fbuf0[:, 0:3], 0.0)
                S.inc(v.memset(fbuf1[:, 0:3], 0.0), "VC")
            VV(zt)
            ufm_done = S.n["SYD"]

            wslab = [wins[:, s * 1024:(s + 1) * 1024]
                         .rearrange("p (k e) -> p k e", e=128) for s in range(3)]
            slab_gpd = {}

            def wl0(g):
                for s in range(min(3, n_ec)):
                    rows = 128 if s < nec_all else rem
                    S.inc(g.dma_start(wslab[s][:, :, 0:rows],
                                      src3[:, :, s * 128:s * 128 + rows]),
                          "GPD", 16)
                    slab_gpd[s] = S.n["GPD"]
            GG(wl0)

            last_evac = {psl: ("VC", S.n["VC"]) for psl in range(4)}
            fb_free = [None, None]
            cc_free = [None]
            ub_free = [None, None]

            for ec in range(n_ec):
                mm_rows = 128 if ec < nec_all else rem
                slot = ec % 3
                gpd_here = slab_gpd[ec]
                par = ec % 2
                fb = [fbuf0, fbuf1][par]
                cc = cacc0
                ub = [ubz0, ubz1][par]
                evac_cnts = []

                for ti in range(NTILE):
                    psl = ti % 4
                    if ec < nec_all:
                        sl = psA[0:mm_rows, psl * 512:psl * 512 + TILE]
                    else:
                        sl = psA[dt_bp:dt_bp + mm_rows, psl * 512:psl * 512 + TILE]

                    def mm(t, ec=ec, ti=ti, sl=sl, mm_rows=mm_rows, slot=slot,
                           gpd_here=gpd_here, ufm_done=ufm_done, psl=psl):
                        S.wait(t, "GPD", gpd_here)
                        S.wait(t, "SYD", ufm_done)
                        if psl in last_evac and last_evac[psl] is not None:
                            S.wait(t, last_evac[psl][0], last_evac[psl][1])
                        ins = None
                        for kd in range(8):
                            ins = t.matmul(
                                sl, wslab[slot][:, kd, 0:mm_rows],
                                ufm[:, kd * T + ti * TILE: kd * T + (ti + 1) * TILE],
                                start=(kd == 0), stop=(kd == 7))
                        S.inc(ins, "TC")
                    TT(mm)
                    tc_here = S.n["TC"]
                    if ti == NTILE - 1 and ec + 3 < n_ec:
                        def wln(g, ec=ec, slot=slot, tc_here=tc_here):
                            S.wait(g, "TC", tc_here)
                            rows = 128 if ec + 3 < nec_all else rem
                            S.inc(g.dma_start(
                                wslab[slot][:, :, 0:rows],
                                src3[:, :, (ec + 3) * 128:(ec + 3) * 128 + rows]),
                                "GPD", 16)
                        GG(wln)
                        slab_gpd[ec + 3] = S.n["GPD"]

                    if ec < nec_all:
                        def ev(v, ti=ti, sl=sl, tc_here=tc_here, fb=fb, par=par):
                            S.wait(v, "TC", tc_here)
                            if ti == 0 and fb_free[par] is not None:
                                S.wait(v, fb_free[par][0], fb_free[par][1])
                            S.inc(v.tensor_copy(
                                fb[:, 3 + ti * TILE: 3 + (ti + 1) * TILE], sl),
                                "VC")
                        VV(ev)
                    else:
                        def ev(v, ti=ti, sl=sl, tc_here=tc_here,
                               dtraw_dst=dtraw_dst):
                            S.wait(v, "TC", tc_here)
                            S.inc(v.tensor_copy(
                                dtraw_dst[:, ti * TILE:(ti + 1) * TILE], sl), "VC")
                        VV(ev)
                    last_evac[psl] = ("VC", S.n["VC"])

                if ec >= nec_all:
                    continue

                if ec < zch:
                    def zsil(sc, fb=fb, ub=ub, par=par):
                        S.wait(sc, "VC")
                        if ub_free[par] is not None:
                            S.wait(sc, ub_free[par][0], ub_free[par][1])
                        S.inc(sc.activation(ub[:, :], fb[:, 3:3 + T], ACT.Silu),
                              "SC")
                    SS(zsil)
                    fb_free[par] = ("SC", S.n["SC"])

                    def zsp(y, ec=ec, ub=ub):
                        S.wait(y, "SC")
                        S.inc(y.dma_start(sz_sp[ec, :, :], ub[:, :]), "SYD", 16)
                    SY(zsp)
                    ub_free[par] = ("SYD", S.n["SYD"])
                else:
                    xc = ec - zch
                    def cv(v, fb=fb, cc=cc, xc=xc, par=par, cw_t=cw_t):
                        if cc_free[0] is not None:
                            S.wait(v, cc_free[0][0], cc_free[0][1])
                        v.tensor_scalar(cc[:, :], fb[:, 0:T],
                                        cw_t[:, xc * 4:xc * 4 + 1], None, ALU.mult)
                        ins = None
                        for j in range(1, 4):
                            v.drain()
                            ins = v.scalar_tensor_tensor(
                                cc[:, :], fb[:, j:j + T],
                                cw_t[:, xc * 4 + j:xc * 4 + j + 1], cc[:, :],
                                ALU.mult, ALU.add)
                        S.inc(ins, "VC")
                    VV(cv)
                    fb_free[par] = ("VC", S.n["VC"])

                    def csil(sc, cc=cc, ub=ub, par=par, xc=xc, cb_t=cb_t):
                        S.wait(sc, "VC")
                        if ub_free[par] is not None:
                            S.wait(sc, ub_free[par][0], ub_free[par][1])
                        S.inc(sc.activation(ub[:, :], cc[:, :], ACT.Silu,
                                            bias=cb_t[:, xc:xc + 1]), "SC")
                    SS(csil)
                    cc_free[0] = ("SC", S.n["SC"])

                    if xc < 16:
                        def xsp(y, xc=xc, ub=ub, spill=spill):
                            S.wait(y, "SC")
                            S.inc(y.dma_start(spill[xc, :, :], ub[:, :]),
                                  "SYD", 16)
                        SY(xsp)
                        ub_free[par] = ("SYD", S.n["SYD"])
                    else:
                        def bcp(v, ub=ub, bc_dst=bc_dst):
                            S.wait(v, "SC")
                            S.inc(v.tensor_copy(bc_dst[:, :], ub[:, :]), "VC")
                        VV(bcp)
                        ub_free[par] = ("VC", S.n["VC"])

        def phase_a3(dtr, dtb_t, a_t, dv, ac, w2_, l64, dvT, w2T_, er, scr):
            bp3 = dtr.base_partition()

            def sp1(sc):
                S.wait(sc, "VC")
                sc.activation(scr[:, :], dtr[:, :], ACT.Abs, bias=dtb_t[:, 0:1])
                sc.activation(scr[:, :], scr[:, :], ACT.Exp, scale=-1.0)
                S.inc(sc.activation(scr[:, :], scr[:, :], ACT.Ln,
                                    bias=cvals[bp3:bp3 + 32, 1:2]), "SC")
            SS(sp1)

            def sp2(v):
                S.wait(v, "SC")
                v.tensor_scalar(dv[:, :], dtr[:, :], dtb_t[:, 0:1], 0.0,
                                ALU.add, ALU.max)
                v.drain()
                v.tensor_tensor(dv[:, :], dv[:, :], scr[:, :], ALU.add)
                v.drain()
                ins = None
                v.tensor_scalar(scr[:, :], dv[:, :], a_t[:, 0:1], None, ALU.mult)
                v.drain()
                for c in range(NTC):
                    ins = v.tensor_tensor_scan(ac[:, c * Q:(c + 1) * Q],
                                               scr[:, c * Q:(c + 1) * Q],
                                               scr[:, c * Q:(c + 1) * Q], 0.0,
                                               ALU.add, ALU.bypass)
                S.inc(ins, "VC")
            VV(sp2)

            if er is not None:
                def sp4(sc):
                    S.wait(sc, "VC")
                    S.inc(sc.activation(er[:, :], ac[:, :], ACT.Exp), "SC")
                SS(sp4)

            for c in range(NTC):
                def trp(t, c=c):
                    S.wait(t, "VC")
                    S.wait(t, "SC")
                    bp = ac.base_partition()
                    t.transpose(psB[0:1, 64:96], ac[:, (c + 1) * Q - 1:(c + 1) * Q],
                                c_idf[bp:bp + 32, bp:bp + 32])
                    bpd = dv.base_partition()
                    S.inc(t.transpose(psB[:, 32:64], dv[:, c * Q:(c + 1) * Q],
                                      c_idf[bpd:bpd + 32, bpd:bpd + 32]), "TC")
                TT(trp)

                def tev(v, c=c, dvT=dvT):
                    S.wait(v, "TC")
                    v.tensor_copy(dvT[:, c * H:(c + 1) * H], psB[:, 32:64])
                    S.inc(v.tensor_copy(rmsrow[0:1, 0:32], psB[0:1, 64:96]), "VC")
                VV(tev)

                def lse(sc, c=c):
                    S.wait(sc, "VC")
                    S.inc(sc.activation(rmsrow[0:1, 32:64], rmsrow[0:1, 0:32],
                                        ACT.Exp), "SC")
                SS(lse)

                def lmm(t, c=c):
                    S.wait(t, "SC")
                    S.inc(t.matmul(psB[0:64, 64:96], ones1f[0:1, 0:64],
                                   rmsrow[0:1, 32:64], start=True, stop=True), "TC")
                TT(lmm)

                def lev(v, c=c, l64=l64):
                    S.wait(v, "TC")
                    S.inc(v.tensor_copy(l64[:, c * H:(c + 1) * H], psB[0:64, 64:96]),
                          "VC")
                VV(lev)

                def w2v(v, c=c, ac=ac):
                    S.inc(v.tensor_scalar(scr[:, c * Q:(c + 1) * Q],
                                          ac[:, c * Q:(c + 1) * Q],
                                          ac[:, (c + 1) * Q - 1:(c + 1) * Q], None,
                                          ALU.subtract), "VC")
                VV(w2v)

                def w2e(sc, c=c):
                    S.wait(sc, "VC")
                    S.inc(sc.activation(scr[:, c * Q:(c + 1) * Q],
                                        scr[:, c * Q:(c + 1) * Q], ACT.Exp,
                                        scale=-1.0), "SC")
                SS(w2e)

                def w2f(v, c=c, dv=dv, w2_=w2_):
                    S.wait(v, "SC")
                    S.inc(v.tensor_tensor(w2_[:, c * Q:(c + 1) * Q],
                                          dv[:, c * Q:(c + 1) * Q],
                                          scr[:, c * Q:(c + 1) * Q], ALU.mult),
                          "VC")
                VV(w2f)

                def w2t(t, c=c, w2_=w2_):
                    S.wait(t, "VC")
                    bpw = w2_.base_partition()
                    S.inc(t.transpose(psB[:, 96:128], w2_[:, c * Q:(c + 1) * Q],
                                      c_idf[bpw:bpw + 32, bpw:bpw + 32]), "TC")
                TT(w2t)

                def w2tv(v, c=c, w2T_=w2T_):
                    S.wait(v, "TC")
                    S.inc(v.tensor_copy(w2T_[:, c * H:(c + 1) * H], psB[:, 96:128]),
                          "VC")
                VV(w2tv)

        phase_ln1_fixed(d["xaux"], 2)
        phase_a2(d["winTa"], CD + H, xa_sp, bcma, dtrawA, 0, cwa_s, cba_s)
        fence()
        phase_ln1_fixed(d["xin"], 0)
        phase_a2(d["winT"], DIP, x_sp, bcm, dtraw, 16, cw_s, cb_s)
        fence()
        phase_a3(dtrawA, dtba_s, acola, dtvA, acincA, w2a, lam64A, dtvTA, w2TA,
                 None, scr32A)
        phase_a3(dtraw, dtb_s, acol, dtv, acinc, w2m, lam64, dtvT, w2T, erow, scr32)
        fence()
        if "u" in dbg:
            def du(g):
                S.inc(g.dma_start(dbg["u"], ufm[:, :]), "ST", 16)
            GG(du)
        if "dtv" in dbg:
            def dd(y):
                S.inc(y.dma_start(dbg["dtv"], dtv[:, :]), "ST", 16)
            SY(dd)
        if "acinc" in dbg:
            def da(y):
                S.inc(y.dma_start(dbg["acinc"], acinc[:, :]), "ST", 16)
            SY(da)

    pop_to(m_a)
    m_bc = mark()
    wout2 = push("wout2", [128, 16 * D_], BF16)
    xt0 = push("xt0", [128, E], BF16, is_transpose_dest=True)
    xt1 = push("xt1", [128, E], BF16, is_transpose_dest=True)
    wq = push("wq", [128, 2 * E], BF16)
    e2sb = push("e2sb", [128, E], BF16)
    gpmb = push("gpmb", [128, 128], BF16)
    zal = push("zal", [128, E], BF16)
    btp = push("btp", [128, 64], BF16)
    ybuf = push("ybuf", [128, E], F32)
    gbuf = push("gbuf", [128, E], F32)
    g2b = push("g2b", [128, E], BF16)
    xck = push("xck", [128, E], BF16)
    szt = push("szt", [128, E], BF16)
    ynb = push("ynb", [128, E], BF16)
    rib = push("rib", [128, 128], F32)
    dab = push("dab", [128, 32], F32)
    vbuf = push("vbuf", [128, D_], F32)
    xres = push("xres", [128, D_], F32)
    x1t = push("x1t", [128, D_], F32)
    cmt = push("cmt", [64, Q], BF16)
    if True:

        XT = [xt0, xt1]

        def w2l(g):
            S.inc(g.dma_start(wout2[:, :].rearrange("p (k dd) -> p k dd", k=16),
                              d["woutT"].rearrange("(k p) dd -> p k dd", p=128)),
                  "GPD", 16)
        GG(w2l)
        gpd_wout = S.n["GPD"]

        def w2sc(v):
            S.wait(v, "GPD", gpd_wout)
            ins = None
            for ec in range(16):
                ins = v.tensor_scalar(wout2[:, ec * D_:(ec + 1) * D_],
                                      wout2[:, ec * D_:(ec + 1) * D_],
                                      nwp_s[:, ec:ec + 1], None, ALU.mult)
            S.inc(ins, "VC")
        VV(w2sc)

        for c in range(NTC):
            par = c % 2
            def xtl(y, c=c, par=par):
                S.wait(y, "VC")
                for ec in range(16):
                    S.inc(y.dma_start(XT[par][:, ec * 128:(ec + 1) * 128],
                                      xa_sp[ec, :, c * Q:(c + 1) * Q],
                                      transpose=True), "SYD", 16)
            SY(xtl)
            syd_xt = S.n["SYD"]

            def btr(t, c=c):
                S.wait(t, "VC")
                S.inc(t.transpose(psB[:, 0:32].bitcast(BF16),
                                  bcma[0:64, c * Q:(c + 1) * Q],
                                  identb[0:64, 0:64]), "TC")
            TT(btr)

            def bev(v, c=c):
                S.wait(v, "TC")
                S.inc(v.tensor_copy(btp[:, :], psB[:, 0:32].bitcast(BF16)), "VC")
            VV(bev)

            def zb(v, c=c, par=par, syd_xt=syd_xt):
                S.wait(v, "SYD", syd_xt)
                S.inc(v.tensor_tensor(
                    zal[:, :].rearrange("p (h q) -> p h q", q=P),
                    XT[par][:, :].rearrange("p (h q) -> p h q", q=P),
                    w2TA[:, c * H:(c + 1) * H].unsqueeze(2)
                        .broadcast_to([128, H, P]), ALU.mult), "VC")
            VV(zb)

            def scm(t, c=c):
                S.wait(t, "VC")
                ins = None
                for i in range(4):
                    ins = t.matmul(psA[0:64, i * 512:(i + 1) * 512], btp[:, :],
                                   zal[:, i * 512:(i + 1) * 512], start=True,
                                   stop=True)
                S.inc(ins, "TC")
            TT(scm)

            def hup(v, c=c):
                S.wait(v, "TC")
                src, dst = HB[c % 2], HB[1 - c % 2]
                v.tensor_tensor(dst[:, :].rearrange("n (h q) -> n h q", q=P),
                                src[:, :].rearrange("n (h q) -> n h q", q=P),
                                lam64A[:, c * H:(c + 1) * H].unsqueeze(2)
                                    .broadcast_to([64, H, P]), ALU.mult)
                v.drain()
                S.inc(v.tensor_tensor(dst[:, :], dst[:, :], psA[0:64, :], ALU.add),
                      "VC")
            VV(hup)
        fence()
        if "Hinit" in dbg:
            def dh(y):
                S.inc(y.dma_start(dbg["Hinit"], HB[0][:, :]), "ST", 16)
            SY(dh)

        vc_hup = {}
        vc_resv = {}
        vc_yc = {}
        sc_a3 = S.n["SC"]
        for c in range(NTC):
            par = c % 2
            hsrc, hdst = HB[c % 2], HB[1 - c % 2]

            def xtl(y, c=c, par=par):
                if c - 1 in vc_resv:
                    S.wait(y, "VC", vc_resv[c - 1])
                for ec in range(16):
                    S.inc(y.dma_start(XT[par][:, ec * 128:(ec + 1) * 128],
                                      x_sp[ec, :, c * Q:(c + 1) * Q],
                                      transpose=True), "SYD", 16)
                S.inc(y.dma_start(xck[:, :].rearrange("p (ec q) -> p ec q", q=Q),
                                  x_sp[:, :, c * Q:(c + 1) * Q].transpose([1, 0, 2])),
                      "SYD", 16)
                S.inc(y.dma_start(szt[:, :].rearrange("p (ec q) -> p ec q", q=Q),
                                  sz_sp[:, :, c * Q:(c + 1) * Q].transpose([1, 0, 2])),
                      "SYD", 16)
                S.inc(y.dma_start(xres[:, :], d["xin"][c * 128:(c + 1) * 128, :]),
                      "SYD", 16)
                S.inc(y.dma_start(cmt[:, :], bcm[64:128, c * Q:(c + 1) * Q]),
                      "SYD", 16)
            SY(xtl)
            syd_xt = S.n["SYD"]

            def e2t(t, c=c):
                if c - 1 in vc_resv:
                    S.wait(t, "VC", vc_resv[c - 1])
                S.wait(t, "SC", sc_a3)
                ins = None
                for fc in range(16):
                    ins = t.matmul(psB[:, fc * 128:(fc + 1) * 128],
                                   c_ind[:, fc * 128:(fc + 1) * 128],
                                   erow[:, c * Q:(c + 1) * Q], start=True, stop=True)
                S.inc(ins, "TC")
            TT(e2t)

            def e2v(v, c=c):
                S.wait(v, "TC")
                S.inc(v.tensor_copy(e2sb[:, :], psB[:, :]), "VC")
            VV(e2v)
            vc_e2 = S.n["VC"]

            def dtar(v, c=c):
                S.inc(v.tensor_scalar(scr32[:, c * Q:(c + 1) * Q],
                                      dtv[:, c * Q:(c + 1) * Q], acol[:, 0:1], None,
                                      ALU.mult), "VC")
            VV(dtar)

            vc_dtar = S.n["VC"]

            def dtat(t, c=c, vc_dtar=vc_dtar):
                S.wait(t, "VC", vc_dtar)
                if c - 1 in vc_hup:
                    S.wait(t, "VC", vc_hup[c - 1])
                S.inc(t.transpose(psA[:, 0:32], scr32[:, c * Q:(c + 1) * Q],
                                  c_idf[0:32, 0:32]), "TC")
            TT(dtat)

            def dtae(v, c=c):
                S.wait(v, "TC")
                S.inc(v.tensor_copy(dab[:, :], psA[:, 0:32]), "VC")
            VV(dtae)

            def rhb(v, c=c):
                S.inc(v.tensor_tensor(
                    wq[:, :].rearrange("p (h q) -> p h q", q=Q),
                    c_umask[:, :].unsqueeze(1).broadcast_to([128, H, Q]),
                    dab[:, :].unsqueeze(2).broadcast_to([128, H, Q]),
                    ALU.mult), "VC")
            VV(rhb)
            vc_rh = S.n["VC"]

            sc_mde = [None]
            for hf in range(2):
                def cdf(t, c=c, hf=hf, vc_rh=vc_rh):
                    S.wait(t, "VC", vc_rh)
                    if hf == 1:
                        S.wait(t, "SC", sc_mde[0])
                    elif c - 1 in vc_hup:
                        S.wait(t, "VC", vc_hup[c - 1])
                    ins = None
                    for hh in range(16):
                        h = hf * 16 + hh
                        ins = t.matmul(psA[:, hh * 128:(hh + 1) * 128],
                                       c_strict[:, :], wq[:, h * 128:(h + 1) * 128],
                                       start=True, stop=True)
                    S.inc(ins, "TC")
                TT(cdf)

                def mde(sc, c=c, hf=hf):
                    S.wait(sc, "TC")
                    S.inc(sc.activation(wq[:, hf * E:(hf + 1) * E], psA[:, :],
                                        ACT.Exp), "SC")
                SS(mde)
                sc_mde[0] = S.n["SC"]

            def gpm(t, c=c, vc_e2=vc_e2, syd_xt=syd_xt):
                S.wait(t, "VC", vc_e2)
                S.wait(t, "SYD", syd_xt)
                S.inc(t.matmul(psB[:, 0:128], bcm[0:64, c * Q:(c + 1) * Q],
                               cmt[:, :], start=True, stop=True), "TC")
            TT(gpm)

            def gpe(v, c=c):
                S.wait(v, "TC")
                S.inc(v.tensor_tensor(gpmb[:, :], psB[:, 0:128], c_umask[:, :],
                                      ALU.mult), "VC")
            VV(gpe)

            def wpb(v, c=c):
                S.wait(v, "SC", sc_mde[0])
                ins = None
                for hf in range(2):
                    ins = v.tensor_tensor(
                        wq[:, hf * E:(hf + 1) * E].rearrange("p (h q) -> p h q", q=Q),
                        wq[:, hf * E:(hf + 1) * E].rearrange("p (h q) -> p h q", q=Q),
                        gpmb[:, :].unsqueeze(1).broadcast_to([128, 16, Q]),
                        ALU.mult)
                S.inc(ins, "VC")
            VV(wpb)

            def xdt(v, c=c, par=par, syd_xt=syd_xt):
                S.wait(v, "SYD", syd_xt)
                S.inc(v.tensor_tensor(
                    zal[:, :].rearrange("p (h q) -> p h q", q=P),
                    XT[par][:, :].rearrange("p (h q) -> p h q", q=P),
                    dtvT[:, c * H:(c + 1) * H].unsqueeze(2)
                        .broadcast_to([128, H, P]), ALU.mult), "VC")
            VV(xdt)

            def hcast(v, c=c, hsrc=hsrc):
                S.inc(v.tensor_copy(g2b[0:64, :], hsrc[:, :]), "VC")
            VV(hcast)
            vc_xdt = S.n["VC"]

            def yim(t, c=c, vc_xdt=vc_xdt):
                S.wait(t, "VC", vc_xdt)
                ins = None
                for h in range(H):
                    fc, odd = h // 2, h % 2
                    ins = t.matmul(psB[odd * 64:odd * 64 + 64,
                                       fc * 128:(fc + 1) * 128],
                                   zal[:, h * P:(h + 1) * P],
                                   wq[:, h * 128:(h + 1) * 128],
                                   start=True, stop=True)
                S.inc(ins, "TC")
            TT(yim)

            def ysm(t, c=c):
                S.wait(t, "SC", sc_mde[0])
                ins = None
                for fc in range(16):
                    ins = t.matmul(psA[:, fc * 128:(fc + 1) * 128],
                                   g2b[0:64, fc * 128:(fc + 1) * 128],
                                   cmt[:, :], start=True, stop=True)
                S.inc(ins, "TC")
            TT(ysm)
            tc_y = S.n["TC"]

            def yc(v, c=c, tc_y=tc_y, syd_xt=syd_xt):
                S.wait(v, "TC", tc_y)
                S.wait(v, "SYD", syd_xt)
                v.tensor_tensor(ybuf[:, :], psA[:, :], e2sb[:, :], ALU.mult)
                v.drain()
                v.tensor_tensor(ybuf[:, :], ybuf[:, :], psB[:, :], ALU.add)
                v.tensor_tensor(gbuf[:, :].rearrange("p (fc q) -> p fc q", q=Q),
                                xck[:, :].rearrange("p (fc q) -> p fc q", q=Q),
                                dexp_s[:, :].unsqueeze(2)
                                    .broadcast_to([128, 16, Q]), ALU.mult)
                v.drain()
                S.inc(v.tensor_tensor(ybuf[:, :], ybuf[:, :], gbuf[:, :], ALU.add),
                      "VC")
            VV(yc)
            vc_y = S.n["VC"]
            vc_yc[c] = vc_y
            if c == 0 and "y0" in dbg:
                def dy(y, vc_y=vc_y):
                    S.wait(y, "VC", vc_y)
                    S.inc(y.dma_start(dbg["y0"], ybuf[:, :]), "ST", 16)
                SY(dy)

            def zb2(v, c=c, par=par):
                S.inc(v.tensor_tensor(
                    zal[:, :].rearrange("p (h q) -> p h q", q=P),
                    XT[par][:, :].rearrange("p (h q) -> p h q", q=P),
                    w2T[:, c * H:(c + 1) * H].unsqueeze(2)
                        .broadcast_to([128, H, P]), ALU.mult), "VC")
            VV(zb2)

            def btr2(t, c=c):
                S.wait(t, "VC")
                S.inc(t.transpose(psB[:, 0:32].bitcast(BF16),
                                  bcm[0:64, c * Q:(c + 1) * Q],
                                  identb[0:64, 0:64]), "TC")
            TT(btr2)

            def bev2(v, c=c):
                S.wait(v, "TC")
                S.inc(v.tensor_copy(btp[:, :], psB[:, 0:32].bitcast(BF16)), "VC")
            VV(bev2)

            def scm2(t, c=c):
                S.wait(t, "VC")
                ins = None
                for i in range(4):
                    ins = t.matmul(psA[0:64, i * 512:(i + 1) * 512], btp[:, :],
                                   zal[:, i * 512:(i + 1) * 512], start=True,
                                   stop=True)
                S.inc(ins, "TC")
            TT(scm2)

            def hup2(v, c=c, hsrc=hsrc, hdst=hdst):
                S.wait(v, "TC")
                v.tensor_tensor(hdst[:, :].rearrange("n (h q) -> n h q", q=P),
                                hsrc[:, :].rearrange("n (h q) -> n h q", q=P),
                                lam64[:, c * H:(c + 1) * H].unsqueeze(2)
                                    .broadcast_to([64, H, P]), ALU.mult)
                v.drain()
                S.inc(v.tensor_tensor(hdst[:, :], hdst[:, :], psA[0:64, :], ALU.add),
                      "VC")
            VV(hup2)
            vc_hup[c] = S.n["VC"]

            def gat(v, c=c):
                v.tensor_tensor(gbuf[:, :], ybuf[:, :], szt[:, :], ALU.mult)
                v.drain()
                S.inc(v.tensor_tensor(g2b[:, :], gbuf[:, :], gbuf[:, :], ALU.mult),
                      "VC")
            VV(gat)

            def rmsm(t, c=c):
                S.wait(t, "VC")
                ins = None
                for fc in range(16):
                    ins = t.matmul(psB[0:1, 0:128], onescol[:, 0:1],
                                   g2b[:, fc * 128:(fc + 1) * 128],
                                   start=(fc == 0), stop=(fc == 15))
                S.inc(ins, "TC")
            TT(rmsm)

            def rmss(sc, c=c):
                S.wait(sc, "TC")
                sc.activation(rmsrow[0:1, 0:128], psB[0:1, 0:128], ACT.Ln,
                              bias=cvals[0:1, 0:1], scale=1.0 / E)
                S.inc(sc.activation(rmsrow[0:1, 128:256], rmsrow[0:1, 0:128],
                                    ACT.Exp, scale=-0.5), "SC")
            SS(rmss)

            def rim(t, c=c):
                S.wait(t, "SC")
                S.inc(t.matmul(psB[:, 128:256], ones1f[0:1, :],
                               rmsrow[0:1, 128:256], start=True, stop=True), "TC")
            TT(rim)

            def rie(v, c=c):
                S.wait(v, "TC")
                S.inc(v.tensor_copy(rib[:, :], psB[:, 128:256]), "VC")
            VV(rie)

            def ynbv(v, c=c):
                S.inc(v.tensor_tensor(
                    ynb[:, :].rearrange("p (fc q) -> p fc q", q=Q),
                    gbuf[:, :].rearrange("p (fc q) -> p fc q", q=Q),
                    rib[:, :].unsqueeze(1).broadcast_to([128, 16, Q]),
                    ALU.mult), "VC")
            VV(ynbv)
            vc_ynb = S.n["VC"]

            def wom(t, c=c, vc_ynb=vc_ynb):
                S.wait(t, "VC", vc_ynb)
                ins = None
                for ek in range(16):
                    for hh in range(2):
                        ins = t.matmul(
                            psB[:, hh * 512:(hh + 1) * 512],
                            ynb[:, ek * 128:(ek + 1) * 128],
                            wout2[:, ek * D_ + hh * 512: ek * D_ + (hh + 1) * 512],
                            start=(ek == 0), stop=(ek == 15))
                S.inc(ins, "TC")
            TT(wom)

            def resv(v, c=c):
                S.wait(v, "TC")
                S.wait(v, "SYD")
                S.inc(v.scalar_tensor_tensor(vbuf[:, :], xres[:, :], 2.0,
                                             psB[:, 0:1024], ALU.mult, ALU.add),
                      "VC")
            VV(resv)
            vc_resv[c] = S.n["VC"]

            ln_chunk(vbuf[:, :], x1t[:, :])

            def x1s(y, c=c):
                S.wait(y, "VC")
                S.inc(y.dma_start(x1_sp[c * 128:(c + 1) * 128, :], x1t[:, :]),
                      "SYD", 16)
            SY(x1s)

        fence()
        if "Hend" in dbg:
            def dh2(y):
                S.inc(y.dma_start(dbg["Hend"], HB[0][:, :]), "ST", 16)
            SY(dh2)

    pop_to(m_bc)
    pop_to(m_ab)
    MACC = push("maccb", [128, NTC * D_], F32)
    x2bf = push("x2bf", [128, 8 * T], BF16, is_transpose_dest=True)
    m_d = mark()
    x2lo = push("x2lo", [128, 8 * T], BF16, is_transpose_dest=True)
    hib = push("hib", [128, D_], BF16)
    lob = push("lob", [128, D_], BF16)
    scoresF = push("scoresF", [32, T], F32)
    if True:
        def x1l(y):
            for tc in range(NTC):
                S.inc(y.dma_start(MACC[:, tc * D_:(tc + 1) * D_],
                                  x1_sp[tc * 128:(tc + 1) * 128, :]), "SYD", 16)
        SY(x1l)
        if "x1" in dbg:
            def dx1(y):
                S.wait(y, "SYD")
                S.inc(y.dma_start(dbg["x1"], MACC[:, :]), "ST", 16)
            SY(dx1)

        syd_tc = {}
        for tc in range(NTC):
            def hilo(v, tc=tc):
                S.wait(v, "SYD")
                v.tensor_copy(hib[:, :], MACC[:, tc * D_:(tc + 1) * D_])
                v.drain()
                S.inc(v.tensor_tensor(lob[:, :], MACC[:, tc * D_:(tc + 1) * D_],
                                      hib[:, :], ALU.subtract), "VC")
            VV(hilo)

            def htr(y, tc=tc):
                S.wait(y, "VC")
                for kd in range(4):
                    S.inc(y.dma_start(
                        x2bf[:, kd * T + tc * 128: kd * T + (tc + 1) * 128],
                        hib[:, kd * 128:(kd + 1) * 128], transpose=True),
                        "SYD", 16)
                    S.inc(y.dma_start(
                        x2lo[:, kd * T + tc * 128: kd * T + (tc + 1) * 128],
                        lob[:, kd * 128:(kd + 1) * 128], transpose=True),
                        "SYD", 16)
            SY(htr)

            def htr2(sc, tc=tc):
                S.wait(sc, "VC")
                for kd in range(4, 8):
                    S.inc(sc.dma_start(
                        x2bf[:, kd * T + tc * 128: kd * T + (tc + 1) * 128],
                        hib[:, kd * 128:(kd + 1) * 128], transpose=True),
                        "SYD", 16)
                    S.inc(sc.dma_start(
                        x2lo[:, kd * T + tc * 128: kd * T + (tc + 1) * 128],
                        lob[:, kd * 128:(kd + 1) * 128], transpose=True),
                        "SYD", 16)
            SS(htr2)
            syd_tc[tc] = S.n["SYD"]
        syd_hilo = S.n["SYD"]

        for tg in range(4):
            def rlm(t, tg=tg):
                S.wait(t, "SYD", syd_tc[4 * tg + 3])
                if tg > 0:
                    S.wait(t, "SC")
                mms = []
                for kd in range(8):
                    for a_, b_ in [(gwh, x2bf), (gwl, x2bf), (gwh, x2lo)]:
                        mms.append((a_, b_, kd))
                ins = None
                for i, (a_, b_, kd) in enumerate(mms):
                    ins = t.matmul(psB[0:32, 0:512],
                                   a_[:, kd * H:(kd + 1) * H],
                                   b_[:, kd * T + tg * 512: kd * T + (tg + 1) * 512],
                                   start=(i == 0), stop=(i == len(mms) - 1))
                S.inc(ins, "TC")
            TT(rlm)

            def sig(sc, tg=tg):
                S.wait(sc, "TC")
                S.inc(sc.activation(scoresF[:, tg * 512:(tg + 1) * 512],
                                    psB[0:32, 0:512], ACT.Sigmoid), "SC")
            SS(sig)

        for tc in range(NTC):
            def str_(t, tc=tc):
                S.wait(t, "SC")
                S.wait(t, "VC")
                S.inc(t.transpose(psA[:, 0:32],
                                  scoresF[:, tc * 128:(tc + 1) * 128],
                                  c_idf[0:32, 0:32]), "TC")
            TT(str_)

            def rmath(v, tc=tc):
                S.wait(v, "TC")
                sco = scorestk[:, tc * H:(tc + 1) * H]
                v.tensor_copy(sco, psA[:, 0:32])
                sc_ = rscr[:, 0:32]
                t2s = rscr[:, 32:40]
                t2b = rscr2[:, 32:40]
                msk = rscr2[:, 0:32]
                v.drain()
                v.tensor_tensor(sc_, sco, c_eb[:, :], ALU.add)
                v.drain()
                pairs = [(0, 1), (0, 2), (0, 3), (1, 2), (1, 3), (2, 3)]
                for k_, (i_, j_) in enumerate(pairs):
                    dstp = t2s if k_ == 0 else t2b
                    v.tensor_tensor(dstp, sc_[:, i_:32:4], sc_[:, j_:32:4],
                                    ALU.add)
                    v.drain()
                    if k_ > 0:
                        v.tensor_tensor(t2s, t2s, t2b, ALU.max)
                        v.drain()
                v.tensor_copy(t2b, t2s)
                v.drain()
                for _ in range(3):
                    v.reduce_max(stat[:, 0:1], t2b, axis=AXX)
                    v.drain()
                    v.tensor_scalar(sqscr[:, 0:8], t2b, stat[:, 0:1], -1e30,
                                    ALU.is_ge, ALU.mult)
                    v.drain()
                    v.tensor_tensor(t2b, t2b, sqscr[:, 0:8], ALU.add)
                    v.drain()
                v.reduce_max(stat[:, 0:1], t2b, axis=AXX)
                v.drain()
                v.tensor_scalar(t2b, t2s, stat[:, 0:1], None, ALU.is_ge)
                v.drain()
                v.tensor_tensor(msk.rearrange("p (g j) -> p g j", j=4),
                                sc_.rearrange("p (g j) -> p g j", j=4),
                                t2b.unsqueeze(2)
                                   .broadcast_to([128, 8, 4]), ALU.mult)
                v.drain()
                v.tensor_copy(sqscr[:, 0:32], msk)
                v.drain()
                for _ in range(5):
                    v.reduce_max(stat[:, 0:1], sqscr[:, 0:32], axis=AXX)
                    v.drain()
                    v.tensor_scalar(sqscr[:, 32:64], sqscr[:, 0:32],
                                    stat[:, 0:1], -1e30, ALU.is_ge, ALU.mult)
                    v.drain()
                    v.tensor_tensor(sqscr[:, 0:32], sqscr[:, 0:32],
                                    sqscr[:, 32:64], ALU.add)
                    v.drain()
                v.reduce_max(stat[:, 0:1], sqscr[:, 0:32], axis=AXX)
                v.drain()
                v.tensor_scalar(msk, msk, stat[:, 0:1], None, ALU.is_ge)
                v.drain()
                v.tensor_tensor(msk, msk, sco, ALU.mult)
                v.drain()
                v.reduce_sum(stat[:, 1:2], msk, axis=AXX)
                v.drain()
                v.tensor_scalar(stat[:, 1:2], stat[:, 1:2], 1e-20, None, ALU.add)
                v.drain()
                v.reciprocal(stat[:, 2:3], stat[:, 1:2])
                v.drain()
                S.inc(v.tensor_scalar(combB[:, tc * H:(tc + 1) * H], msk,
                                      stat[:, 2:3], 2.5, ALU.mult, ALU.mult),
                      "VC")
            VV(rmath)

            def ctr(t, tc=tc):
                S.wait(t, "VC")
                S.inc(t.transpose(psA[0:32, 128:256], combB[:, tc * H:(tc + 1) * H],
                                  c_idf[:, :]), "TC")
            TT(ctr)

            def cte(v, tc=tc):
                S.wait(v, "TC")
                S.inc(v.tensor_copy(combTb[0:32, tc * 128:(tc + 1) * 128],
                                    psA[0:32, 128:256]), "VC")
            VV(cte)
        fence()
        if "scores" in dbg:
            def ds(y):
                S.inc(y.dma_start(dbg["scores"], scorestk[:, :]), "ST", 16)
            SY(ds)
        if "comb" in dbg:
            def dc(y):
                S.inc(y.dma_start(dbg["comb"], combB[:, :]), "ST", 16)
            SY(dc)

    pop_to(m_d)
    m_e = mark()
    wtu = push("wtu", [128, 4 * 8 * 512], BF16)
    wtd = push("wtd", [128, 4 * 4 * D_], BF16)
    heb = push("heb", [128, 4096], BF16)
    cbc4 = push("cbc4", [128, 1024], BF16)
    crow = push("crow", [1, 4 * T], BF16)
    if True:

        groups = [[0, 1, 2, 3], [4, 5, 6, 7], [8, 9, 10, 11], [12, 13, 14, 15],
                  [16, 17, 18, 19], [20, 21, 22, 23], [24, 25, 26, 27],
                  [28, 29, 30, 31], [32, 33, 34, 35]]

        def pf_expert(g, e, slot):
            S.inc(g.dma_start(
                wtu[:, slot * 4096:(slot + 1) * 4096]
                    .rearrange("p (k i) -> p k i", k=8),
                d["upT"][e].rearrange("(k p) i -> p k i", p=128)), "GPD", 16)
            S.inc(g.dma_start(
                wtd[:, slot * 4 * D_:(slot + 1) * 4 * D_]
                    .rearrange("p (k dd) -> p k dd", k=4),
                d["dnT"][e].rearrange("(k p) dd -> p k dd", p=128)), "GPD", 16)

        def pf0(g):
            for slot, e in enumerate(groups[0]):
                pf_expert(g, e, slot)
        GG(pf0)

        gpd_exp = {}
        for slot, e in enumerate(groups[0]):
            gpd_exp[(0, slot)] = None
        gpd_pf0 = S.n["GPD"]
        vc_ep = {}
        tc_dn = {}
        sc_sq = {}
        vc_mev = {}
        tc_cbm = {}

        for gi, grp in enumerate(groups):
            def crl(y, grp=grp, gi=gi):
                S.wait(y, "TC")
                S.wait(y, "VC")
                for si, e in enumerate(grp):
                    S.inc(y.dma_start(crow[0:1, si * T:(si + 1) * T],
                                      combTb[e:e + 1, :]), "SYD", 16)
            SY(crl)
            syd_crow = S.n["SYD"]

            for tt in range(8):
                def cbm4(t, gi=gi, tt=tt, grp=grp, syd_crow=syd_crow):
                    S.wait(t, "SYD", syd_crow)
                    if (gi, tt - 1, 3) in vc_ep:
                        S.wait(t, "VC", vc_ep[(gi, tt - 1, 3)])
                    elif gi > 0 and (gi - 1, 7, 3) in vc_ep:
                        S.wait(t, "VC", vc_ep[(gi - 1, 7, 3)])
                    ins = None
                    for si in range(len(grp)):
                        ins = t.matmul(psB[:, si * 256:(si + 1) * 256],
                                       ones1b[0:1, :],
                                       crow[0:1, si * T + tt * 256:
                                            si * T + (tt + 1) * 256],
                                       start=True, stop=True)
                    S.inc(ins, "TC")
                TT(cbm4)
                tc_cbm[(gi, tt)] = S.n["TC"]

                def cbe4(v, gi=gi, tt=tt):
                    S.wait(v, "TC", tc_cbm[(gi, tt)])
                    S.inc(v.tensor_copy(cbc4[:, :], psB[:, 0:1024]), "VC")
                VV(cbe4)
                vc_cbe = S.n["VC"]

                vc_r = {}
                for si, e in enumerate(grp):
                    half = si % 2
                    def upm(t, e=e, tt=tt, si=si, gi=gi, half=half,
                            vc_cbe=vc_cbe):
                        g_ = gpd_exp.get((gi, si))
                        S.wait(t, "GPD", gpd_pf0 if g_ is None else g_)
                        if si < 2:
                            S.wait(t, "VC", vc_cbe)
                        for key in [(gi, tt, si - 2), (gi, tt - 1, si),
                                    (gi - 1, 7, si)]:
                            if key in vc_r:
                                S.wait(t, "VC", vc_r[key])
                                break
                        ins = None
                        for ic in range(4):
                            for kd in range(8):
                                ins = t.matmul(
                                    psB[:, half * 1024 + ic * 256:
                                        half * 1024 + (ic + 1) * 256],
                                    wtu[:, si * 4096 + kd * 512 + ic * 128:
                                        si * 4096 + kd * 512 + (ic + 1) * 128],
                                    x2bf[:, kd * T + tt * 256:
                                         kd * T + (tt + 1) * 256],
                                    start=(kd == 0), stop=(kd == 7))
                        S.inc(ins, "TC")
                    TT(upm)
                    tc_up = S.n["TC"]

                    def ep1v(v, si=si, half=half, tc_up=tc_up, gi=gi, tt=tt):
                        S.wait(v, "TC", tc_up)
                        for key in [(gi, tt - 1, si), (gi - 1, 7, si)]:
                            if key in tc_dn:
                                S.wait(v, "TC", tc_dn[key])
                                break
                        S.inc(v.tensor_scalar(
                            heb[:, si * 1024:(si + 1) * 1024],
                            psB[:, half * 1024:(half + 1) * 1024],
                            0.0, None, ALU.max), "VC")
                    VV(ep1v)
                    vc_r[(gi, tt, si)] = S.n["VC"]

                def sqbody(sc, gi=gi, tt=tt, grp=grp):
                    for si in range(len(grp)):
                        S.wait(sc, "VC", vc_r[(gi, tt, si)])
                        S.inc(sc.activation(heb[:, si * 1024:(si + 1) * 1024],
                                            heb[:, si * 1024:(si + 1) * 1024],
                                            ACT.Square), "SC")
                        sc_sq[(gi, tt, si)] = S.n["SC"]
                SS(sqbody)

                for si, e in enumerate(grp):
                    def ep3v(v, si=si, gi=gi, tt=tt):
                        S.wait(v, "SC", sc_sq[(gi, tt, si)])
                        S.inc(v.tensor_tensor(
                            heb[:, si * 1024:(si + 1) * 1024]
                                .rearrange("p (ic q) -> p ic q", q=256),
                            heb[:, si * 1024:(si + 1) * 1024]
                                .rearrange("p (ic q) -> p ic q", q=256),
                            cbc4[:, si * 256:(si + 1) * 256].unsqueeze(1)
                                .broadcast_to([128, 4, 256]), ALU.mult), "VC")
                    VV(ep3v)
                    vc_ep[(gi, tt, si)] = S.n["VC"]

                    first = (si == 0)
                    last = (si == len(grp) - 1)
                    def dnm(t, e=e, tt=tt, si=si, gi=gi, first=first, last=last):
                        S.wait(t, "VC", vc_ep[(gi, tt, si)])
                        if first:
                            for key in [(gi, tt - 1), (gi - 1, 7)]:
                                if key in vc_mev:
                                    S.wait(t, "VC", vc_mev[key])
                                    break
                        ins = None
                        for tc2 in range(2):
                            for hh in range(2):
                                for ik in range(4):
                                    ins = t.matmul(
                                        psA[:, tc2 * 1024 + hh * 512:
                                            tc2 * 1024 + (hh + 1) * 512],
                                        heb[:, si * 1024 + ik * 256 + tc2 * 128:
                                            si * 1024 + ik * 256 + tc2 * 128 + 128],
                                        wtd[:, si * 4 * D_ + ik * D_ + hh * 512:
                                            si * 4 * D_ + ik * D_ + (hh + 1) * 512],
                                        start=(first and ik == 0),
                                        stop=(last and ik == 3))
                        S.inc(ins, "TC")
                    TT(dnm)
                    tc_dn[(gi, tt, si)] = S.n["TC"]

                    if tt == 7 and gi + 1 < len(groups) and si < len(groups[gi + 1]):
                        def pfn(g, gi=gi, si=si):
                            S.wait(g, "TC", tc_dn[(gi, 7, si)])
                            pf_expert(g, groups[gi + 1][si], si)
                        GG(pfn)
                        gpd_exp[(gi + 1, si)] = S.n["GPD"]

                def mev(v, tt=tt, gi=gi):
                    S.wait(v, "TC", tc_dn[(gi, tt, len(grp) - 1)])
                    ins = None
                    for tc2 in range(2):
                        tcg = tt * 2 + tc2
                        ins = v.tensor_tensor(
                            MACC[:, tcg * D_:(tcg + 1) * D_],
                            MACC[:, tcg * D_:(tcg + 1) * D_],
                            psA[:, tc2 * 1024:(tc2 + 1) * 1024], ALU.add)
                    S.inc(ins, "VC")
                VV(mev)
                vc_mev[(gi, tt)] = S.n["VC"]
        fence()
        if "macc" in dbg:
            def dm(y):
                S.inc(y.dma_start(dbg["macc"], MACC[:, :]), "ST", 16)
            SY(dm)

        for tc in range(NTC):
            ln_chunk(MACC[:, tc * D_:(tc + 1) * D_], MACC[:, tc * D_:(tc + 1) * D_])

            def out_dma(y, tc=tc):
                S.wait(y, "VC")
                S.inc(y.dma_start(yout[tc * 128:(tc + 1) * 128, :],
                                  MACC[:, tc * D_:(tc + 1) * D_]), "ST", 16)
            SY(out_dma)
            st_now = S.n["ST"]

            def stw(sc, st_now=st_now):
                S.wait(sc, "ST", st_now)
            SS(stw)

            def vtw(v, st_now=st_now):
                S.wait(v, "ST", st_now)
            VV(vtw)

        def final_wait(y):
            S.wait(y, "ST")
        SY(final_wait)

    blk.__exit__(None, None, None)
    pop_to(0)
    for g in reversed(guards):
        g.__exit__(None, None, None)
    return nc, dbg


def _prep_core_inputs(inputs, core):
    b, half = core >> 1, core & 1
    f32 = lambda a: np.ascontiguousarray(np.asarray(a, dtype=np.float32))
    x = f32(inputs["L0_attn_x"])
    W_in = f32(inputs["L0_attn_W_in"])
    conv_w = f32(inputs["L0_attn_conv_w"])
    conv_b = f32(inputs["L0_attn_conv_b"])
    lnw = f32(inputs["L0_attn_ln_w"]); lnb = f32(inputs["L0_attn_ln_b"])
    dtb = f32(inputs["L0_attn_dt_bias"]); alog = f32(inputs["L0_attn_A_log"])
    Dv = f32(inputs["L0_attn_D"]); nw = f32(inputs["L0_attn_norm_w"])
    W_out = f32(inputs["L0_attn_W_out"])
    gw = f32(inputs["L0_ffn_gate_w"]); eb = f32(inputs["L0_ffn_e_bias"])
    up = f32(inputs["L0_ffn_up_w"]); dn = f32(inputs["L0_ffn_down_w"])
    sup = f32(inputs["L0_ffn_shared_up_w"]); sdn = f32(inputs["L0_ffn_shared_down_w"])

    def convlay(w):
        return np.ascontiguousarray(w.reshape(NEC, 128, 4).transpose(1, 0, 2)
                                    .reshape(128, NEC * 4))

    def convblay(w):
        return np.ascontiguousarray(w.reshape(NEC, 128).T)

    upT = np.ascontiguousarray(np.concatenate(
        [up.transpose(0, 2, 1),
         np.ascontiguousarray(sup.T).reshape(D_, 4, 512).transpose(1, 0, 2)],
        axis=0))
    dnT = np.ascontiguousarray(np.concatenate(
        [dn.transpose(0, 2, 1),
         np.ascontiguousarray(sdn.T).reshape(4, 512, D_)], axis=0))

    if half == 1:
        xaux = np.ascontiguousarray(x[b, 0:T])
        winTa = np.ascontiguousarray(W_in[E:].T)
        lnwa, lnba = lnw, lnb
        cwa, cba = convlay(conv_w), convblay(conv_b)
        dtba, aloga = dtb, alog
    else:
        xaux = np.zeros((T, D_), np.float32)
        winTa = np.zeros((D_, CD + H), np.float32)
        lnwa = np.zeros(D_, np.float32); lnba = np.zeros(D_, np.float32)
        cwa = np.zeros((128, NEC * 4), np.float32)
        cba = np.zeros((128, NEC), np.float32)
        dtba = np.zeros(H, np.float32); aloga = np.zeros(H, np.float32)

    return {
        "xin": np.ascontiguousarray(x[b, half * T:(half + 1) * T]),
        "xaux": xaux,
        "winT": np.ascontiguousarray(W_in.T),
        "winTa": winTa,
        "lnrows": np.ascontiguousarray(
            np.concatenate([lnw, lnb, lnwa, lnba]).reshape(1, 4 * D_)),
        "cw": convlay(conv_w), "cb": convblay(conv_b),
        "cwa": cwa, "cba": cba,
        "dtb": dtb.reshape(H, 1), "alog": alog.reshape(H, 1),
        "dtba": dtba.reshape(H, 1), "aloga": aloga.reshape(H, 1),
        "dexp": np.ascontiguousarray(np.repeat(Dv, P).reshape(16, 128).T),
        "nwp": np.ascontiguousarray(nw.reshape(16, 128).T),
        "woutT": np.ascontiguousarray(W_out.T),
        "gwT": np.ascontiguousarray(gw.T),
        "ebias": eb.reshape(1, H),
        "upT": upT, "dnT": dnT,
    }


_CACHE = {}


def kernel(**inputs):
    if "prog" not in _CACHE:
        _CACHE["prog"] = build_program()
    nc, _ = _CACHE["prog"]
    in_maps = [_prep_core_inputs(inputs, c) for c in range(8)]
    res = run_bass_kernel_spmd(nc, in_maps, core_ids=list(range(8)))
    out = np.zeros((B_, L_, D_), np.float32)
    for c in range(8):
        b, half = c >> 1, c & 1
        out[b, half * T:(half + 1) * T] = np.asarray(res.results[c]["yout"])
    return out


if __name__ == "__main__":
    nc, _ = build_program()
    print("build ok, instructions:", nc.n_instructions())


# version 10
# speedup vs baseline: 1.2452x; 1.0299x over previous
import numpy as np
import ml_dtypes
import concourse.bass as bass
import concourse.mybir as mybir
from concourse.bass_utils import run_bass_kernel_spmd

F32 = mybir.dt.float32
BF16 = mybir.dt.bfloat16
ALU = mybir.AluOpType
ACT = mybir.ActivationFunctionType
AXX = mybir.AxisListType.X

B_, L_, D_ = 4, 4096, 1024
T = 2048
NTC = 16
Q = 128
H, P, NS = 32, 64, 64
E = 2048
CD = 2176
DIP = 4256
NEX = 36
EPS = 1e-5
EPSI = 1e-6
TILE = 256
NTILE = T // TILE
NEC = 17


class Sems:
    def __init__(self, nc, names):
        self.h, self.n, self._g = {}, {}, []
        for nm in names:
            g = nc.semaphore(nm)
            self.h[nm] = g.__enter__()
            self._g.append(g)
            self.n[nm] = 0

    def inc(self, instr, nm, k=1):
        instr.then_inc(self.h[nm], k)
        self.n[nm] += k
        return instr

    def wait(self, eng, nm, val=None):
        eng.wait_ge(self.h[nm], self.n[nm] if val is None else val)


def build_program(debug=()):
    nc = bass.Bass("TRN2", target_bir_lowering=False)
    d = {}

    def din(name, shape):
        d[name] = nc.dram_tensor(name, list(shape), F32, kind="ExternalInput").ap()

    din("xin", (T, D_)); din("xaux", (T, D_))
    din("winT", (D_, DIP)); din("winTa", (D_, CD + H))
    din("lnrows", (1, 4 * D_))
    din("cw", (128, NEC * 4)); din("cb", (128, NEC))
    din("cwa", (128, NEC * 4)); din("cba", (128, NEC))
    din("dtb", (H, 1)); din("alog", (H, 1)); din("dtba", (H, 1)); din("aloga", (H, 1))
    din("dexp", (128, 16)); din("nwp", (128, 16))
    din("woutT", (E, D_)); din("gwT", (D_, H)); din("ebias", (1, H))
    din("upT", (NEX, D_, 512)); din("dnT", (NEX, 512, D_))
    yout = nc.dram_tensor("yout", [T, D_], F32, kind="ExternalOutput").ap()

    x_sp = nc.dram_tensor("x_sp", [16, 128, T], BF16, kind="Internal").ap()
    xa_sp = nc.dram_tensor("xa_sp", [16, 128, T], BF16, kind="Internal").ap()
    sz_sp = nc.dram_tensor("sz_sp", [16, 128, T], BF16, kind="Internal").ap()
    x1_sp = nc.dram_tensor("x1_sp", [T, D_], F32, kind="Internal").ap()

    dbg = {}
    dbg_shapes = {"u": (128, 8 * T), "dtv": (H, T), "acinc": (H, T),
                  "Hinit": (NS, E), "Hend": (NS, E), "y0": (128, E),
                  "x1": (128, NTC * D_), "scores": (128, NTC * H),
                  "comb": (128, NTC * H), "macc": (128, NTC * D_)}
    for nm in debug:
        dbg[nm] = nc.dram_tensor("dbg_" + nm, list(dbg_shapes[nm]), F32,
                                 kind="ExternalOutput").ap()

    S = Sems(nc, ["TC", "VC", "SC", "GC", "SYD", "GPD", "ST"])
    guards = []

    def sbuf(name, shape, dt, **kw):
        g = nc.sbuf_tensor(name, shape, dt, **kw)
        t = g.__enter__()
        guards.append(g)
        return t

    c_strict = sbuf("c_strict", [128, 128], BF16)
    c_umask = sbuf("c_umask", [128, 128], BF16)
    c_idf = sbuf("c_idf", [128, 128], F32)
    identb = sbuf("identb", [128, 128], BF16)
    c_ind = sbuf("c_ind", [32, 2048], BF16)
    ones1b = sbuf("ones1b", [1, 128], BF16)
    ones1f = sbuf("ones1f", [1, 128], F32)
    onescol = sbuf("onescol", [128, 1], BF16)
    c_eb = sbuf("c_eb", [128, H], F32)
    cw_s = sbuf("cw_s", [128, NEC * 4], F32)
    cb_s = sbuf("cb_s", [128, NEC], F32)
    cwa_s = sbuf("cwa_s", [128, NEC * 4], F32)
    cba_s = sbuf("cba_s", [128, NEC], F32)
    dexp_s = sbuf("dexp_s", [128, 16], F32)
    nwp_s = sbuf("nwp_s", [128, 16], F32)
    ebrow = sbuf("ebrow", [1, H], F32)
    gwh = sbuf("gwh", [128, 8 * H], BF16)
    gwl = sbuf("gwl", [128, 8 * H], BF16)
    scorestk = sbuf("scorestk", [128, NTC * H], F32)
    combB = sbuf("combB", [128, NTC * H], F32)
    combTb = sbuf("combTb", [NEX, T], BF16)
    stat = sbuf("stat", [128, 8], F32)
    sqscr = sbuf("sqscr", [128, D_], F32)
    rmsrow = sbuf("rmsrow", [1, 256], F32)
    rscr = sbuf("rscr", [128, 40], F32)
    rscr2 = sbuf("rscr2", [128, 40], F32)
    cvals = sbuf("cvals", [128, 2], F32)

    psA = nc.psum_tensor("psA", [128, 2048], F32).__enter__()
    psB = nc.psum_tensor("psB", [128, 2048], F32).__enter__()


    arena = []

    def push(name, shape, dt, **kw):
        g = nc.sbuf_tensor(name, shape, dt, **kw)
        t = g.__enter__()
        arena.append(g)
        return t

    def mark():
        return len(arena)

    def pop_to(m):
        while len(arena) > m:
            arena.pop().__exit__(None, None, None)

    m_ab = mark()
    pk1 = push("pk1", [64, T], F32)
    pk2 = push("pk2", [64, T], F32)
    pk3 = push("pk3", [64, T], F32)
    pk4 = push("pk4", [64, T], F32)
    pk5 = push("pk5", [64, T], F32)
    dtraw, dtrawA = pk1[0:32, :], pk1[32:64, :]
    dtv, dtvA = pk2[0:32, :], pk2[32:64, :]
    acinc, acincA = pk3[0:32, :], pk3[32:64, :]
    w2m, w2a = pk4[0:32, :], pk4[32:64, :]
    scr32, scr32A = pk5[0:32, :], pk5[32:64, :]
    apack = push("apack", [64, 1], F32)
    iotaE = push("iotaE", [128, H], F32)
    gwf = push("gwf", [128, 8 * H], F32)
    dtpack = push("dtpack", [64, 1], F32)
    acol, acola = apack[0:32, :], apack[32:64, :]
    dtb_s, dtba_s = dtpack[0:32, :], dtpack[32:64, :]
    erow = push("erow", [H, T], BF16)
    lam64 = push("lam64", [64, NTC * H], F32)
    lam64A = push("lam64A", [64, NTC * H], F32)
    dtvT = push("dtvT", [128, NTC * H], F32)
    dtvTA = push("dtvTA", [128, NTC * H], F32)
    w2T = push("w2T", [128, NTC * H], F32)
    w2TA = push("w2TA", [128, NTC * H], F32)
    bcm = push("bcm", [128, T], BF16)
    bcma = push("bcma", [128, T], BF16)
    h0 = push("h0", [NS, E], F32)
    h1 = push("h1", [NS, E], F32)
    HB = [h0, h1]

    blk = nc.Block().__enter__()
    TT, VV, SS, GG, SY = blk.tensor, blk.vector, blk.scalar, blk.gpsimd, blk.sync

    def consts_g(g):
        for nm, dst in [("cw", cw_s), ("cb", cb_s), ("cwa", cwa_s), ("cba", cba_s),
                        ("dexp", dexp_s), ("nwp", nwp_s), ("dtb", dtb_s),
                        ("dtba", dtba_s), ("ebias", ebrow), ("alog", acol),
                        ("aloga", acola)]:
            S.inc(g.dma_start(dst[:, :], d[nm]), "GPD", 16)
        S.inc(g.dma_start(gwf[:, :].rearrange("p (k h) -> p k h", k=8),
                          d["gwT"].rearrange("(k p) h -> p k h", p=128)), "GPD", 16)
        g.memset(ones1b[:, :], 1.0)
        g.memset(ones1f[:, :], 1.0)
        g.memset(onescol[:, :], 1.0)
        g.memset(c_strict[:, :], 1.0)
        g.memset(c_umask[:, :], 1.0)
        g.memset(c_idf[:, :], 1.0)
        g.memset(identb[:, :], 1.0)
        g.memset(combTb[32:36, :], 1.0)
        g.drain()
        g.affine_select(c_strict[:, :], c_strict[:, :], [[-1, 128]], ALU.is_ge, 0.0,
                        base=-1, channel_multiplier=1)
        g.affine_select(c_umask[:, :], c_umask[:, :], [[1, 128]], ALU.is_ge, 0.0,
                        channel_multiplier=-1)
        g.affine_select(c_idf[:, :], c_idf[:, :], [[-1, 128]], ALU.is_equal, 0.0,
                        channel_multiplier=1)
        g.drain()
        g.affine_select(identb[:, :], identb[:, :], [[-1, 128]], ALU.is_equal, 0.0,
                        channel_multiplier=1)
        g.memset(cvals[:, 0:1], EPS)
        g.memset(cvals[:, 1:2], 1.0)
        g.memset(c_ind[:, :], 1.0)
        g.drain()
        g.affine_select(c_ind[:, :], c_ind[:, :], [[1, 2048]], ALU.is_ge, 0.0,
                        channel_multiplier=-64)
        g.drain()
        g.affine_select(c_ind[:, :], c_ind[:, :], [[-1, 2048]], ALU.is_ge, 0.0,
                        base=63, channel_multiplier=64)
        g.iota(iotaE[:, :], [[1, H]], base=0, channel_multiplier=0,
               allow_small_or_imprecise_dtypes=True)
        S.inc(g.memset(h0[:, :], 0.0), "GC")
    GG(consts_g)

    def consts_v(v):
        S.wait(v, "GC")
        S.wait(v, "GPD")
        v.tensor_copy(gwh[:, :], gwf[:, :])
        v.drain()
        S.inc(v.tensor_tensor(gwl[:, :], gwf[:, :], gwh[:, :], ALU.subtract), "VC")
    VV(consts_v)

    def consts_s(sc):
        S.wait(sc, "GPD")
        sc.activation(acol[:, :], acol[:, :], ACT.Exp)
        S.inc(sc.activation(acola[:, :], acola[:, :], ACT.Exp), "SC")
    SS(consts_s)

    def consts_v2(v):
        S.wait(v, "SC")
        v.tensor_scalar_mul(acol[:, :], acol[:, :], -1.0)
        S.inc(v.tensor_scalar_mul(acola[:, :], acola[:, :], -1.0), "VC")
    VV(consts_v2)

    def eb_t(t):
        S.wait(t, "VC")
        S.inc(t.matmul(psA[:, 0:H], ones1f[0:1, :], ebrow[0:1, :], start=True,
                       stop=True), "TC")
    TT(eb_t)

    def eb_v(v):
        S.wait(v, "TC")
        S.inc(v.scalar_tensor_tensor(c_eb[:, :], iotaE[:, :], -EPSI, psA[:, 0:H],
                                     ALU.mult, ALU.add), "VC")
    VV(eb_v)

    def fence():
        for body in [TT, VV, SS, GG, SY]:
            def f(e):
                for nm in ["TC", "VC", "SC", "GC", "SYD", "GPD"]:
                    S.wait(e, nm)
            body(f)

    def ln_chunk(src_ap, dst_ap):
        def v1(v):
            S.wait(v, "SC")
            v.reduce_sum(stat[:, 1:2], src_ap, axis=AXX)
            v.drain()
            S.inc(v.tensor_scalar_mul(stat[:, 2:3], stat[:, 1:2], 1.0 / D_), "VC")
        VV(v1)

        def s1(sc):
            S.wait(sc, "VC")
            S.inc(sc.activation(sqscr[:, :], src_ap, ACT.Square,
                                accum_out=stat[:, 0:1]), "SC")
        SS(s1)

        def v2(v):
            S.wait(v, "SC")
            v.tensor_scalar_mul(stat[:, 3:4], stat[:, 0:1], 1.0 / D_)
            v.drain()
            v.tensor_tensor(stat[:, 4:5], stat[:, 2:3], stat[:, 2:3], ALU.mult)
            v.drain()
            S.inc(v.tensor_tensor(stat[:, 4:5], stat[:, 3:4], stat[:, 4:5],
                                  ALU.subtract), "VC")
        VV(v2)

        def s2(sc):
            S.wait(sc, "VC")
            sc.activation(stat[:, 5:6], stat[:, 4:5], ACT.Ln, bias=cvals[:, 0:1])
            S.inc(sc.activation(stat[:, 6:7], stat[:, 5:6], ACT.Exp, scale=-0.5),
                  "SC")
        SS(s2)

        def v3(v):
            S.wait(v, "SC")
            S.inc(v.tensor_scalar(dst_ap, src_ap, stat[:, 2:3], stat[:, 6:7],
                                  ALU.subtract, ALU.mult), "VC")
        VV(v3)

    m_a = mark()
    ufm = push("ufm", [128, 8 * T], BF16, is_transpose_dest=True)
    wins = push("wins", [128, 3 * 8 * 128], BF16)
    gbB = push("gbB", [128, 4 * D_], F32)
    gbrow = push("gbrow", [1, D_], F32)
    xtile = push("xtile", [128, D_], F32)
    utile = push("utile", [128, D_], F32)
    ubtile = push("ubtile", [128, D_], BF16)
    fbuf0 = push("fbuf0", [128, T + 3], BF16)
    fbuf1 = push("fbuf1", [128, T + 3], BF16)
    cacc0 = push("cacc0", [128, T], BF16)
    ubz0 = push("ubz0", [128, T], BF16)
    ubz1 = push("ubz1", [128, T], BF16)
    if True:

        for i in range(4):
            def gb_load(g, i=i):
                S.wait(g, "TC")
                S.inc(g.dma_start(gbrow[:, :], d["lnrows"][0:1, i * D_:(i + 1) * D_]),
                      "GPD", 16)
            GG(gb_load)
            gpd_r = S.n["GPD"]

            def bc_t(t, i=i, gpd_r=gpd_r):
                S.wait(t, "GPD", gpd_r)
                S.wait(t, "VC")
                t.matmul(psA[:, 0:512], ones1f[0:1, :],
                         gbrow[0:1, 0:512], start=True, stop=True)
                S.inc(t.matmul(psA[:, 512:1024], ones1f[0:1, :],
                               gbrow[0:1, 512:1024], start=True,
                               stop=True), "TC")
            TT(bc_t)

            def bc_v(v, i=i):
                S.wait(v, "TC")
                S.inc(v.tensor_copy(gbB[:, i * D_:(i + 1) * D_], psA[:, 0:1024]),
                      "VC")
            VV(bc_v)

        def phase_ln1(src_dram, goff):
            for tc in range(NTC):
                def ld(y, tc=tc):
                    S.wait(y, "VC")
                    S.wait(y, "SC")
                    S.inc(y.dma_start(xtile[:, :],
                                      src_dram[tc * 128:(tc + 1) * 128, :]),
                          "SYD", 16)
                SY(ld)

                ln_chunk(xtile[:, :], utile[:, :])

                def v3(v, goff=goff):
                    S.wait(v, "SYD")
                    v.tensor_tensor(utile[:, :], utile[:, :],
                                    gbB[:, goff * D_:(goff + 1) * D_], ALU.mult)
                    S.inc(v.tensor_tensor(ubtile[:, :], utile[:, :],
                                          gbB[:, (goff + 1) * D_:(goff + 2) * D_],
                                          ALU.add), "VC")
                VV(v3)

                def tr(y, tc=tc):
                    S.wait(y, "VC")
                    for kd in range(8):
                        S.inc(y.dma_start(
                            ufm[:, kd * T + tc * 128: kd * T + (tc + 1) * 128],
                            ubtile[:, kd * 128:(kd + 1) * 128], transpose=True),
                            "SYD", 16)
                SY(tr)

        def phase_ln1_fixed(src_dram, goff):
            for tc in range(NTC):
                def ld(y, tc=tc):
                    S.wait(y, "VC")
                    S.wait(y, "SC")
                    S.inc(y.dma_start(xtile[:, :],
                                      src_dram[tc * 128:(tc + 1) * 128, :]),
                          "SYD", 16)
                SY(ld)
                syd_x = S.n["SYD"]

                def s1(sc, syd_x=syd_x):
                    S.wait(sc, "SYD", syd_x)
                    S.inc(sc.activation(sqscr[:, :], xtile[:, :], ACT.Square,
                                        accum_out=stat[:, 0:1]), "SC")
                SS(s1)

                def v1(v, syd_x=syd_x):
                    S.wait(v, "SYD", syd_x)
                    v.reduce_sum(stat[:, 1:2], xtile[:, :], axis=AXX)
                    v.drain()
                    S.wait(v, "SC")
                    v.tensor_scalar_mul(stat[:, 2:3], stat[:, 1:2], 1.0 / D_)
                    v.tensor_scalar_mul(stat[:, 3:4], stat[:, 0:1], 1.0 / D_)
                    v.drain()
                    v.tensor_tensor(stat[:, 4:5], stat[:, 2:3], stat[:, 2:3], ALU.mult)
                    v.drain()
                    S.inc(v.tensor_tensor(stat[:, 4:5], stat[:, 3:4], stat[:, 4:5],
                                          ALU.subtract), "VC")
                VV(v1)

                def s2(sc):
                    S.wait(sc, "VC")
                    sc.activation(stat[:, 5:6], stat[:, 4:5], ACT.Ln, bias=cvals[:, 0:1])
                    S.inc(sc.activation(stat[:, 6:7], stat[:, 5:6], ACT.Exp,
                                        scale=-0.5), "SC")
                SS(s2)

                def v2(v, goff=goff):
                    S.wait(v, "SC")
                    S.wait(v, "SYD")
                    v.tensor_scalar(utile[:, :], xtile[:, :], stat[:, 2:3],
                                    stat[:, 6:7], ALU.subtract, ALU.mult)
                    v.drain()
                    v.tensor_tensor(utile[:, :], utile[:, :],
                                    gbB[:, goff * D_:(goff + 1) * D_], ALU.mult)
                    v.drain()
                    S.inc(v.tensor_tensor(ubtile[:, :], utile[:, :],
                                          gbB[:, (goff + 1) * D_:(goff + 2) * D_],
                                          ALU.add), "VC")
                VV(v2)

                def tr(y, tc=tc):
                    S.wait(y, "VC")
                    for kd in range(8):
                        S.inc(y.dma_start(
                            ufm[:, kd * T + tc * 128: kd * T + (tc + 1) * 128],
                            ubtile[:, kd * 128:(kd + 1) * 128], transpose=True),
                            "SYD", 16)
                SY(tr)

        def phase_a2(winT_dram, necols, spill, bc_dst, dtraw_dst, zch, cw_t, cb_t):
            dt_bp = dtraw_dst.base_partition()
            nec_all = necols // 128
            rem = necols - nec_all * 128
            n_ec = nec_all + (1 if rem else 0)
            src3 = winT_dram.rearrange("(k p) e -> p k e", p=128)

            def zt(v):
                v.memset(fbuf0[:, 0:3], 0.0)
                S.inc(v.memset(fbuf1[:, 0:3], 0.0), "VC")
            VV(zt)
            ufm_done = S.n["SYD"]

            wslab = [wins[:, s * 1024:(s + 1) * 1024]
                         .rearrange("p (k e) -> p k e", e=128) for s in range(3)]
            slab_gpd = {}

            def wl0(g):
                for s in range(min(3, n_ec)):
                    rows = 128 if s < nec_all else rem
                    S.inc(g.dma_start(wslab[s][:, :, 0:rows],
                                      src3[:, :, s * 128:s * 128 + rows]),
                          "GPD", 16)
                    slab_gpd[s] = S.n["GPD"]
            GG(wl0)

            last_evac = {psl: ("VC", S.n["VC"]) for psl in range(4)}
            fb_free = [None, None]
            cc_free = [None]
            ub_free = [None, None]

            for ec in range(n_ec):
                mm_rows = 128 if ec < nec_all else rem
                slot = ec % 3
                gpd_here = slab_gpd[ec]
                par = ec % 2
                fb = [fbuf0, fbuf1][par]
                cc = cacc0
                ub = [ubz0, ubz1][par]
                evac_cnts = []

                for ti in range(NTILE):
                    psl = ti % 4
                    if ec < nec_all:
                        sl = psA[0:mm_rows, psl * 512:psl * 512 + TILE]
                    else:
                        sl = psA[dt_bp:dt_bp + mm_rows, psl * 512:psl * 512 + TILE]

                    def mm(t, ec=ec, ti=ti, sl=sl, mm_rows=mm_rows, slot=slot,
                           gpd_here=gpd_here, ufm_done=ufm_done, psl=psl):
                        S.wait(t, "GPD", gpd_here)
                        S.wait(t, "SYD", ufm_done)
                        if psl in last_evac and last_evac[psl] is not None:
                            S.wait(t, last_evac[psl][0], last_evac[psl][1])
                        ins = None
                        for kd in range(8):
                            ins = t.matmul(
                                sl, wslab[slot][:, kd, 0:mm_rows],
                                ufm[:, kd * T + ti * TILE: kd * T + (ti + 1) * TILE],
                                start=(kd == 0), stop=(kd == 7))
                        S.inc(ins, "TC")
                    TT(mm)
                    tc_here = S.n["TC"]
                    if ti == NTILE - 1 and ec + 3 < n_ec:
                        def wln(g, ec=ec, slot=slot, tc_here=tc_here):
                            S.wait(g, "TC", tc_here)
                            rows = 128 if ec + 3 < nec_all else rem
                            S.inc(g.dma_start(
                                wslab[slot][:, :, 0:rows],
                                src3[:, :, (ec + 3) * 128:(ec + 3) * 128 + rows]),
                                "GPD", 16)
                        GG(wln)
                        slab_gpd[ec + 3] = S.n["GPD"]

                    if ec < nec_all:
                        def ev(v, ti=ti, sl=sl, tc_here=tc_here, fb=fb, par=par):
                            S.wait(v, "TC", tc_here)
                            if ti == 0 and fb_free[par] is not None:
                                S.wait(v, fb_free[par][0], fb_free[par][1])
                            S.inc(v.tensor_copy(
                                fb[:, 3 + ti * TILE: 3 + (ti + 1) * TILE], sl),
                                "VC")
                        VV(ev)
                    else:
                        def ev(v, ti=ti, sl=sl, tc_here=tc_here,
                               dtraw_dst=dtraw_dst):
                            S.wait(v, "TC", tc_here)
                            S.inc(v.tensor_copy(
                                dtraw_dst[:, ti * TILE:(ti + 1) * TILE], sl), "VC")
                        VV(ev)
                    last_evac[psl] = ("VC", S.n["VC"])

                if ec >= nec_all:
                    continue

                if ec < zch:
                    def zsil(sc, fb=fb, ub=ub, par=par):
                        S.wait(sc, "VC")
                        if ub_free[par] is not None:
                            S.wait(sc, ub_free[par][0], ub_free[par][1])
                        S.inc(sc.activation(ub[:, :], fb[:, 3:3 + T], ACT.Silu),
                              "SC")
                    SS(zsil)
                    fb_free[par] = ("SC", S.n["SC"])

                    def zsp(y, ec=ec, ub=ub):
                        S.wait(y, "SC")
                        S.inc(y.dma_start(sz_sp[ec, :, :], ub[:, :]), "SYD", 16)
                    SY(zsp)
                    ub_free[par] = ("SYD", S.n["SYD"])
                else:
                    xc = ec - zch
                    def cv(v, fb=fb, cc=cc, xc=xc, par=par, cw_t=cw_t):
                        if cc_free[0] is not None:
                            S.wait(v, cc_free[0][0], cc_free[0][1])
                        v.tensor_scalar(cc[:, :], fb[:, 0:T],
                                        cw_t[:, xc * 4:xc * 4 + 1], None, ALU.mult)
                        ins = None
                        for j in range(1, 4):
                            v.drain()
                            ins = v.scalar_tensor_tensor(
                                cc[:, :], fb[:, j:j + T],
                                cw_t[:, xc * 4 + j:xc * 4 + j + 1], cc[:, :],
                                ALU.mult, ALU.add)
                        S.inc(ins, "VC")
                    VV(cv)
                    fb_free[par] = ("VC", S.n["VC"])

                    def csil(sc, cc=cc, ub=ub, par=par, xc=xc, cb_t=cb_t):
                        S.wait(sc, "VC")
                        if ub_free[par] is not None:
                            S.wait(sc, ub_free[par][0], ub_free[par][1])
                        S.inc(sc.activation(ub[:, :], cc[:, :], ACT.Silu,
                                            bias=cb_t[:, xc:xc + 1]), "SC")
                    SS(csil)
                    cc_free[0] = ("SC", S.n["SC"])

                    if xc < 16:
                        def xsp(y, xc=xc, ub=ub, spill=spill):
                            S.wait(y, "SC")
                            S.inc(y.dma_start(spill[xc, :, :], ub[:, :]),
                                  "SYD", 16)
                        SY(xsp)
                        ub_free[par] = ("SYD", S.n["SYD"])
                    else:
                        def bcp(v, ub=ub, bc_dst=bc_dst):
                            S.wait(v, "SC")
                            S.inc(v.tensor_copy(bc_dst[:, :], ub[:, :]), "VC")
                        VV(bcp)
                        ub_free[par] = ("VC", S.n["VC"])

        def phase_a3(dtr, dtb_t, a_t, dv, ac, w2_, l64, dvT, w2T_, er, scr):
            bp3 = dtr.base_partition()

            def sp1(sc):
                S.wait(sc, "VC")
                sc.activation(scr[:, :], dtr[:, :], ACT.Abs, bias=dtb_t[:, 0:1])
                sc.activation(scr[:, :], scr[:, :], ACT.Exp, scale=-1.0)
                S.inc(sc.activation(scr[:, :], scr[:, :], ACT.Ln,
                                    bias=cvals[bp3:bp3 + 32, 1:2]), "SC")
            SS(sp1)

            def sp2(v):
                S.wait(v, "SC")
                v.tensor_scalar(dv[:, :], dtr[:, :], dtb_t[:, 0:1], 0.0,
                                ALU.add, ALU.max)
                v.drain()
                v.tensor_tensor(dv[:, :], dv[:, :], scr[:, :], ALU.add)
                v.drain()
                ins = None
                v.tensor_scalar(scr[:, :], dv[:, :], a_t[:, 0:1], None, ALU.mult)
                v.drain()
                for c in range(NTC):
                    ins = v.tensor_tensor_scan(ac[:, c * Q:(c + 1) * Q],
                                               scr[:, c * Q:(c + 1) * Q],
                                               scr[:, c * Q:(c + 1) * Q], 0.0,
                                               ALU.add, ALU.bypass)
                S.inc(ins, "VC")
            VV(sp2)

            if er is not None:
                def sp4(sc):
                    S.wait(sc, "VC")
                    S.inc(sc.activation(er[:, :], ac[:, :], ACT.Exp), "SC")
                SS(sp4)

            for c in range(NTC):
                def trp(t, c=c):
                    S.wait(t, "VC")
                    S.wait(t, "SC")
                    bp = ac.base_partition()
                    t.transpose(psB[0:1, 64:96], ac[:, (c + 1) * Q - 1:(c + 1) * Q],
                                c_idf[bp:bp + 32, bp:bp + 32])
                    bpd = dv.base_partition()
                    S.inc(t.transpose(psB[:, 32:64], dv[:, c * Q:(c + 1) * Q],
                                      c_idf[bpd:bpd + 32, bpd:bpd + 32]), "TC")
                TT(trp)

                def tev(v, c=c, dvT=dvT):
                    S.wait(v, "TC")
                    v.tensor_copy(dvT[:, c * H:(c + 1) * H], psB[:, 32:64])
                    S.inc(v.tensor_copy(rmsrow[0:1, 0:32], psB[0:1, 64:96]), "VC")
                VV(tev)

                def lse(sc, c=c):
                    S.wait(sc, "VC")
                    S.inc(sc.activation(rmsrow[0:1, 32:64], rmsrow[0:1, 0:32],
                                        ACT.Exp), "SC")
                SS(lse)

                def lmm(t, c=c):
                    S.wait(t, "SC")
                    S.inc(t.matmul(psB[0:64, 64:96], ones1f[0:1, 0:64],
                                   rmsrow[0:1, 32:64], start=True, stop=True), "TC")
                TT(lmm)

                def lev(v, c=c, l64=l64):
                    S.wait(v, "TC")
                    S.inc(v.tensor_copy(l64[:, c * H:(c + 1) * H], psB[0:64, 64:96]),
                          "VC")
                VV(lev)

                def w2v(v, c=c, ac=ac):
                    S.inc(v.tensor_scalar(scr[:, c * Q:(c + 1) * Q],
                                          ac[:, c * Q:(c + 1) * Q],
                                          ac[:, (c + 1) * Q - 1:(c + 1) * Q], None,
                                          ALU.subtract), "VC")
                VV(w2v)

                def w2e(sc, c=c):
                    S.wait(sc, "VC")
                    S.inc(sc.activation(scr[:, c * Q:(c + 1) * Q],
                                        scr[:, c * Q:(c + 1) * Q], ACT.Exp,
                                        scale=-1.0), "SC")
                SS(w2e)

                def w2f(v, c=c, dv=dv, w2_=w2_):
                    S.wait(v, "SC")
                    S.inc(v.tensor_tensor(w2_[:, c * Q:(c + 1) * Q],
                                          dv[:, c * Q:(c + 1) * Q],
                                          scr[:, c * Q:(c + 1) * Q], ALU.mult),
                          "VC")
                VV(w2f)

                def w2t(t, c=c, w2_=w2_):
                    S.wait(t, "VC")
                    bpw = w2_.base_partition()
                    S.inc(t.transpose(psB[:, 96:128], w2_[:, c * Q:(c + 1) * Q],
                                      c_idf[bpw:bpw + 32, bpw:bpw + 32]), "TC")
                TT(w2t)

                def w2tv(v, c=c, w2T_=w2T_):
                    S.wait(v, "TC")
                    S.inc(v.tensor_copy(w2T_[:, c * H:(c + 1) * H], psB[:, 96:128]),
                          "VC")
                VV(w2tv)

        phase_ln1_fixed(d["xaux"], 2)
        phase_a2(d["winTa"], CD + H, xa_sp, bcma, dtrawA, 0, cwa_s, cba_s)
        fence()
        phase_ln1_fixed(d["xin"], 0)
        phase_a2(d["winT"], DIP, x_sp, bcm, dtraw, 16, cw_s, cb_s)
        fence()
        phase_a3(dtrawA, dtba_s, acola, dtvA, acincA, w2a, lam64A, dtvTA, w2TA,
                 None, scr32A)
        phase_a3(dtraw, dtb_s, acol, dtv, acinc, w2m, lam64, dtvT, w2T, erow, scr32)
        fence()
        if "u" in dbg:
            def du(g):
                S.inc(g.dma_start(dbg["u"], ufm[:, :]), "ST", 16)
            GG(du)
        if "dtv" in dbg:
            def dd(y):
                S.inc(y.dma_start(dbg["dtv"], dtv[:, :]), "ST", 16)
            SY(dd)
        if "acinc" in dbg:
            def da(y):
                S.inc(y.dma_start(dbg["acinc"], acinc[:, :]), "ST", 16)
            SY(da)

    pop_to(m_a)
    m_bc = mark()
    wout2 = push("wout2", [128, 16 * D_], BF16)
    xt0 = push("xt0", [128, E], BF16, is_transpose_dest=True)
    xt1 = push("xt1", [128, E], BF16, is_transpose_dest=True)
    wq = push("wq", [128, 2 * E], BF16)
    e2sb = push("e2sb", [128, E], BF16)
    gpmb = push("gpmb", [128, 128], BF16)
    zal = push("zal", [128, E], BF16)
    btp = push("btp", [128, 64], BF16)
    ybuf = push("ybuf", [128, E], F32)
    gbuf = push("gbuf", [128, E], F32)
    g2b = push("g2b", [128, E], BF16)
    xck = push("xck", [128, E], BF16)
    szt = push("szt", [128, E], BF16)
    ynb = push("ynb", [128, E], BF16)
    rib = push("rib", [128, 128], F32)
    dab = push("dab", [128, 32], F32)
    vbuf = push("vbuf", [128, D_], F32)
    xres = push("xres", [128, D_], F32)
    x1t = push("x1t", [128, D_], F32)
    cmt = push("cmt", [64, Q], BF16)
    if True:

        XT = [xt0, xt1]

        def w2l(g):
            S.inc(g.dma_start(wout2[:, :].rearrange("p (k dd) -> p k dd", k=16),
                              d["woutT"].rearrange("(k p) dd -> p k dd", p=128)),
                  "GPD", 16)
        GG(w2l)
        gpd_wout = S.n["GPD"]

        def w2sc(v):
            S.wait(v, "GPD", gpd_wout)
            ins = None
            for ec in range(16):
                ins = v.tensor_scalar(wout2[:, ec * D_:(ec + 1) * D_],
                                      wout2[:, ec * D_:(ec + 1) * D_],
                                      nwp_s[:, ec:ec + 1], None, ALU.mult)
            S.inc(ins, "VC")
        VV(w2sc)

        for c in range(NTC):
            par = c % 2
            def xtl(y, c=c, par=par):
                S.wait(y, "VC")
                for ec in range(16):
                    S.inc(y.dma_start(XT[par][:, ec * 128:(ec + 1) * 128],
                                      xa_sp[ec, :, c * Q:(c + 1) * Q],
                                      transpose=True), "SYD", 16)
            SY(xtl)
            syd_xt = S.n["SYD"]

            def btr(t, c=c):
                S.wait(t, "VC")
                S.inc(t.transpose(psB[:, 0:32].bitcast(BF16),
                                  bcma[0:64, c * Q:(c + 1) * Q],
                                  identb[0:64, 0:64]), "TC")
            TT(btr)

            def bev(v, c=c):
                S.wait(v, "TC")
                S.inc(v.tensor_copy(btp[:, :], psB[:, 0:32].bitcast(BF16)), "VC")
            VV(bev)

            def zb(v, c=c, par=par, syd_xt=syd_xt):
                S.wait(v, "SYD", syd_xt)
                S.inc(v.tensor_tensor(
                    zal[:, :].rearrange("p (h q) -> p h q", q=P),
                    XT[par][:, :].rearrange("p (h q) -> p h q", q=P),
                    w2TA[:, c * H:(c + 1) * H].unsqueeze(2)
                        .broadcast_to([128, H, P]), ALU.mult), "VC")
            VV(zb)

            def scm(t, c=c):
                S.wait(t, "VC")
                ins = None
                for i in range(4):
                    ins = t.matmul(psA[0:64, i * 512:(i + 1) * 512], btp[:, :],
                                   zal[:, i * 512:(i + 1) * 512], start=True,
                                   stop=True)
                S.inc(ins, "TC")
            TT(scm)

            def hup(v, c=c):
                S.wait(v, "TC")
                src, dst = HB[c % 2], HB[1 - c % 2]
                v.tensor_tensor(dst[:, :].rearrange("n (h q) -> n h q", q=P),
                                src[:, :].rearrange("n (h q) -> n h q", q=P),
                                lam64A[:, c * H:(c + 1) * H].unsqueeze(2)
                                    .broadcast_to([64, H, P]), ALU.mult)
                v.drain()
                S.inc(v.tensor_tensor(dst[:, :], dst[:, :], psA[0:64, :], ALU.add),
                      "VC")
            VV(hup)
        fence()
        if "Hinit" in dbg:
            def dh(y):
                S.inc(y.dma_start(dbg["Hinit"], HB[0][:, :]), "ST", 16)
            SY(dh)

        vc_hup = {}
        vc_resv = {}
        vc_yc = {}
        vc_bev = {}
        tc_ysm_map = {}
        sc_a3 = S.n["SC"]
        for c in range(NTC):
            par = c % 2
            hsrc, hdst = HB[c % 2], HB[1 - c % 2]

            def xtl(y, c=c, par=par):
                if c - 1 in vc_resv:
                    S.wait(y, "VC", vc_resv[c - 1])
                if c - 1 in tc_ysm_map:
                    S.wait(y, "TC", tc_ysm_map[c - 1])
                for ec in range(8):
                    S.inc(y.dma_start(XT[par][:, ec * 128:(ec + 1) * 128],
                                      x_sp[ec, :, c * Q:(c + 1) * Q],
                                      transpose=True), "SYD", 16)
                S.inc(y.dma_start(xck[:, :].rearrange("p (ec q) -> p ec q", q=Q),
                                  x_sp[:, :, c * Q:(c + 1) * Q].transpose([1, 0, 2])),
                      "SYD", 16)
                S.inc(y.dma_start(szt[:, :].rearrange("p (ec q) -> p ec q", q=Q),
                                  sz_sp[:, :, c * Q:(c + 1) * Q].transpose([1, 0, 2])),
                      "SYD", 16)
                S.inc(y.dma_start(xres[:, :], d["xin"][c * 128:(c + 1) * 128, :]),
                      "SYD", 16)
                S.inc(y.dma_start(cmt[:, :], bcm[64:128, c * Q:(c + 1) * Q]),
                      "SYD", 16)
            SY(xtl)

            def xtl2(sc, c=c, par=par):
                if c - 1 in vc_resv:
                    S.wait(sc, "VC", vc_resv[c - 1])
                if c - 1 in tc_ysm_map:
                    S.wait(sc, "TC", tc_ysm_map[c - 1])
                for ec in range(8, 16):
                    S.inc(sc.dma_start(XT[par][:, ec * 128:(ec + 1) * 128],
                                       x_sp[ec, :, c * Q:(c + 1) * Q],
                                       transpose=True), "SYD", 16)
            SS(xtl2)
            syd_xt = S.n["SYD"]

            def e2t(t, c=c):
                if c - 1 in vc_bev:
                    S.wait(t, "VC", vc_bev[c - 1])
                S.wait(t, "SC", sc_a3)
                ins = None
                for fc in range(16):
                    ins = t.matmul(psB[:, fc * 128:(fc + 1) * 128],
                                   c_ind[:, fc * 128:(fc + 1) * 128],
                                   erow[:, c * Q:(c + 1) * Q], start=True, stop=True)
                S.inc(ins, "TC")
            TT(e2t)

            def e2v(v, c=c):
                S.wait(v, "TC")
                if c - 1 in vc_yc:
                    S.wait(v, "VC", vc_yc[c - 1])
                S.inc(v.tensor_copy(e2sb[:, :], psB[:, :]), "VC")
            VV(e2v)
            vc_e2 = S.n["VC"]

            def dtar(v, c=c):
                S.inc(v.tensor_scalar(scr32[:, c * Q:(c + 1) * Q],
                                      dtv[:, c * Q:(c + 1) * Q], acol[:, 0:1], None,
                                      ALU.mult), "VC")
            VV(dtar)

            vc_dtar = S.n["VC"]

            def dtat(t, c=c, vc_dtar=vc_dtar):
                S.wait(t, "VC", vc_dtar)
                if c - 1 in vc_hup:
                    S.wait(t, "VC", vc_hup[c - 1])
                S.inc(t.transpose(psA[:, 0:32], scr32[:, c * Q:(c + 1) * Q],
                                  c_idf[0:32, 0:32]), "TC")
            TT(dtat)

            def dtae(v, c=c):
                S.wait(v, "TC")
                S.inc(v.tensor_copy(dab[:, :], psA[:, 0:32]), "VC")
            VV(dtae)

            def rhb(v, c=c):
                S.inc(v.tensor_tensor(
                    wq[:, :].rearrange("p (h q) -> p h q", q=Q),
                    c_umask[:, :].unsqueeze(1).broadcast_to([128, H, Q]),
                    dab[:, :].unsqueeze(2).broadcast_to([128, H, Q]),
                    ALU.mult), "VC")
            VV(rhb)
            vc_rh = S.n["VC"]

            sc_mde = [None]
            for hf in range(4):
                def cdf(t, c=c, hf=hf, vc_rh=vc_rh):
                    S.wait(t, "VC", vc_rh)
                    if hf >= 1:
                        S.wait(t, "SC", sc_mde[0])
                    elif c - 1 in vc_hup:
                        S.wait(t, "VC", vc_hup[c - 1])
                    ins = None
                    for hh in range(8):
                        h = hf * 8 + hh
                        ins = t.matmul(psA[:, hh * 128:(hh + 1) * 128],
                                       c_strict[:, :], wq[:, h * 128:(h + 1) * 128],
                                       start=True, stop=True)
                    S.inc(ins, "TC")
                TT(cdf)

                def mde(sc, c=c, hf=hf):
                    S.wait(sc, "TC")
                    S.inc(sc.activation(wq[:, hf * 1024:(hf + 1) * 1024],
                                        psA[:, 0:1024], ACT.Exp), "SC")
                SS(mde)
                sc_mde[0] = S.n["SC"]

            def gpm(t, c=c, vc_e2=vc_e2, syd_xt=syd_xt):
                S.wait(t, "VC", vc_e2)
                S.wait(t, "SYD", syd_xt)
                S.inc(t.matmul(psB[:, 0:128], bcm[0:64, c * Q:(c + 1) * Q],
                               cmt[:, :], start=True, stop=True), "TC")
            TT(gpm)

            def gpe(v, c=c):
                S.wait(v, "TC")
                S.inc(v.tensor_tensor(gpmb[:, :], psB[:, 0:128], c_umask[:, :],
                                      ALU.mult), "VC")
            VV(gpe)

            def wpb(v, c=c):
                S.wait(v, "SC", sc_mde[0])
                ins = None
                for hf in range(2):
                    ins = v.tensor_tensor(
                        wq[:, hf * E:(hf + 1) * E].rearrange("p (h q) -> p h q", q=Q),
                        wq[:, hf * E:(hf + 1) * E].rearrange("p (h q) -> p h q", q=Q),
                        gpmb[:, :].unsqueeze(1).broadcast_to([128, 16, Q]),
                        ALU.mult)
                S.inc(ins, "VC")
            VV(wpb)

            def xdt(v, c=c, par=par, syd_xt=syd_xt):
                S.wait(v, "SYD", syd_xt)
                S.inc(v.tensor_tensor(
                    zal[:, :].rearrange("p (h q) -> p h q", q=P),
                    XT[par][:, :].rearrange("p (h q) -> p h q", q=P),
                    dtvT[:, c * H:(c + 1) * H].unsqueeze(2)
                        .broadcast_to([128, H, P]), ALU.mult), "VC")
            VV(xdt)

            def hcast(v, c=c, hsrc=hsrc):
                S.inc(v.tensor_copy(g2b[0:64, :], hsrc[:, :]), "VC")
            VV(hcast)
            vc_xdt = S.n["VC"]

            def yim(t, c=c, vc_xdt=vc_xdt):
                S.wait(t, "VC", vc_xdt)
                ins = None
                for h in range(H):
                    fc, odd = h // 2, h % 2
                    ins = t.matmul(psB[odd * 64:odd * 64 + 64,
                                       fc * 128:(fc + 1) * 128],
                                   zal[:, h * P:(h + 1) * P],
                                   wq[:, h * 128:(h + 1) * 128],
                                   start=True, stop=True)
                S.inc(ins, "TC")
            TT(yim)

            def ysm(t, c=c):
                S.wait(t, "SC", sc_mde[0])
                if c - 1 in vc_resv:
                    S.wait(t, "VC", vc_resv[c - 1])
                ins = None
                for fc in range(16):
                    ins = t.matmul(psA[:, fc * 128:(fc + 1) * 128],
                                   g2b[0:64, fc * 128:(fc + 1) * 128],
                                   cmt[:, :], start=True, stop=True)
                S.inc(ins, "TC")
            TT(ysm)
            tc_y = S.n["TC"]
            tc_ysm_map[c] = tc_y

            def yc(v, c=c, tc_y=tc_y, syd_xt=syd_xt):
                S.wait(v, "TC", tc_y)
                S.wait(v, "SYD", syd_xt)
                v.tensor_tensor(ybuf[:, :], psA[:, :], e2sb[:, :], ALU.mult)
                v.drain()
                v.tensor_tensor(ybuf[:, :], ybuf[:, :], psB[:, :], ALU.add)
                v.tensor_tensor(gbuf[:, :].rearrange("p (fc q) -> p fc q", q=Q),
                                xck[:, :].rearrange("p (fc q) -> p fc q", q=Q),
                                dexp_s[:, :].unsqueeze(2)
                                    .broadcast_to([128, 16, Q]), ALU.mult)
                v.drain()
                S.inc(v.tensor_tensor(ybuf[:, :], ybuf[:, :], gbuf[:, :], ALU.add),
                      "VC")
            VV(yc)
            vc_y = S.n["VC"]
            vc_yc[c] = vc_y
            vc_yc[c] = vc_y
            if c == 0 and "y0" in dbg:
                def dy(y, vc_y=vc_y):
                    S.wait(y, "VC", vc_y)
                    S.inc(y.dma_start(dbg["y0"], ybuf[:, :]), "ST", 16)
                SY(dy)

            def zb2(v, c=c, par=par):
                S.inc(v.tensor_tensor(
                    zal[:, :].rearrange("p (h q) -> p h q", q=P),
                    XT[par][:, :].rearrange("p (h q) -> p h q", q=P),
                    w2T[:, c * H:(c + 1) * H].unsqueeze(2)
                        .broadcast_to([128, H, P]), ALU.mult), "VC")
            VV(zb2)

            def btr2(t, c=c):
                S.wait(t, "VC")
                S.inc(t.transpose(psB[:, 0:32].bitcast(BF16),
                                  bcm[0:64, c * Q:(c + 1) * Q],
                                  identb[0:64, 0:64]), "TC")
            TT(btr2)

            def bev2(v, c=c):
                S.wait(v, "TC")
                S.inc(v.tensor_copy(btp[:, :], psB[:, 0:32].bitcast(BF16)), "VC")
            VV(bev2)
            vc_bev[c] = S.n["VC"]

            def scm2(t, c=c):
                S.wait(t, "VC")
                if c - 1 in vc_resv:
                    S.wait(t, "VC", vc_resv[c - 1])
                ins = None
                for i in range(4):
                    ins = t.matmul(psA[0:64, i * 512:(i + 1) * 512], btp[:, :],
                                   zal[:, i * 512:(i + 1) * 512], start=True,
                                   stop=True)
                S.inc(ins, "TC")
            TT(scm2)

            def hup2(v, c=c, hsrc=hsrc, hdst=hdst):
                S.wait(v, "TC")
                v.tensor_tensor(hdst[:, :].rearrange("n (h q) -> n h q", q=P),
                                hsrc[:, :].rearrange("n (h q) -> n h q", q=P),
                                lam64[:, c * H:(c + 1) * H].unsqueeze(2)
                                    .broadcast_to([64, H, P]), ALU.mult)
                v.drain()
                S.inc(v.tensor_tensor(hdst[:, :], hdst[:, :], psA[0:64, :], ALU.add),
                      "VC")
            VV(hup2)
            vc_hup[c] = S.n["VC"]

            def gat(v, c=c):
                v.tensor_tensor(gbuf[:, :], ybuf[:, :], szt[:, :], ALU.mult)
                v.drain()
                S.inc(v.tensor_tensor(g2b[:, :], gbuf[:, :], gbuf[:, :], ALU.mult),
                      "VC")
            VV(gat)

            def rmsm(t, c=c):
                S.wait(t, "VC")
                S.wait(t, "VC", vc_hup[c])
                ins = None
                for fc in range(16):
                    ins = t.matmul(psA[0:1, 1024:1152], onescol[:, 0:1],
                                   g2b[:, fc * 128:(fc + 1) * 128],
                                   start=(fc == 0), stop=(fc == 15))
                S.inc(ins, "TC")
            TT(rmsm)

            def rmss(sc, c=c):
                S.wait(sc, "TC")
                sc.activation(rmsrow[0:1, 0:128], psA[0:1, 1024:1152], ACT.Ln,
                              bias=cvals[0:1, 0:1], scale=1.0 / E)
                S.inc(sc.activation(rmsrow[0:1, 128:256], rmsrow[0:1, 0:128],
                                    ACT.Exp, scale=-0.5), "SC")
            SS(rmss)

            def rim(t, c=c):
                S.wait(t, "SC")
                S.inc(t.matmul(psA[:, 1152:1280], ones1f[0:1, :],
                               rmsrow[0:1, 128:256], start=True, stop=True), "TC")
            TT(rim)

            def rie(v, c=c):
                S.wait(v, "TC")
                S.inc(v.tensor_copy(rib[:, :], psA[:, 1152:1280]), "VC")
            VV(rie)

            def ynbv(v, c=c):
                S.inc(v.tensor_tensor(
                    ynb[:, :].rearrange("p (fc q) -> p fc q", q=Q),
                    gbuf[:, :].rearrange("p (fc q) -> p fc q", q=Q),
                    rib[:, :].unsqueeze(1).broadcast_to([128, 16, Q]),
                    ALU.mult), "VC")
            VV(ynbv)
            vc_ynb = S.n["VC"]

            def wom(t, c=c, vc_ynb=vc_ynb):
                S.wait(t, "VC", vc_ynb)
                S.wait(t, "VC", vc_hup[c])
                ins = None
                for ek in range(16):
                    for hh in range(2):
                        ins = t.matmul(
                            psA[:, 1024 + hh * 512:1024 + (hh + 1) * 512],
                            ynb[:, ek * 128:(ek + 1) * 128],
                            wout2[:, ek * D_ + hh * 512: ek * D_ + (hh + 1) * 512],
                            start=(ek == 0), stop=(ek == 15))
                S.inc(ins, "TC")
            TT(wom)

            def resv(v, c=c):
                S.wait(v, "TC")
                S.wait(v, "SYD")
                S.inc(v.scalar_tensor_tensor(vbuf[:, :], xres[:, :], 2.0,
                                             psA[:, 1024:2048], ALU.mult, ALU.add),
                      "VC")
            VV(resv)
            vc_resv[c] = S.n["VC"]

            ln_chunk(vbuf[:, :], x1t[:, :])

            def x1s(y, c=c):
                S.wait(y, "VC")
                S.inc(y.dma_start(x1_sp[c * 128:(c + 1) * 128, :], x1t[:, :]),
                      "SYD", 16)
            SY(x1s)

        fence()
        if "Hend" in dbg:
            def dh2(y):
                S.inc(y.dma_start(dbg["Hend"], HB[0][:, :]), "ST", 16)
            SY(dh2)

    pop_to(m_bc)
    pop_to(m_ab)
    MACC = push("maccb", [128, NTC * D_], F32)
    x2bf = push("x2bf", [128, 8 * T], BF16, is_transpose_dest=True)
    m_d = mark()
    x2lo = push("x2lo", [128, 8 * T], BF16, is_transpose_dest=True)
    hib = push("hib", [128, D_], BF16)
    lob = push("lob", [128, D_], BF16)
    scoresF = push("scoresF", [32, T], F32)
    if True:
        def x1l(y):
            for tc in range(NTC):
                S.inc(y.dma_start(MACC[:, tc * D_:(tc + 1) * D_],
                                  x1_sp[tc * 128:(tc + 1) * 128, :]), "SYD", 16)
        SY(x1l)
        if "x1" in dbg:
            def dx1(y):
                S.wait(y, "SYD")
                S.inc(y.dma_start(dbg["x1"], MACC[:, :]), "ST", 16)
            SY(dx1)

        syd_tc = {}
        for tc in range(NTC):
            def hilo(v, tc=tc):
                S.wait(v, "SYD")
                v.tensor_copy(hib[:, :], MACC[:, tc * D_:(tc + 1) * D_])
                v.drain()
                S.inc(v.tensor_tensor(lob[:, :], MACC[:, tc * D_:(tc + 1) * D_],
                                      hib[:, :], ALU.subtract), "VC")
            VV(hilo)

            def htr(y, tc=tc):
                S.wait(y, "VC")
                for kd in range(4):
                    S.inc(y.dma_start(
                        x2bf[:, kd * T + tc * 128: kd * T + (tc + 1) * 128],
                        hib[:, kd * 128:(kd + 1) * 128], transpose=True),
                        "SYD", 16)
                    S.inc(y.dma_start(
                        x2lo[:, kd * T + tc * 128: kd * T + (tc + 1) * 128],
                        lob[:, kd * 128:(kd + 1) * 128], transpose=True),
                        "SYD", 16)
            SY(htr)

            def htr2(sc, tc=tc):
                S.wait(sc, "VC")
                for kd in range(4, 8):
                    S.inc(sc.dma_start(
                        x2bf[:, kd * T + tc * 128: kd * T + (tc + 1) * 128],
                        hib[:, kd * 128:(kd + 1) * 128], transpose=True),
                        "SYD", 16)
                    S.inc(sc.dma_start(
                        x2lo[:, kd * T + tc * 128: kd * T + (tc + 1) * 128],
                        lob[:, kd * 128:(kd + 1) * 128], transpose=True),
                        "SYD", 16)
            SS(htr2)
            syd_tc[tc] = S.n["SYD"]
        syd_hilo = S.n["SYD"]

        for tg in range(4):
            def rlm(t, tg=tg):
                S.wait(t, "SYD", syd_tc[4 * tg + 3])
                if tg > 0:
                    S.wait(t, "SC")
                mms = []
                for kd in range(8):
                    for a_, b_ in [(gwh, x2bf), (gwl, x2bf), (gwh, x2lo)]:
                        mms.append((a_, b_, kd))
                ins = None
                for i, (a_, b_, kd) in enumerate(mms):
                    ins = t.matmul(psB[0:32, 0:512],
                                   a_[:, kd * H:(kd + 1) * H],
                                   b_[:, kd * T + tg * 512: kd * T + (tg + 1) * 512],
                                   start=(i == 0), stop=(i == len(mms) - 1))
                S.inc(ins, "TC")
            TT(rlm)

            def sig(sc, tg=tg):
                S.wait(sc, "TC")
                S.inc(sc.activation(scoresF[:, tg * 512:(tg + 1) * 512],
                                    psB[0:32, 0:512], ACT.Sigmoid), "SC")
            SS(sig)

        for tc in range(NTC):
            def str_(t, tc=tc):
                S.wait(t, "SC")
                S.wait(t, "VC")
                S.inc(t.transpose(psA[:, 0:32],
                                  scoresF[:, tc * 128:(tc + 1) * 128],
                                  c_idf[0:32, 0:32]), "TC")
            TT(str_)

            def rmath(v, tc=tc):
                S.wait(v, "TC")
                sco = scorestk[:, tc * H:(tc + 1) * H]
                v.tensor_copy(sco, psA[:, 0:32])
                sc_ = rscr[:, 0:32]
                t2s = rscr[:, 32:40]
                t2b = rscr2[:, 32:40]
                msk = rscr2[:, 0:32]
                v.drain()
                v.tensor_tensor(sc_, sco, c_eb[:, :], ALU.add)
                v.drain()
                pairs = [(0, 1), (0, 2), (0, 3), (1, 2), (1, 3), (2, 3)]
                for k_, (i_, j_) in enumerate(pairs):
                    dstp = t2s if k_ == 0 else t2b
                    v.tensor_tensor(dstp, sc_[:, i_:32:4], sc_[:, j_:32:4],
                                    ALU.add)
                    v.drain()
                    if k_ > 0:
                        v.tensor_tensor(t2s, t2s, t2b, ALU.max)
                        v.drain()
                v.tensor_copy(t2b, t2s)
                v.drain()
                for _ in range(3):
                    v.reduce_max(stat[:, 0:1], t2b, axis=AXX)
                    v.drain()
                    v.tensor_scalar(sqscr[:, 0:8], t2b, stat[:, 0:1], -1e30,
                                    ALU.is_ge, ALU.mult)
                    v.drain()
                    v.tensor_tensor(t2b, t2b, sqscr[:, 0:8], ALU.add)
                    v.drain()
                v.reduce_max(stat[:, 0:1], t2b, axis=AXX)
                v.drain()
                v.tensor_scalar(t2b, t2s, stat[:, 0:1], None, ALU.is_ge)
                v.drain()
                v.tensor_tensor(msk.rearrange("p (g j) -> p g j", j=4),
                                sc_.rearrange("p (g j) -> p g j", j=4),
                                t2b.unsqueeze(2)
                                   .broadcast_to([128, 8, 4]), ALU.mult)
                v.drain()
                v.tensor_copy(sqscr[:, 0:32], msk)
                v.drain()
                for _ in range(5):
                    v.reduce_max(stat[:, 0:1], sqscr[:, 0:32], axis=AXX)
                    v.drain()
                    v.tensor_scalar(sqscr[:, 32:64], sqscr[:, 0:32],
                                    stat[:, 0:1], -1e30, ALU.is_ge, ALU.mult)
                    v.drain()
                    v.tensor_tensor(sqscr[:, 0:32], sqscr[:, 0:32],
                                    sqscr[:, 32:64], ALU.add)
                    v.drain()
                v.reduce_max(stat[:, 0:1], sqscr[:, 0:32], axis=AXX)
                v.drain()
                v.tensor_scalar(msk, msk, stat[:, 0:1], None, ALU.is_ge)
                v.drain()
                v.tensor_tensor(msk, msk, sco, ALU.mult)
                v.drain()
                v.reduce_sum(stat[:, 1:2], msk, axis=AXX)
                v.drain()
                v.tensor_scalar(stat[:, 1:2], stat[:, 1:2], 1e-20, None, ALU.add)
                v.drain()
                v.reciprocal(stat[:, 2:3], stat[:, 1:2])
                v.drain()
                S.inc(v.tensor_scalar(combB[:, tc * H:(tc + 1) * H], msk,
                                      stat[:, 2:3], 2.5, ALU.mult, ALU.mult),
                      "VC")
            VV(rmath)

            def ctr(t, tc=tc):
                S.wait(t, "VC")
                S.inc(t.transpose(psA[0:32, 128:256], combB[:, tc * H:(tc + 1) * H],
                                  c_idf[:, :]), "TC")
            TT(ctr)

            def cte(v, tc=tc):
                S.wait(v, "TC")
                S.inc(v.tensor_copy(combTb[0:32, tc * 128:(tc + 1) * 128],
                                    psA[0:32, 128:256]), "VC")
            VV(cte)
        fence()
        if "scores" in dbg:
            def ds(y):
                S.inc(y.dma_start(dbg["scores"], scorestk[:, :]), "ST", 16)
            SY(ds)
        if "comb" in dbg:
            def dc(y):
                S.inc(y.dma_start(dbg["comb"], combB[:, :]), "ST", 16)
            SY(dc)

    pop_to(m_d)
    m_e = mark()
    wtu = push("wtu", [128, 4 * 8 * 512], BF16)
    wtd = push("wtd", [128, 4 * 4 * D_], BF16)
    heb = push("heb", [128, 4096], BF16)
    cbc4 = push("cbc4", [128, 1024], BF16)
    crow = push("crow", [1, 4 * T], BF16)
    if True:

        groups = [[0, 1, 2, 3], [4, 5, 6, 7], [8, 9, 10, 11], [12, 13, 14, 15],
                  [16, 17, 18, 19], [20, 21, 22, 23], [24, 25, 26, 27],
                  [28, 29, 30, 31], [32, 33, 34, 35]]

        def pf_expert(g, e, slot):
            S.inc(g.dma_start(
                wtu[:, slot * 4096:(slot + 1) * 4096]
                    .rearrange("p (k i) -> p k i", k=8),
                d["upT"][e].rearrange("(k p) i -> p k i", p=128)), "GPD", 16)
            S.inc(g.dma_start(
                wtd[:, slot * 4 * D_:(slot + 1) * 4 * D_]
                    .rearrange("p (k dd) -> p k dd", k=4),
                d["dnT"][e].rearrange("(k p) dd -> p k dd", p=128)), "GPD", 16)

        def pf0(g):
            for slot, e in enumerate(groups[0]):
                pf_expert(g, e, slot)
        GG(pf0)

        gpd_exp = {}
        for slot, e in enumerate(groups[0]):
            gpd_exp[(0, slot)] = None
        gpd_pf0 = S.n["GPD"]
        vc_ep = {}
        tc_dn = {}
        sc_sq = {}
        vc_mev = {}
        tc_cbm = {}

        for gi, grp in enumerate(groups):
            def crl(y, grp=grp, gi=gi):
                S.wait(y, "TC")
                S.wait(y, "VC")
                for si, e in enumerate(grp):
                    S.inc(y.dma_start(crow[0:1, si * T:(si + 1) * T],
                                      combTb[e:e + 1, :]), "SYD", 16)
            SY(crl)
            syd_crow = S.n["SYD"]

            for tt in range(8):
                def cbm4(t, gi=gi, tt=tt, grp=grp, syd_crow=syd_crow):
                    S.wait(t, "SYD", syd_crow)
                    if (gi, tt - 1, 3) in vc_ep:
                        S.wait(t, "VC", vc_ep[(gi, tt - 1, 3)])
                    elif gi > 0 and (gi - 1, 7, 3) in vc_ep:
                        S.wait(t, "VC", vc_ep[(gi - 1, 7, 3)])
                    ins = None
                    for si in range(len(grp)):
                        ins = t.matmul(psB[:, si * 256:(si + 1) * 256],
                                       ones1b[0:1, :],
                                       crow[0:1, si * T + tt * 256:
                                            si * T + (tt + 1) * 256],
                                       start=True, stop=True)
                    S.inc(ins, "TC")
                TT(cbm4)
                tc_cbm[(gi, tt)] = S.n["TC"]

                def cbe4(v, gi=gi, tt=tt):
                    S.wait(v, "TC", tc_cbm[(gi, tt)])
                    S.inc(v.tensor_copy(cbc4[:, :], psB[:, 0:1024]), "VC")
                VV(cbe4)
                vc_cbe = S.n["VC"]

                vc_r = {}
                for si, e in enumerate(grp):
                    half = si % 2
                    def upm(t, e=e, tt=tt, si=si, gi=gi, half=half,
                            vc_cbe=vc_cbe):
                        g_ = gpd_exp.get((gi, si))
                        S.wait(t, "GPD", gpd_pf0 if g_ is None else g_)
                        if si < 2:
                            S.wait(t, "VC", vc_cbe)
                        for key in [(gi, tt, si - 2), (gi, tt - 1, si),
                                    (gi - 1, 7, si)]:
                            if key in vc_r:
                                S.wait(t, "VC", vc_r[key])
                                break
                        ins = None
                        for ic in range(4):
                            for kd in range(8):
                                ins = t.matmul(
                                    psB[:, half * 1024 + ic * 256:
                                        half * 1024 + (ic + 1) * 256],
                                    wtu[:, si * 4096 + kd * 512 + ic * 128:
                                        si * 4096 + kd * 512 + (ic + 1) * 128],
                                    x2bf[:, kd * T + tt * 256:
                                         kd * T + (tt + 1) * 256],
                                    start=(kd == 0), stop=(kd == 7))
                        S.inc(ins, "TC")
                    TT(upm)
                    tc_up = S.n["TC"]

                    def ep1v(v, si=si, half=half, tc_up=tc_up, gi=gi, tt=tt):
                        S.wait(v, "TC", tc_up)
                        for key in [(gi, tt - 1, si), (gi - 1, 7, si)]:
                            if key in tc_dn:
                                S.wait(v, "TC", tc_dn[key])
                                break
                        S.inc(v.tensor_scalar(
                            heb[:, si * 1024:(si + 1) * 1024],
                            psB[:, half * 1024:(half + 1) * 1024],
                            0.0, None, ALU.max), "VC")
                    VV(ep1v)
                    vc_r[(gi, tt, si)] = S.n["VC"]

                def sqbody(sc, gi=gi, tt=tt, grp=grp):
                    for si in range(len(grp)):
                        S.wait(sc, "VC", vc_r[(gi, tt, si)])
                        S.inc(sc.activation(heb[:, si * 1024:(si + 1) * 1024],
                                            heb[:, si * 1024:(si + 1) * 1024],
                                            ACT.Square), "SC")
                        sc_sq[(gi, tt, si)] = S.n["SC"]
                SS(sqbody)

                for si, e in enumerate(grp):
                    def ep3v(v, si=si, gi=gi, tt=tt):
                        S.wait(v, "SC", sc_sq[(gi, tt, si)])
                        S.inc(v.tensor_tensor(
                            heb[:, si * 1024:(si + 1) * 1024]
                                .rearrange("p (ic q) -> p ic q", q=256),
                            heb[:, si * 1024:(si + 1) * 1024]
                                .rearrange("p (ic q) -> p ic q", q=256),
                            cbc4[:, si * 256:(si + 1) * 256].unsqueeze(1)
                                .broadcast_to([128, 4, 256]), ALU.mult), "VC")
                    VV(ep3v)
                    vc_ep[(gi, tt, si)] = S.n["VC"]

                    first = (si == 0)
                    last = (si == len(grp) - 1)
                    def dnm(t, e=e, tt=tt, si=si, gi=gi, first=first, last=last):
                        S.wait(t, "VC", vc_ep[(gi, tt, si)])
                        if first:
                            for key in [(gi, tt - 1), (gi - 1, 7)]:
                                if key in vc_mev:
                                    S.wait(t, "VC", vc_mev[key])
                                    break
                        ins = None
                        for tc2 in range(2):
                            for hh in range(2):
                                for ik in range(4):
                                    ins = t.matmul(
                                        psA[:, tc2 * 1024 + hh * 512:
                                            tc2 * 1024 + (hh + 1) * 512],
                                        heb[:, si * 1024 + ik * 256 + tc2 * 128:
                                            si * 1024 + ik * 256 + tc2 * 128 + 128],
                                        wtd[:, si * 4 * D_ + ik * D_ + hh * 512:
                                            si * 4 * D_ + ik * D_ + (hh + 1) * 512],
                                        start=(first and ik == 0),
                                        stop=(last and ik == 3))
                        S.inc(ins, "TC")
                    TT(dnm)
                    tc_dn[(gi, tt, si)] = S.n["TC"]

                    if tt == 7 and gi + 1 < len(groups) and si < len(groups[gi + 1]):
                        def pfn(g, gi=gi, si=si):
                            S.wait(g, "TC", tc_dn[(gi, 7, si)])
                            pf_expert(g, groups[gi + 1][si], si)
                        GG(pfn)
                        gpd_exp[(gi + 1, si)] = S.n["GPD"]

                def mev(v, tt=tt, gi=gi):
                    S.wait(v, "TC", tc_dn[(gi, tt, len(grp) - 1)])
                    ins = None
                    for tc2 in range(2):
                        tcg = tt * 2 + tc2
                        ins = v.tensor_tensor(
                            MACC[:, tcg * D_:(tcg + 1) * D_],
                            MACC[:, tcg * D_:(tcg + 1) * D_],
                            psA[:, tc2 * 1024:(tc2 + 1) * 1024], ALU.add)
                    S.inc(ins, "VC")
                VV(mev)
                vc_mev[(gi, tt)] = S.n["VC"]
        fence()
        if "macc" in dbg:
            def dm(y):
                S.inc(y.dma_start(dbg["macc"], MACC[:, :]), "ST", 16)
            SY(dm)

        for tc in range(NTC):
            ln_chunk(MACC[:, tc * D_:(tc + 1) * D_], MACC[:, tc * D_:(tc + 1) * D_])

            def out_dma(y, tc=tc):
                S.wait(y, "VC")
                S.inc(y.dma_start(yout[tc * 128:(tc + 1) * 128, :],
                                  MACC[:, tc * D_:(tc + 1) * D_]), "ST", 16)
            SY(out_dma)
            st_now = S.n["ST"]

            def stw(sc, st_now=st_now):
                S.wait(sc, "ST", st_now)
            SS(stw)

            def vtw(v, st_now=st_now):
                S.wait(v, "ST", st_now)
            VV(vtw)

        def final_wait(y):
            S.wait(y, "ST")
        SY(final_wait)

    blk.__exit__(None, None, None)
    pop_to(0)
    for g in reversed(guards):
        g.__exit__(None, None, None)
    return nc, dbg


def _prep_core_inputs(inputs, core):
    b, half = core >> 1, core & 1
    f32 = lambda a: np.ascontiguousarray(np.asarray(a, dtype=np.float32))
    x = f32(inputs["L0_attn_x"])
    W_in = f32(inputs["L0_attn_W_in"])
    conv_w = f32(inputs["L0_attn_conv_w"])
    conv_b = f32(inputs["L0_attn_conv_b"])
    lnw = f32(inputs["L0_attn_ln_w"]); lnb = f32(inputs["L0_attn_ln_b"])
    dtb = f32(inputs["L0_attn_dt_bias"]); alog = f32(inputs["L0_attn_A_log"])
    Dv = f32(inputs["L0_attn_D"]); nw = f32(inputs["L0_attn_norm_w"])
    W_out = f32(inputs["L0_attn_W_out"])
    gw = f32(inputs["L0_ffn_gate_w"]); eb = f32(inputs["L0_ffn_e_bias"])
    up = f32(inputs["L0_ffn_up_w"]); dn = f32(inputs["L0_ffn_down_w"])
    sup = f32(inputs["L0_ffn_shared_up_w"]); sdn = f32(inputs["L0_ffn_shared_down_w"])

    def convlay(w):
        return np.ascontiguousarray(w.reshape(NEC, 128, 4).transpose(1, 0, 2)
                                    .reshape(128, NEC * 4))

    def convblay(w):
        return np.ascontiguousarray(w.reshape(NEC, 128).T)

    upT = np.ascontiguousarray(np.concatenate(
        [up.transpose(0, 2, 1),
         np.ascontiguousarray(sup.T).reshape(D_, 4, 512).transpose(1, 0, 2)],
        axis=0))
    dnT = np.ascontiguousarray(np.concatenate(
        [dn.transpose(0, 2, 1),
         np.ascontiguousarray(sdn.T).reshape(4, 512, D_)], axis=0))

    if half == 1:
        xaux = np.ascontiguousarray(x[b, 0:T])
        winTa = np.ascontiguousarray(W_in[E:].T)
        lnwa, lnba = lnw, lnb
        cwa, cba = convlay(conv_w), convblay(conv_b)
        dtba, aloga = dtb, alog
    else:
        xaux = np.zeros((T, D_), np.float32)
        winTa = np.zeros((D_, CD + H), np.float32)
        lnwa = np.zeros(D_, np.float32); lnba = np.zeros(D_, np.float32)
        cwa = np.zeros((128, NEC * 4), np.float32)
        cba = np.zeros((128, NEC), np.float32)
        dtba = np.zeros(H, np.float32); aloga = np.zeros(H, np.float32)

    return {
        "xin": np.ascontiguousarray(x[b, half * T:(half + 1) * T]),
        "xaux": xaux,
        "winT": np.ascontiguousarray(W_in.T),
        "winTa": winTa,
        "lnrows": np.ascontiguousarray(
            np.concatenate([lnw, lnb, lnwa, lnba]).reshape(1, 4 * D_)),
        "cw": convlay(conv_w), "cb": convblay(conv_b),
        "cwa": cwa, "cba": cba,
        "dtb": dtb.reshape(H, 1), "alog": alog.reshape(H, 1),
        "dtba": dtba.reshape(H, 1), "aloga": aloga.reshape(H, 1),
        "dexp": np.ascontiguousarray(np.repeat(Dv, P).reshape(16, 128).T),
        "nwp": np.ascontiguousarray(nw.reshape(16, 128).T),
        "woutT": np.ascontiguousarray(W_out.T),
        "gwT": np.ascontiguousarray(gw.T),
        "ebias": eb.reshape(1, H),
        "upT": upT, "dnT": dnT,
    }


_CACHE = {}


def kernel(**inputs):
    if "prog" not in _CACHE:
        _CACHE["prog"] = build_program()
    nc, _ = _CACHE["prog"]
    in_maps = [_prep_core_inputs(inputs, c) for c in range(8)]
    res = run_bass_kernel_spmd(nc, in_maps, core_ids=list(range(8)))
    out = np.zeros((B_, L_, D_), np.float32)
    for c in range(8):
        b, half = c >> 1, c & 1
        out[b, half * T:(half + 1) * T] = np.asarray(res.results[c]["yout"])
    return out


if __name__ == "__main__":
    nc, _ = build_program()
    print("build ok, instructions:", nc.n_instructions())


# version 11
# speedup vs baseline: 1.3029x; 1.0463x over previous
import numpy as np
import ml_dtypes
import concourse.bass as bass
import concourse.mybir as mybir
from concourse.bass_utils import run_bass_kernel_spmd

F32 = mybir.dt.float32
BF16 = mybir.dt.bfloat16
ALU = mybir.AluOpType
ACT = mybir.ActivationFunctionType
AXX = mybir.AxisListType.X

B_, L_, D_ = 4, 4096, 1024
T = 2048
NTC = 16
Q = 128
H, P, NS = 32, 64, 64
E = 2048
CD = 2176
DIP = 4256
NEX = 36
EPS = 1e-5
EPSI = 1e-6
TILE = 256
NTILE = T // TILE
NEC = 17


class Sems:
    def __init__(self, nc, names):
        self.h, self.n, self._g = {}, {}, []
        for nm in names:
            g = nc.semaphore(nm)
            self.h[nm] = g.__enter__()
            self._g.append(g)
            self.n[nm] = 0

    def inc(self, instr, nm, k=1):
        instr.then_inc(self.h[nm], k)
        self.n[nm] += k
        return instr

    def wait(self, eng, nm, val=None):
        eng.wait_ge(self.h[nm], self.n[nm] if val is None else val)


def build_program(debug=()):
    nc = bass.Bass("TRN2", target_bir_lowering=False)
    d = {}

    def din(name, shape):
        d[name] = nc.dram_tensor(name, list(shape), F32, kind="ExternalInput").ap()

    din("xin", (T, D_)); din("xaux", (T, D_))
    din("winT", (D_, DIP)); din("winTa", (D_, CD + H))
    din("lnrows", (1, 4 * D_))
    din("cw", (128, NEC * 4)); din("cb", (128, NEC))
    din("cwa", (128, NEC * 4)); din("cba", (128, NEC))
    din("dtb", (H, 1)); din("alog", (H, 1)); din("dtba", (H, 1)); din("aloga", (H, 1))
    din("dexp", (128, 16)); din("nwp", (128, 16))
    din("woutT", (E, D_)); din("gwT", (D_, H)); din("ebias", (1, H))
    din("upT", (NEX, D_, 512)); din("dnT", (NEX, 512, D_))
    yout = nc.dram_tensor("yout", [T, D_], F32, kind="ExternalOutput").ap()

    x_sp = nc.dram_tensor("x_sp", [16, 128, T], BF16, kind="Internal").ap()
    xa_sp = nc.dram_tensor("xa_sp", [16, 128, T], BF16, kind="Internal").ap()
    sz_sp = nc.dram_tensor("sz_sp", [16, 128, T], BF16, kind="Internal").ap()
    x1_sp = nc.dram_tensor("x1_sp", [T, D_], F32, kind="Internal").ap()

    dbg = {}
    dbg_shapes = {"u": (128, 8 * T), "dtv": (H, T), "acinc": (H, T),
                  "Hinit": (NS, E), "Hend": (NS, E), "y0": (128, E),
                  "x1": (128, NTC * D_), "scores": (128, NTC * H),
                  "comb": (128, NTC * H), "macc": (128, NTC * D_)}
    for nm in debug:
        dbg[nm] = nc.dram_tensor("dbg_" + nm, list(dbg_shapes[nm]), F32,
                                 kind="ExternalOutput").ap()

    S = Sems(nc, ["TC", "VC", "SC", "GC", "SYD", "GPD", "ST"])
    guards = []

    def sbuf(name, shape, dt, **kw):
        g = nc.sbuf_tensor(name, shape, dt, **kw)
        t = g.__enter__()
        guards.append(g)
        return t

    c_strict = sbuf("c_strict", [128, 128], BF16)
    c_umask = sbuf("c_umask", [128, 128], BF16)
    c_idf = sbuf("c_idf", [128, 128], F32)
    identb = sbuf("identb", [128, 128], BF16)
    c_ind = sbuf("c_ind", [32, 2048], BF16)
    ones1b = sbuf("ones1b", [1, 128], BF16)
    ones1f = sbuf("ones1f", [1, 128], F32)
    onescol = sbuf("onescol", [128, 1], BF16)
    c_eb = sbuf("c_eb", [128, H], F32)
    cw_s = sbuf("cw_s", [128, NEC * 4], F32)
    cb_s = sbuf("cb_s", [128, NEC], F32)
    cwa_s = sbuf("cwa_s", [128, NEC * 4], F32)
    cba_s = sbuf("cba_s", [128, NEC], F32)
    dexp_s = sbuf("dexp_s", [128, 16], F32)
    nwp_s = sbuf("nwp_s", [128, 16], F32)
    ebrow = sbuf("ebrow", [1, H], F32)
    gwh = sbuf("gwh", [128, 8 * H], BF16)
    gwl = sbuf("gwl", [128, 8 * H], BF16)
    scorestk = sbuf("scorestk", [128, NTC * H], F32)
    combB = sbuf("combB", [128, NTC * H], F32)
    combTb = sbuf("combTb", [NEX, T], BF16)
    stat = sbuf("stat", [128, 8], F32)
    sqscr = sbuf("sqscr", [128, D_], F32)
    rmsrow = sbuf("rmsrow", [1, 256], F32)
    rscr = sbuf("rscr", [128, 40], F32)
    rscr2 = sbuf("rscr2", [128, 40], F32)
    cvals = sbuf("cvals", [128, 2], F32)

    psA = nc.psum_tensor("psA", [128, 2048], F32).__enter__()
    psB = nc.psum_tensor("psB", [128, 2048], F32).__enter__()


    arena = []

    def push(name, shape, dt, **kw):
        g = nc.sbuf_tensor(name, shape, dt, **kw)
        t = g.__enter__()
        arena.append(g)
        return t

    def mark():
        return len(arena)

    def pop_to(m):
        while len(arena) > m:
            arena.pop().__exit__(None, None, None)

    m_ab = mark()
    pk1 = push("pk1", [64, T], F32)
    pk2 = push("pk2", [64, T], F32)
    pk3 = push("pk3", [64, T], F32)
    pk4 = push("pk4", [64, T], F32)
    pk5 = push("pk5", [64, T], F32)
    dtraw, dtrawA = pk1[0:32, :], pk1[32:64, :]
    dtv, dtvA = pk2[0:32, :], pk2[32:64, :]
    acinc, acincA = pk3[0:32, :], pk3[32:64, :]
    w2m, w2a = pk4[0:32, :], pk4[32:64, :]
    scr32, scr32A = pk5[0:32, :], pk5[32:64, :]
    apack = push("apack", [64, 1], F32)
    iotaE = push("iotaE", [128, H], F32)
    gwf = push("gwf", [128, 8 * H], F32)
    dtpack = push("dtpack", [64, 1], F32)
    acol, acola = apack[0:32, :], apack[32:64, :]
    dtb_s, dtba_s = dtpack[0:32, :], dtpack[32:64, :]
    erow = push("erow", [H, T], BF16)
    lam64 = push("lam64", [64, NTC * H], F32)
    lam64A = push("lam64A", [64, NTC * H], F32)
    dtvT = push("dtvT", [128, NTC * H], F32)
    dtvTA = push("dtvTA", [128, NTC * H], F32)
    w2T = push("w2T", [128, NTC * H], F32)
    w2TA = push("w2TA", [128, NTC * H], F32)
    bcm = push("bcm", [128, T], BF16)
    bcma = push("bcma", [128, T], BF16)
    h0 = push("h0", [NS, E], F32)
    h1 = push("h1", [NS, E], F32)
    HB = [h0, h1]

    blk = nc.Block().__enter__()
    TT, VV, SS, GG, SY = blk.tensor, blk.vector, blk.scalar, blk.gpsimd, blk.sync

    def consts_g(g):
        for nm, dst in [("cw", cw_s), ("cb", cb_s), ("cwa", cwa_s), ("cba", cba_s),
                        ("dexp", dexp_s), ("nwp", nwp_s), ("dtb", dtb_s),
                        ("dtba", dtba_s), ("ebias", ebrow), ("alog", acol),
                        ("aloga", acola)]:
            S.inc(g.dma_start(dst[:, :], d[nm]), "GPD", 16)
        S.inc(g.dma_start(gwf[:, :].rearrange("p (k h) -> p k h", k=8),
                          d["gwT"].rearrange("(k p) h -> p k h", p=128)), "GPD", 16)
        g.memset(ones1b[:, :], 1.0)
        g.memset(ones1f[:, :], 1.0)
        g.memset(onescol[:, :], 1.0)
        g.memset(c_strict[:, :], 1.0)
        g.memset(c_umask[:, :], 1.0)
        g.memset(c_idf[:, :], 1.0)
        g.memset(identb[:, :], 1.0)
        g.memset(combTb[32:36, :], 1.0)
        g.drain()
        g.affine_select(c_strict[:, :], c_strict[:, :], [[-1, 128]], ALU.is_ge, 0.0,
                        base=-1, channel_multiplier=1)
        g.affine_select(c_umask[:, :], c_umask[:, :], [[1, 128]], ALU.is_ge, 0.0,
                        channel_multiplier=-1)
        g.affine_select(c_idf[:, :], c_idf[:, :], [[-1, 128]], ALU.is_equal, 0.0,
                        channel_multiplier=1)
        g.drain()
        g.affine_select(identb[:, :], identb[:, :], [[-1, 128]], ALU.is_equal, 0.0,
                        channel_multiplier=1)
        g.memset(cvals[:, 0:1], EPS)
        g.memset(cvals[:, 1:2], 1.0)
        g.memset(c_ind[:, :], 1.0)
        g.drain()
        g.affine_select(c_ind[:, :], c_ind[:, :], [[1, 2048]], ALU.is_ge, 0.0,
                        channel_multiplier=-64)
        g.drain()
        g.affine_select(c_ind[:, :], c_ind[:, :], [[-1, 2048]], ALU.is_ge, 0.0,
                        base=63, channel_multiplier=64)
        g.iota(iotaE[:, :], [[1, H]], base=0, channel_multiplier=0,
               allow_small_or_imprecise_dtypes=True)
        S.inc(g.memset(h0[:, :], 0.0), "GC")
    GG(consts_g)

    def consts_v(v):
        S.wait(v, "GC")
        S.wait(v, "GPD")
        v.tensor_copy(gwh[:, :], gwf[:, :])
        v.drain()
        S.inc(v.tensor_tensor(gwl[:, :], gwf[:, :], gwh[:, :], ALU.subtract), "VC")
    VV(consts_v)

    def consts_s(sc):
        S.wait(sc, "GPD")
        sc.activation(acol[:, :], acol[:, :], ACT.Exp)
        S.inc(sc.activation(acola[:, :], acola[:, :], ACT.Exp), "SC")
    SS(consts_s)

    def consts_v2(v):
        S.wait(v, "SC")
        v.tensor_scalar_mul(acol[:, :], acol[:, :], -1.0)
        S.inc(v.tensor_scalar_mul(acola[:, :], acola[:, :], -1.0), "VC")
    VV(consts_v2)

    def eb_t(t):
        S.wait(t, "VC")
        S.inc(t.matmul(psA[:, 0:H], ones1f[0:1, :], ebrow[0:1, :], start=True,
                       stop=True), "TC")
    TT(eb_t)

    def eb_v(v):
        S.wait(v, "TC")
        S.inc(v.scalar_tensor_tensor(c_eb[:, :], iotaE[:, :], -EPSI, psA[:, 0:H],
                                     ALU.mult, ALU.add), "VC")
    VV(eb_v)

    def fence():
        for body in [TT, VV, SS, GG, SY]:
            def f(e):
                for nm in ["TC", "VC", "SC", "GC", "SYD", "GPD"]:
                    S.wait(e, nm)
            body(f)

    def ln_chunk(src_ap, dst_ap):
        def v1(v):
            S.wait(v, "SC")
            v.reduce_sum(stat[:, 1:2], src_ap, axis=AXX)
            v.drain()
            S.inc(v.tensor_scalar_mul(stat[:, 2:3], stat[:, 1:2], 1.0 / D_), "VC")
        VV(v1)

        def s1(sc):
            S.wait(sc, "VC")
            S.inc(sc.activation(sqscr[:, :], src_ap, ACT.Square,
                                accum_out=stat[:, 0:1]), "SC")
        SS(s1)

        def v2(v):
            S.wait(v, "SC")
            v.tensor_scalar_mul(stat[:, 3:4], stat[:, 0:1], 1.0 / D_)
            v.drain()
            v.tensor_tensor(stat[:, 4:5], stat[:, 2:3], stat[:, 2:3], ALU.mult)
            v.drain()
            S.inc(v.tensor_tensor(stat[:, 4:5], stat[:, 3:4], stat[:, 4:5],
                                  ALU.subtract), "VC")
        VV(v2)

        def s2(sc):
            S.wait(sc, "VC")
            sc.activation(stat[:, 5:6], stat[:, 4:5], ACT.Ln, bias=cvals[:, 0:1])
            S.inc(sc.activation(stat[:, 6:7], stat[:, 5:6], ACT.Exp, scale=-0.5),
                  "SC")
        SS(s2)

        def v3(v):
            S.wait(v, "SC")
            S.inc(v.tensor_scalar(dst_ap, src_ap, stat[:, 2:3], stat[:, 6:7],
                                  ALU.subtract, ALU.mult), "VC")
        VV(v3)

    m_a = mark()
    ufm = push("ufm", [128, 8 * T], BF16, is_transpose_dest=True)
    wins = push("wins", [128, 3 * 8 * 128], BF16)
    gbB = push("gbB", [128, 4 * D_], F32)
    gbrow = push("gbrow", [1, D_], F32)
    xtile = push("xtile", [128, D_], F32)
    utile = push("utile", [128, D_], F32)
    ubtile = push("ubtile", [128, D_], BF16)
    fbuf0 = push("fbuf0", [128, T + 3], BF16)
    fbuf1 = push("fbuf1", [128, T + 3], BF16)
    cacc0 = push("cacc0", [128, T], BF16)
    ubz0 = push("ubz0", [128, T], BF16)
    ubz1 = push("ubz1", [128, T], BF16)
    if True:

        for i in range(4):
            def gb_load(g, i=i):
                S.wait(g, "TC")
                S.inc(g.dma_start(gbrow[:, :], d["lnrows"][0:1, i * D_:(i + 1) * D_]),
                      "GPD", 16)
            GG(gb_load)
            gpd_r = S.n["GPD"]

            def bc_t(t, i=i, gpd_r=gpd_r):
                S.wait(t, "GPD", gpd_r)
                S.wait(t, "VC")
                t.matmul(psA[:, 0:512], ones1f[0:1, :],
                         gbrow[0:1, 0:512], start=True, stop=True)
                S.inc(t.matmul(psA[:, 512:1024], ones1f[0:1, :],
                               gbrow[0:1, 512:1024], start=True,
                               stop=True), "TC")
            TT(bc_t)

            def bc_v(v, i=i):
                S.wait(v, "TC")
                S.inc(v.tensor_copy(gbB[:, i * D_:(i + 1) * D_], psA[:, 0:1024]),
                      "VC")
            VV(bc_v)

        def phase_ln1(src_dram, goff):
            for tc in range(NTC):
                def ld(y, tc=tc):
                    S.wait(y, "VC")
                    S.wait(y, "SC")
                    S.inc(y.dma_start(xtile[:, :],
                                      src_dram[tc * 128:(tc + 1) * 128, :]),
                          "SYD", 16)
                SY(ld)

                ln_chunk(xtile[:, :], utile[:, :])

                def v3(v, goff=goff):
                    S.wait(v, "SYD")
                    v.tensor_tensor(utile[:, :], utile[:, :],
                                    gbB[:, goff * D_:(goff + 1) * D_], ALU.mult)
                    S.inc(v.tensor_tensor(ubtile[:, :], utile[:, :],
                                          gbB[:, (goff + 1) * D_:(goff + 2) * D_],
                                          ALU.add), "VC")
                VV(v3)

                def tr(y, tc=tc):
                    S.wait(y, "VC")
                    for kd in range(8):
                        S.inc(y.dma_start(
                            ufm[:, kd * T + tc * 128: kd * T + (tc + 1) * 128],
                            ubtile[:, kd * 128:(kd + 1) * 128], transpose=True),
                            "SYD", 16)
                SY(tr)

        def phase_ln1_fixed(src_dram, goff):
            for tc in range(NTC):
                def ld(y, tc=tc):
                    S.wait(y, "VC")
                    S.wait(y, "SC")
                    S.inc(y.dma_start(xtile[:, :],
                                      src_dram[tc * 128:(tc + 1) * 128, :]),
                          "SYD", 16)
                SY(ld)
                syd_x = S.n["SYD"]

                def s1(sc, syd_x=syd_x):
                    S.wait(sc, "SYD", syd_x)
                    S.inc(sc.activation(sqscr[:, :], xtile[:, :], ACT.Square,
                                        accum_out=stat[:, 0:1]), "SC")
                SS(s1)

                def v1(v, syd_x=syd_x):
                    S.wait(v, "SYD", syd_x)
                    v.reduce_sum(stat[:, 1:2], xtile[:, :], axis=AXX)
                    v.drain()
                    S.wait(v, "SC")
                    v.tensor_scalar_mul(stat[:, 2:3], stat[:, 1:2], 1.0 / D_)
                    v.tensor_scalar_mul(stat[:, 3:4], stat[:, 0:1], 1.0 / D_)
                    v.drain()
                    v.tensor_tensor(stat[:, 4:5], stat[:, 2:3], stat[:, 2:3], ALU.mult)
                    v.drain()
                    S.inc(v.tensor_tensor(stat[:, 4:5], stat[:, 3:4], stat[:, 4:5],
                                          ALU.subtract), "VC")
                VV(v1)

                def s2(sc):
                    S.wait(sc, "VC")
                    sc.activation(stat[:, 5:6], stat[:, 4:5], ACT.Ln, bias=cvals[:, 0:1])
                    S.inc(sc.activation(stat[:, 6:7], stat[:, 5:6], ACT.Exp,
                                        scale=-0.5), "SC")
                SS(s2)

                def v2(v, goff=goff):
                    S.wait(v, "SC")
                    S.wait(v, "SYD")
                    v.scalar_tensor_tensor(utile[:, :], xtile[:, :], stat[:, 2:3],
                                           gbB[:, goff * D_:(goff + 1) * D_],
                                           ALU.subtract, ALU.mult)
                    v.drain()
                    S.inc(v.scalar_tensor_tensor(
                        ubtile[:, :], utile[:, :], stat[:, 6:7],
                        gbB[:, (goff + 1) * D_:(goff + 2) * D_],
                        ALU.mult, ALU.add), "VC")
                VV(v2)

                def tr(y, tc=tc):
                    S.wait(y, "VC")
                    for kd in range(8):
                        S.inc(y.dma_start(
                            ufm[:, kd * T + tc * 128: kd * T + (tc + 1) * 128],
                            ubtile[:, kd * 128:(kd + 1) * 128], transpose=True),
                            "SYD", 16)
                SY(tr)

        def phase_a2(winT_dram, necols, spill, bc_dst, dtraw_dst, zch, cw_t, cb_t):
            dt_bp = dtraw_dst.base_partition()
            nec_all = necols // 128
            rem = necols - nec_all * 128
            n_ec = nec_all + (1 if rem else 0)
            src3 = winT_dram.rearrange("(k p) e -> p k e", p=128)

            def zt(v):
                v.memset(fbuf0[:, 0:3], 0.0)
                S.inc(v.memset(fbuf1[:, 0:3], 0.0), "VC")
            VV(zt)
            ufm_done = S.n["SYD"]

            wslab = [wins[:, s * 1024:(s + 1) * 1024]
                         .rearrange("p (k e) -> p k e", e=128) for s in range(3)]
            slab_gpd = {}

            def wl0(g):
                for s in range(min(3, n_ec)):
                    rows = 128 if s < nec_all else rem
                    S.inc(g.dma_start(wslab[s][:, :, 0:rows],
                                      src3[:, :, s * 128:s * 128 + rows]),
                          "GPD", 16)
                    slab_gpd[s] = S.n["GPD"]
            GG(wl0)

            last_evac = {psl: ("VC", S.n["VC"]) for psl in range(4)}
            fb_free = [None, None]
            cc_free = [None]
            ub_free = [None, None]

            for ec in range(n_ec):
                mm_rows = 128 if ec < nec_all else rem
                slot = ec % 3
                gpd_here = slab_gpd[ec]
                par = ec % 2
                fb = [fbuf0, fbuf1][par]
                cc = cacc0
                ub = [ubz0, ubz1][par]
                evac_cnts = []

                for ti in range(NTILE):
                    psl = ti % 4
                    if ec < nec_all:
                        sl = psA[0:mm_rows, psl * 512:psl * 512 + TILE]
                    else:
                        sl = psA[dt_bp:dt_bp + mm_rows, psl * 512:psl * 512 + TILE]

                    def mm(t, ec=ec, ti=ti, sl=sl, mm_rows=mm_rows, slot=slot,
                           gpd_here=gpd_here, ufm_done=ufm_done, psl=psl):
                        S.wait(t, "GPD", gpd_here)
                        S.wait(t, "SYD", ufm_done)
                        if psl in last_evac and last_evac[psl] is not None:
                            S.wait(t, last_evac[psl][0], last_evac[psl][1])
                        ins = None
                        for kd in range(8):
                            ins = t.matmul(
                                sl, wslab[slot][:, kd, 0:mm_rows],
                                ufm[:, kd * T + ti * TILE: kd * T + (ti + 1) * TILE],
                                start=(kd == 0), stop=(kd == 7))
                        S.inc(ins, "TC")
                    TT(mm)
                    tc_here = S.n["TC"]
                    if ti == NTILE - 1 and ec + 3 < n_ec:
                        def wln(g, ec=ec, slot=slot, tc_here=tc_here):
                            S.wait(g, "TC", tc_here)
                            rows = 128 if ec + 3 < nec_all else rem
                            S.inc(g.dma_start(
                                wslab[slot][:, :, 0:rows],
                                src3[:, :, (ec + 3) * 128:(ec + 3) * 128 + rows]),
                                "GPD", 16)
                        GG(wln)
                        slab_gpd[ec + 3] = S.n["GPD"]

                    if ec < nec_all:
                        def ev(v, ti=ti, sl=sl, tc_here=tc_here, fb=fb, par=par):
                            S.wait(v, "TC", tc_here)
                            if ti == 0 and fb_free[par] is not None:
                                S.wait(v, fb_free[par][0], fb_free[par][1])
                            S.inc(v.tensor_copy(
                                fb[:, 3 + ti * TILE: 3 + (ti + 1) * TILE], sl),
                                "VC")
                        VV(ev)
                    else:
                        def ev(v, ti=ti, sl=sl, tc_here=tc_here,
                               dtraw_dst=dtraw_dst):
                            S.wait(v, "TC", tc_here)
                            S.inc(v.tensor_copy(
                                dtraw_dst[:, ti * TILE:(ti + 1) * TILE], sl), "VC")
                        VV(ev)
                    last_evac[psl] = ("VC", S.n["VC"])

                if ec >= nec_all:
                    continue

                if ec < zch:
                    def zsil(sc, fb=fb, ub=ub, par=par):
                        S.wait(sc, "VC")
                        if ub_free[par] is not None:
                            S.wait(sc, ub_free[par][0], ub_free[par][1])
                        S.inc(sc.activation(ub[:, :], fb[:, 3:3 + T], ACT.Silu),
                              "SC")
                    SS(zsil)
                    fb_free[par] = ("SC", S.n["SC"])

                    def zsp(y, ec=ec, ub=ub):
                        S.wait(y, "SC")
                        S.inc(y.dma_start(sz_sp[ec, :, :], ub[:, :]), "SYD", 16)
                    SY(zsp)
                    ub_free[par] = ("SYD", S.n["SYD"])
                else:
                    xc = ec - zch
                    def cv(v, fb=fb, cc=cc, xc=xc, par=par, cw_t=cw_t):
                        if cc_free[0] is not None:
                            S.wait(v, cc_free[0][0], cc_free[0][1])
                        v.tensor_scalar(cc[:, :], fb[:, 0:T],
                                        cw_t[:, xc * 4:xc * 4 + 1], None, ALU.mult)
                        ins = None
                        for j in range(1, 4):
                            v.drain()
                            ins = v.scalar_tensor_tensor(
                                cc[:, :], fb[:, j:j + T],
                                cw_t[:, xc * 4 + j:xc * 4 + j + 1], cc[:, :],
                                ALU.mult, ALU.add)
                        S.inc(ins, "VC")
                    VV(cv)
                    fb_free[par] = ("VC", S.n["VC"])

                    def csil(sc, cc=cc, ub=ub, par=par, xc=xc, cb_t=cb_t):
                        S.wait(sc, "VC")
                        if ub_free[par] is not None:
                            S.wait(sc, ub_free[par][0], ub_free[par][1])
                        S.inc(sc.activation(ub[:, :], cc[:, :], ACT.Silu,
                                            bias=cb_t[:, xc:xc + 1]), "SC")
                    SS(csil)
                    cc_free[0] = ("SC", S.n["SC"])

                    if xc < 16:
                        def xsp(y, xc=xc, ub=ub, spill=spill):
                            S.wait(y, "SC")
                            S.inc(y.dma_start(spill[xc, :, :], ub[:, :]),
                                  "SYD", 16)
                        SY(xsp)
                        ub_free[par] = ("SYD", S.n["SYD"])
                    else:
                        def bcp(v, ub=ub, bc_dst=bc_dst):
                            S.wait(v, "SC")
                            S.inc(v.tensor_copy(bc_dst[:, :], ub[:, :]), "VC")
                        VV(bcp)
                        ub_free[par] = ("VC", S.n["VC"])

        def phase_a3(dtr, dtb_t, a_t, dv, ac, w2_, l64, dvT, w2T_, er, scr):
            bp3 = dtr.base_partition()

            def sp1(sc):
                S.wait(sc, "VC")
                sc.activation(scr[:, :], dtr[:, :], ACT.Abs, bias=dtb_t[:, 0:1])
                sc.activation(scr[:, :], scr[:, :], ACT.Exp, scale=-1.0)
                S.inc(sc.activation(scr[:, :], scr[:, :], ACT.Ln,
                                    bias=cvals[bp3:bp3 + 32, 1:2]), "SC")
            SS(sp1)

            def sp2(v):
                S.wait(v, "SC")
                v.tensor_scalar(dv[:, :], dtr[:, :], dtb_t[:, 0:1], 0.0,
                                ALU.add, ALU.max)
                v.drain()
                v.tensor_tensor(dv[:, :], dv[:, :], scr[:, :], ALU.add)
                v.drain()
                ins = None
                v.tensor_scalar(scr[:, :], dv[:, :], a_t[:, 0:1], None, ALU.mult)
                v.drain()
                for c in range(NTC):
                    ins = v.tensor_tensor_scan(ac[:, c * Q:(c + 1) * Q],
                                               scr[:, c * Q:(c + 1) * Q],
                                               scr[:, c * Q:(c + 1) * Q], 0.0,
                                               ALU.add, ALU.bypass)
                S.inc(ins, "VC")
            VV(sp2)

            if er is not None:
                def sp4(sc):
                    S.wait(sc, "VC")
                    S.inc(sc.activation(er[:, :], ac[:, :], ACT.Exp), "SC")
                SS(sp4)

            for c in range(NTC):
                def trp(t, c=c):
                    S.wait(t, "VC")
                    S.wait(t, "SC")
                    bp = ac.base_partition()
                    t.transpose(psB[0:1, 64:96], ac[:, (c + 1) * Q - 1:(c + 1) * Q],
                                c_idf[bp:bp + 32, bp:bp + 32])
                    bpd = dv.base_partition()
                    S.inc(t.transpose(psB[:, 32:64], dv[:, c * Q:(c + 1) * Q],
                                      c_idf[bpd:bpd + 32, bpd:bpd + 32]), "TC")
                TT(trp)

                def tev(v, c=c, dvT=dvT):
                    S.wait(v, "TC")
                    v.tensor_copy(dvT[:, c * H:(c + 1) * H], psB[:, 32:64])
                    S.inc(v.tensor_copy(rmsrow[0:1, 0:32], psB[0:1, 64:96]), "VC")
                VV(tev)

                def lse(sc, c=c):
                    S.wait(sc, "VC")
                    S.inc(sc.activation(rmsrow[0:1, 32:64], rmsrow[0:1, 0:32],
                                        ACT.Exp), "SC")
                SS(lse)

                def lmm(t, c=c):
                    S.wait(t, "SC")
                    S.inc(t.matmul(psB[0:64, 64:96], ones1f[0:1, 0:64],
                                   rmsrow[0:1, 32:64], start=True, stop=True), "TC")
                TT(lmm)

                def lev(v, c=c, l64=l64):
                    S.wait(v, "TC")
                    S.inc(v.tensor_copy(l64[:, c * H:(c + 1) * H], psB[0:64, 64:96]),
                          "VC")
                VV(lev)

                def w2v(v, c=c, ac=ac):
                    S.inc(v.tensor_scalar(scr[:, c * Q:(c + 1) * Q],
                                          ac[:, c * Q:(c + 1) * Q],
                                          ac[:, (c + 1) * Q - 1:(c + 1) * Q], None,
                                          ALU.subtract), "VC")
                VV(w2v)

                def w2e(sc, c=c):
                    S.wait(sc, "VC")
                    S.inc(sc.activation(scr[:, c * Q:(c + 1) * Q],
                                        scr[:, c * Q:(c + 1) * Q], ACT.Exp,
                                        scale=-1.0), "SC")
                SS(w2e)

                def w2f(v, c=c, dv=dv, w2_=w2_):
                    S.wait(v, "SC")
                    S.inc(v.tensor_tensor(w2_[:, c * Q:(c + 1) * Q],
                                          dv[:, c * Q:(c + 1) * Q],
                                          scr[:, c * Q:(c + 1) * Q], ALU.mult),
                          "VC")
                VV(w2f)

                def w2t(t, c=c, w2_=w2_):
                    S.wait(t, "VC")
                    bpw = w2_.base_partition()
                    S.inc(t.transpose(psB[:, 96:128], w2_[:, c * Q:(c + 1) * Q],
                                      c_idf[bpw:bpw + 32, bpw:bpw + 32]), "TC")
                TT(w2t)

                def w2tv(v, c=c, w2T_=w2T_):
                    S.wait(v, "TC")
                    S.inc(v.tensor_copy(w2T_[:, c * H:(c + 1) * H], psB[:, 96:128]),
                          "VC")
                VV(w2tv)

        phase_ln1_fixed(d["xaux"], 2)
        phase_a2(d["winTa"], CD + H, xa_sp, bcma, dtrawA, 0, cwa_s, cba_s)
        fence()
        phase_ln1_fixed(d["xin"], 0)
        phase_a2(d["winT"], DIP, x_sp, bcm, dtraw, 16, cw_s, cb_s)
        fence()
        phase_a3(dtrawA, dtba_s, acola, dtvA, acincA, w2a, lam64A, dtvTA, w2TA,
                 None, scr32A)
        phase_a3(dtraw, dtb_s, acol, dtv, acinc, w2m, lam64, dtvT, w2T, erow, scr32)
        fence()
        if "u" in dbg:
            def du(g):
                S.inc(g.dma_start(dbg["u"], ufm[:, :]), "ST", 16)
            GG(du)
        if "dtv" in dbg:
            def dd(y):
                S.inc(y.dma_start(dbg["dtv"], dtv[:, :]), "ST", 16)
            SY(dd)
        if "acinc" in dbg:
            def da(y):
                S.inc(y.dma_start(dbg["acinc"], acinc[:, :]), "ST", 16)
            SY(da)

    pop_to(m_a)
    m_bc = mark()
    wout2 = push("wout2", [128, 16 * D_], BF16)
    xt0 = push("xt0", [128, E], BF16, is_transpose_dest=True)
    xt1 = push("xt1", [128, E], BF16, is_transpose_dest=True)
    wq = push("wq", [128, 2 * E], BF16)
    e2sb = push("e2sb", [128, E], BF16)
    gpmb = push("gpmb", [128, 128], BF16)
    zal = push("zal", [128, E], BF16)
    btp = push("btp", [128, 64], BF16)
    ybuf = push("ybuf", [128, E], F32)
    gbuf = push("gbuf", [128, E], F32)
    g2b = push("g2b", [128, E], BF16)
    xck = push("xck", [128, E], BF16)
    szt = push("szt", [128, E], BF16)
    ynb = push("ynb", [128, E], BF16)
    rib = push("rib", [128, 128], F32)
    dab = push("dab", [128, 32], F32)
    vbuf = push("vbuf", [128, D_], F32)
    xres = push("xres", [128, D_], F32)
    x1t = push("x1t", [128, D_], F32)
    cmt = push("cmt", [64, Q], BF16)
    if True:

        XT = [xt0, xt1]

        def w2l(g):
            S.inc(g.dma_start(wout2[:, :].rearrange("p (k dd) -> p k dd", k=16),
                              d["woutT"].rearrange("(k p) dd -> p k dd", p=128)),
                  "GPD", 16)
        GG(w2l)
        gpd_wout = S.n["GPD"]

        def w2sc(v):
            S.wait(v, "GPD", gpd_wout)
            ins = None
            for ec in range(16):
                ins = v.tensor_scalar(wout2[:, ec * D_:(ec + 1) * D_],
                                      wout2[:, ec * D_:(ec + 1) * D_],
                                      nwp_s[:, ec:ec + 1], None, ALU.mult)
            S.inc(ins, "VC")
        VV(w2sc)

        for c in range(NTC):
            par = c % 2
            def xtl(y, c=c, par=par):
                S.wait(y, "VC")
                for ec in range(16):
                    S.inc(y.dma_start(XT[par][:, ec * 128:(ec + 1) * 128],
                                      xa_sp[ec, :, c * Q:(c + 1) * Q],
                                      transpose=True), "SYD", 16)
            SY(xtl)
            syd_xt = S.n["SYD"]

            def btr(t, c=c):
                S.wait(t, "VC")
                S.inc(t.transpose(psB[:, 0:32].bitcast(BF16),
                                  bcma[0:64, c * Q:(c + 1) * Q],
                                  identb[0:64, 0:64]), "TC")
            TT(btr)

            def bev(v, c=c):
                S.wait(v, "TC")
                S.inc(v.tensor_copy(btp[:, :], psB[:, 0:32].bitcast(BF16)), "VC")
            VV(bev)

            def zb(v, c=c, par=par, syd_xt=syd_xt):
                S.wait(v, "SYD", syd_xt)
                S.inc(v.tensor_tensor(
                    zal[:, :].rearrange("p (h q) -> p h q", q=P),
                    XT[par][:, :].rearrange("p (h q) -> p h q", q=P),
                    w2TA[:, c * H:(c + 1) * H].unsqueeze(2)
                        .broadcast_to([128, H, P]), ALU.mult), "VC")
            VV(zb)

            def scm(t, c=c):
                S.wait(t, "VC")
                ins = None
                for i in range(4):
                    ins = t.matmul(psA[0:64, i * 512:(i + 1) * 512], btp[:, :],
                                   zal[:, i * 512:(i + 1) * 512], start=True,
                                   stop=True)
                S.inc(ins, "TC")
            TT(scm)

            def hup(v, c=c):
                S.wait(v, "TC")
                src, dst = HB[c % 2], HB[1 - c % 2]
                v.tensor_tensor(dst[:, :].rearrange("n (h q) -> n h q", q=P),
                                src[:, :].rearrange("n (h q) -> n h q", q=P),
                                lam64A[:, c * H:(c + 1) * H].unsqueeze(2)
                                    .broadcast_to([64, H, P]), ALU.mult)
                v.drain()
                S.inc(v.tensor_tensor(dst[:, :], dst[:, :], psA[0:64, :], ALU.add),
                      "VC")
            VV(hup)
        fence()
        if "Hinit" in dbg:
            def dh(y):
                S.inc(y.dma_start(dbg["Hinit"], HB[0][:, :]), "ST", 16)
            SY(dh)

        vc_hup = {}
        vc_resv = {}
        vc_yc = {}
        vc_bev = {}
        tc_ysm_map = {}
        sc_a3 = S.n["SC"]
        for c in range(NTC):
            par = c % 2
            hsrc, hdst = HB[c % 2], HB[1 - c % 2]

            def xtl(y, c=c, par=par):
                if c - 1 in vc_resv:
                    S.wait(y, "VC", vc_resv[c - 1])
                if c - 1 in tc_ysm_map:
                    S.wait(y, "TC", tc_ysm_map[c - 1])
                for ec in range(8):
                    S.inc(y.dma_start(XT[par][:, ec * 128:(ec + 1) * 128],
                                      x_sp[ec, :, c * Q:(c + 1) * Q],
                                      transpose=True), "SYD", 16)
                S.inc(y.dma_start(xck[:, :].rearrange("p (ec q) -> p ec q", q=Q),
                                  x_sp[:, :, c * Q:(c + 1) * Q].transpose([1, 0, 2])),
                      "SYD", 16)
                S.inc(y.dma_start(szt[:, :].rearrange("p (ec q) -> p ec q", q=Q),
                                  sz_sp[:, :, c * Q:(c + 1) * Q].transpose([1, 0, 2])),
                      "SYD", 16)
                S.inc(y.dma_start(xres[:, :], d["xin"][c * 128:(c + 1) * 128, :]),
                      "SYD", 16)
                S.inc(y.dma_start(cmt[:, :], bcm[64:128, c * Q:(c + 1) * Q]),
                      "SYD", 16)
            SY(xtl)

            def xtl2(sc, c=c, par=par):
                if c - 1 in vc_resv:
                    S.wait(sc, "VC", vc_resv[c - 1])
                if c - 1 in tc_ysm_map:
                    S.wait(sc, "TC", tc_ysm_map[c - 1])
                for ec in range(8, 16):
                    S.inc(sc.dma_start(XT[par][:, ec * 128:(ec + 1) * 128],
                                       x_sp[ec, :, c * Q:(c + 1) * Q],
                                       transpose=True), "SYD", 16)
            SS(xtl2)
            syd_xt = S.n["SYD"]

            def e2t(t, c=c):
                if c - 1 in vc_bev:
                    S.wait(t, "VC", vc_bev[c - 1])
                S.wait(t, "SC", sc_a3)
                ins = None
                for fc in range(16):
                    ins = t.matmul(psB[:, fc * 128:(fc + 1) * 128],
                                   c_ind[:, fc * 128:(fc + 1) * 128],
                                   erow[:, c * Q:(c + 1) * Q], start=True, stop=True)
                S.inc(ins, "TC")
            TT(e2t)

            def e2v(v, c=c):
                S.wait(v, "TC")
                if c - 1 in vc_yc:
                    S.wait(v, "VC", vc_yc[c - 1])
                S.inc(v.tensor_copy(e2sb[:, :], psB[:, :]), "VC")
            VV(e2v)
            vc_e2 = S.n["VC"]

            def dtar(v, c=c):
                S.inc(v.tensor_scalar(scr32[:, c * Q:(c + 1) * Q],
                                      dtv[:, c * Q:(c + 1) * Q], acol[:, 0:1], None,
                                      ALU.mult), "VC")
            VV(dtar)

            vc_dtar = S.n["VC"]

            def dtat(t, c=c, vc_dtar=vc_dtar):
                S.wait(t, "VC", vc_dtar)
                if c - 1 in vc_hup:
                    S.wait(t, "VC", vc_hup[c - 1])
                S.inc(t.transpose(psA[:, 0:32], scr32[:, c * Q:(c + 1) * Q],
                                  c_idf[0:32, 0:32]), "TC")
            TT(dtat)

            def dtae(v, c=c):
                S.wait(v, "TC")
                S.inc(v.tensor_copy(dab[:, :], psA[:, 0:32]), "VC")
            VV(dtae)

            def rhb(v, c=c):
                S.inc(v.tensor_tensor(
                    wq[:, :].rearrange("p (h q) -> p h q", q=Q),
                    c_umask[:, :].unsqueeze(1).broadcast_to([128, H, Q]),
                    dab[:, :].unsqueeze(2).broadcast_to([128, H, Q]),
                    ALU.mult), "VC")
            VV(rhb)
            vc_rh = S.n["VC"]

            sc_mde = [None]
            for hf in range(4):
                def cdf(t, c=c, hf=hf, vc_rh=vc_rh):
                    S.wait(t, "VC", vc_rh)
                    if hf >= 1:
                        S.wait(t, "SC", sc_mde[0])
                    elif c - 1 in vc_hup:
                        S.wait(t, "VC", vc_hup[c - 1])
                    ins = None
                    for hh in range(8):
                        h = hf * 8 + hh
                        ins = t.matmul(psA[:, hh * 128:(hh + 1) * 128],
                                       c_strict[:, :], wq[:, h * 128:(h + 1) * 128],
                                       start=True, stop=True)
                    S.inc(ins, "TC")
                TT(cdf)

                def mde(sc, c=c, hf=hf):
                    S.wait(sc, "TC")
                    S.inc(sc.activation(wq[:, hf * 1024:(hf + 1) * 1024],
                                        psA[:, 0:1024], ACT.Exp), "SC")
                SS(mde)
                sc_mde[0] = S.n["SC"]

            def gpm(t, c=c, vc_e2=vc_e2, syd_xt=syd_xt):
                S.wait(t, "VC", vc_e2)
                S.wait(t, "SYD", syd_xt)
                S.inc(t.matmul(psB[:, 0:128], bcm[0:64, c * Q:(c + 1) * Q],
                               cmt[:, :], start=True, stop=True), "TC")
            TT(gpm)

            def gpe(v, c=c):
                S.wait(v, "TC")
                S.inc(v.tensor_tensor(gpmb[:, :], psB[:, 0:128], c_umask[:, :],
                                      ALU.mult), "VC")
            VV(gpe)

            def wpb(v, c=c):
                S.wait(v, "SC", sc_mde[0])
                ins = None
                for hf in range(2):
                    ins = v.tensor_tensor(
                        wq[:, hf * E:(hf + 1) * E].rearrange("p (h q) -> p h q", q=Q),
                        wq[:, hf * E:(hf + 1) * E].rearrange("p (h q) -> p h q", q=Q),
                        gpmb[:, :].unsqueeze(1).broadcast_to([128, 16, Q]),
                        ALU.mult)
                S.inc(ins, "VC")
            VV(wpb)

            def xdt(v, c=c, par=par, syd_xt=syd_xt):
                S.wait(v, "SYD", syd_xt)
                S.inc(v.tensor_tensor(
                    zal[:, :].rearrange("p (h q) -> p h q", q=P),
                    XT[par][:, :].rearrange("p (h q) -> p h q", q=P),
                    dtvT[:, c * H:(c + 1) * H].unsqueeze(2)
                        .broadcast_to([128, H, P]), ALU.mult), "VC")
            VV(xdt)

            def hcast(v, c=c, hsrc=hsrc):
                S.inc(v.tensor_copy(g2b[0:64, :], hsrc[:, :]), "VC")
            VV(hcast)
            vc_xdt = S.n["VC"]

            def yim(t, c=c, vc_xdt=vc_xdt):
                S.wait(t, "VC", vc_xdt)
                ins = None
                for h in range(H):
                    fc, odd = h // 2, h % 2
                    ins = t.matmul(psB[odd * 64:odd * 64 + 64,
                                       fc * 128:(fc + 1) * 128],
                                   zal[:, h * P:(h + 1) * P],
                                   wq[:, h * 128:(h + 1) * 128],
                                   start=True, stop=True)
                S.inc(ins, "TC")
            TT(yim)

            def ysm(t, c=c):
                S.wait(t, "SC", sc_mde[0])
                if c - 1 in vc_resv:
                    S.wait(t, "VC", vc_resv[c - 1])
                ins = None
                for fc in range(16):
                    ins = t.matmul(psA[:, fc * 128:(fc + 1) * 128],
                                   g2b[0:64, fc * 128:(fc + 1) * 128],
                                   cmt[:, :], start=True, stop=True)
                S.inc(ins, "TC")
            TT(ysm)
            tc_y = S.n["TC"]
            tc_ysm_map[c] = tc_y

            def yc(v, c=c, tc_y=tc_y, syd_xt=syd_xt):
                S.wait(v, "TC", tc_y)
                S.wait(v, "SYD", syd_xt)
                v.tensor_tensor(ybuf[:, :], psA[:, :], e2sb[:, :], ALU.mult)
                v.drain()
                v.tensor_tensor(ybuf[:, :], ybuf[:, :], psB[:, :], ALU.add)
                v.tensor_tensor(gbuf[:, :].rearrange("p (fc q) -> p fc q", q=Q),
                                xck[:, :].rearrange("p (fc q) -> p fc q", q=Q),
                                dexp_s[:, :].unsqueeze(2)
                                    .broadcast_to([128, 16, Q]), ALU.mult)
                v.drain()
                S.inc(v.tensor_tensor(ybuf[:, :], ybuf[:, :], gbuf[:, :], ALU.add),
                      "VC")
            VV(yc)
            vc_y = S.n["VC"]
            vc_yc[c] = vc_y
            vc_yc[c] = vc_y
            if c == 0 and "y0" in dbg:
                def dy(y, vc_y=vc_y):
                    S.wait(y, "VC", vc_y)
                    S.inc(y.dma_start(dbg["y0"], ybuf[:, :]), "ST", 16)
                SY(dy)

            def zb2(v, c=c, par=par):
                S.inc(v.tensor_tensor(
                    zal[:, :].rearrange("p (h q) -> p h q", q=P),
                    XT[par][:, :].rearrange("p (h q) -> p h q", q=P),
                    w2T[:, c * H:(c + 1) * H].unsqueeze(2)
                        .broadcast_to([128, H, P]), ALU.mult), "VC")
            VV(zb2)

            def btr2(t, c=c):
                S.wait(t, "VC")
                S.inc(t.transpose(psB[:, 0:32].bitcast(BF16),
                                  bcm[0:64, c * Q:(c + 1) * Q],
                                  identb[0:64, 0:64]), "TC")
            TT(btr2)

            def bev2(v, c=c):
                S.wait(v, "TC")
                S.inc(v.tensor_copy(btp[:, :], psB[:, 0:32].bitcast(BF16)), "VC")
            VV(bev2)
            vc_bev[c] = S.n["VC"]

            def scm2(t, c=c):
                S.wait(t, "VC")
                if c - 1 in vc_resv:
                    S.wait(t, "VC", vc_resv[c - 1])
                ins = None
                for i in range(4):
                    ins = t.matmul(psA[0:64, i * 512:(i + 1) * 512], btp[:, :],
                                   zal[:, i * 512:(i + 1) * 512], start=True,
                                   stop=True)
                S.inc(ins, "TC")
            TT(scm2)

            def hup2(v, c=c, hsrc=hsrc, hdst=hdst):
                S.wait(v, "TC")
                v.tensor_tensor(hdst[:, :].rearrange("n (h q) -> n h q", q=P),
                                hsrc[:, :].rearrange("n (h q) -> n h q", q=P),
                                lam64[:, c * H:(c + 1) * H].unsqueeze(2)
                                    .broadcast_to([64, H, P]), ALU.mult)
                v.drain()
                S.inc(v.tensor_tensor(hdst[:, :], hdst[:, :], psA[0:64, :], ALU.add),
                      "VC")
            VV(hup2)
            vc_hup[c] = S.n["VC"]

            def gat(v, c=c):
                v.tensor_tensor(gbuf[:, :], ybuf[:, :], szt[:, :], ALU.mult)
                v.drain()
                S.inc(v.tensor_tensor(g2b[:, :], gbuf[:, :], gbuf[:, :], ALU.mult),
                      "VC")
            VV(gat)

            def rmsm(t, c=c):
                S.wait(t, "VC")
                S.wait(t, "VC", vc_hup[c])
                ins = None
                for fc in range(16):
                    ins = t.matmul(psA[0:1, 1024:1152], onescol[:, 0:1],
                                   g2b[:, fc * 128:(fc + 1) * 128],
                                   start=(fc == 0), stop=(fc == 15))
                S.inc(ins, "TC")
            TT(rmsm)

            def rmss(sc, c=c):
                S.wait(sc, "TC")
                sc.activation(rmsrow[0:1, 0:128], psA[0:1, 1024:1152], ACT.Ln,
                              bias=cvals[0:1, 0:1], scale=1.0 / E)
                S.inc(sc.activation(rmsrow[0:1, 128:256], rmsrow[0:1, 0:128],
                                    ACT.Exp, scale=-0.5), "SC")
            SS(rmss)

            def rim(t, c=c):
                S.wait(t, "SC")
                S.inc(t.matmul(psA[:, 1152:1280], ones1f[0:1, :],
                               rmsrow[0:1, 128:256], start=True, stop=True), "TC")
            TT(rim)

            def rie(v, c=c):
                S.wait(v, "TC")
                S.inc(v.tensor_copy(rib[:, :], psA[:, 1152:1280]), "VC")
            VV(rie)

            def ynbv(v, c=c):
                S.inc(v.tensor_tensor(
                    ynb[:, :].rearrange("p (fc q) -> p fc q", q=Q),
                    gbuf[:, :].rearrange("p (fc q) -> p fc q", q=Q),
                    rib[:, :].unsqueeze(1).broadcast_to([128, 16, Q]),
                    ALU.mult), "VC")
            VV(ynbv)
            vc_ynb = S.n["VC"]

            def wom(t, c=c, vc_ynb=vc_ynb):
                S.wait(t, "VC", vc_ynb)
                S.wait(t, "VC", vc_hup[c])
                ins = None
                for ek in range(16):
                    for hh in range(2):
                        ins = t.matmul(
                            psA[:, 1024 + hh * 512:1024 + (hh + 1) * 512],
                            ynb[:, ek * 128:(ek + 1) * 128],
                            wout2[:, ek * D_ + hh * 512: ek * D_ + (hh + 1) * 512],
                            start=(ek == 0), stop=(ek == 15))
                S.inc(ins, "TC")
            TT(wom)

            def resv(v, c=c):
                S.wait(v, "TC")
                S.wait(v, "SYD")
                S.inc(v.scalar_tensor_tensor(vbuf[:, :], xres[:, :], 2.0,
                                             psA[:, 1024:2048], ALU.mult, ALU.add),
                      "VC")
            VV(resv)
            vc_resv[c] = S.n["VC"]

            ln_chunk(vbuf[:, :], x1t[:, :])

            def x1s(y, c=c):
                S.wait(y, "VC")
                S.inc(y.dma_start(x1_sp[c * 128:(c + 1) * 128, :], x1t[:, :]),
                      "SYD", 16)
            SY(x1s)

        fence()
        if "Hend" in dbg:
            def dh2(y):
                S.inc(y.dma_start(dbg["Hend"], HB[0][:, :]), "ST", 16)
            SY(dh2)

    pop_to(m_bc)
    pop_to(m_ab)
    MACC = push("maccb", [128, NTC * D_], F32)
    x2bf = push("x2bf", [128, 8 * T], BF16, is_transpose_dest=True)
    m_d = mark()
    x2lo = push("x2lo", [128, 8 * T], BF16, is_transpose_dest=True)
    hib = push("hib", [128, D_], BF16)
    lob = push("lob", [128, D_], BF16)
    scoresF = push("scoresF", [32, T], F32)
    if True:
        def x1l(y):
            for tc in range(NTC):
                S.inc(y.dma_start(MACC[:, tc * D_:(tc + 1) * D_],
                                  x1_sp[tc * 128:(tc + 1) * 128, :]), "SYD", 16)
        SY(x1l)
        if "x1" in dbg:
            def dx1(y):
                S.wait(y, "SYD")
                S.inc(y.dma_start(dbg["x1"], MACC[:, :]), "ST", 16)
            SY(dx1)

        syd_tc = {}
        for tc in range(NTC):
            def hilo(v, tc=tc):
                S.wait(v, "SYD")
                v.tensor_copy(hib[:, :], MACC[:, tc * D_:(tc + 1) * D_])
                v.drain()
                S.inc(v.tensor_tensor(lob[:, :], MACC[:, tc * D_:(tc + 1) * D_],
                                      hib[:, :], ALU.subtract), "VC")
            VV(hilo)

            def htr(y, tc=tc):
                S.wait(y, "VC")
                for kd in range(4):
                    S.inc(y.dma_start(
                        x2bf[:, kd * T + tc * 128: kd * T + (tc + 1) * 128],
                        hib[:, kd * 128:(kd + 1) * 128], transpose=True),
                        "SYD", 16)
                    S.inc(y.dma_start(
                        x2lo[:, kd * T + tc * 128: kd * T + (tc + 1) * 128],
                        lob[:, kd * 128:(kd + 1) * 128], transpose=True),
                        "SYD", 16)
            SY(htr)

            def htr2(sc, tc=tc):
                S.wait(sc, "VC")
                for kd in range(4, 8):
                    S.inc(sc.dma_start(
                        x2bf[:, kd * T + tc * 128: kd * T + (tc + 1) * 128],
                        hib[:, kd * 128:(kd + 1) * 128], transpose=True),
                        "SYD", 16)
                    S.inc(sc.dma_start(
                        x2lo[:, kd * T + tc * 128: kd * T + (tc + 1) * 128],
                        lob[:, kd * 128:(kd + 1) * 128], transpose=True),
                        "SYD", 16)
            SS(htr2)
            syd_tc[tc] = S.n["SYD"]
        syd_hilo = S.n["SYD"]

        for tg in range(4):
            def rlm(t, tg=tg):
                S.wait(t, "SYD", syd_tc[4 * tg + 3])
                if tg > 0:
                    S.wait(t, "SC")
                mms = []
                for kd in range(8):
                    for a_, b_ in [(gwh, x2bf), (gwl, x2bf), (gwh, x2lo)]:
                        mms.append((a_, b_, kd))
                ins = None
                for i, (a_, b_, kd) in enumerate(mms):
                    ins = t.matmul(psB[0:32, 0:512],
                                   a_[:, kd * H:(kd + 1) * H],
                                   b_[:, kd * T + tg * 512: kd * T + (tg + 1) * 512],
                                   start=(i == 0), stop=(i == len(mms) - 1))
                S.inc(ins, "TC")
            TT(rlm)

            def sig(sc, tg=tg):
                S.wait(sc, "TC")
                S.inc(sc.activation(scoresF[:, tg * 512:(tg + 1) * 512],
                                    psB[0:32, 0:512], ACT.Sigmoid), "SC")
            SS(sig)

        for tc in range(NTC):
            def str_(t, tc=tc):
                S.wait(t, "SC")
                S.wait(t, "VC")
                S.inc(t.transpose(psA[:, 0:32],
                                  scoresF[:, tc * 128:(tc + 1) * 128],
                                  c_idf[0:32, 0:32]), "TC")
            TT(str_)

            def rmath(v, tc=tc):
                S.wait(v, "TC")
                sco = scorestk[:, tc * H:(tc + 1) * H]
                v.tensor_copy(sco, psA[:, 0:32])
                sc_ = rscr[:, 0:32]
                t2s = rscr[:, 32:40]
                t2b = rscr2[:, 32:40]
                msk = rscr2[:, 0:32]
                v.drain()
                v.tensor_tensor(sc_, sco, c_eb[:, :], ALU.add)
                v.drain()
                pairs = [(0, 1), (0, 2), (0, 3), (1, 2), (1, 3), (2, 3)]
                for k_, (i_, j_) in enumerate(pairs):
                    dstp = t2s if k_ == 0 else t2b
                    v.tensor_tensor(dstp, sc_[:, i_:32:4], sc_[:, j_:32:4],
                                    ALU.add)
                    v.drain()
                    if k_ > 0:
                        v.tensor_tensor(t2s, t2s, t2b, ALU.max)
                        v.drain()
                v.tensor_copy(t2b, t2s)
                v.drain()
                for _ in range(3):
                    v.reduce_max(stat[:, 0:1], t2b, axis=AXX)
                    v.drain()
                    v.tensor_scalar(sqscr[:, 0:8], t2b, stat[:, 0:1], -1e30,
                                    ALU.is_ge, ALU.mult)
                    v.drain()
                    v.tensor_tensor(t2b, t2b, sqscr[:, 0:8], ALU.add)
                    v.drain()
                v.reduce_max(stat[:, 0:1], t2b, axis=AXX)
                v.drain()
                v.tensor_scalar(t2b, t2s, stat[:, 0:1], None, ALU.is_ge)
                v.drain()
                v.tensor_tensor(msk.rearrange("p (g j) -> p g j", j=4),
                                sc_.rearrange("p (g j) -> p g j", j=4),
                                t2b.unsqueeze(2)
                                   .broadcast_to([128, 8, 4]), ALU.mult)
                v.drain()
                v.tensor_copy(sqscr[:, 0:32], msk)
                v.drain()
                for _ in range(5):
                    v.reduce_max(stat[:, 0:1], sqscr[:, 0:32], axis=AXX)
                    v.drain()
                    v.tensor_scalar(sqscr[:, 32:64], sqscr[:, 0:32],
                                    stat[:, 0:1], -1e30, ALU.is_ge, ALU.mult)
                    v.drain()
                    v.tensor_tensor(sqscr[:, 0:32], sqscr[:, 0:32],
                                    sqscr[:, 32:64], ALU.add)
                    v.drain()
                v.reduce_max(stat[:, 0:1], sqscr[:, 0:32], axis=AXX)
                v.drain()
                v.tensor_scalar(msk, msk, stat[:, 0:1], None, ALU.is_ge)
                v.drain()
                v.tensor_tensor(msk, msk, sco, ALU.mult)
                v.drain()
                v.reduce_sum(stat[:, 1:2], msk, axis=AXX)
                v.drain()
                v.tensor_scalar(stat[:, 1:2], stat[:, 1:2], 1e-20, None, ALU.add)
                v.drain()
                v.reciprocal(stat[:, 2:3], stat[:, 1:2])
                v.drain()
                S.inc(v.tensor_scalar(combB[:, tc * H:(tc + 1) * H], msk,
                                      stat[:, 2:3], 2.5, ALU.mult, ALU.mult),
                      "VC")
            VV(rmath)

            def ctr(t, tc=tc):
                S.wait(t, "VC")
                S.inc(t.transpose(psA[0:32, 128:256], combB[:, tc * H:(tc + 1) * H],
                                  c_idf[:, :]), "TC")
            TT(ctr)

            def cte(v, tc=tc):
                S.wait(v, "TC")
                S.inc(v.tensor_copy(combTb[0:32, tc * 128:(tc + 1) * 128],
                                    psA[0:32, 128:256]), "VC")
            VV(cte)
        fence()
        if "scores" in dbg:
            def ds(y):
                S.inc(y.dma_start(dbg["scores"], scorestk[:, :]), "ST", 16)
            SY(ds)
        if "comb" in dbg:
            def dc(y):
                S.inc(y.dma_start(dbg["comb"], combB[:, :]), "ST", 16)
            SY(dc)

    pop_to(m_d)
    m_e = mark()
    wtu = push("wtu", [128, 4 * 8 * 512], BF16)
    wtd = push("wtd", [128, 4 * 4 * D_], BF16)
    heb = push("heb", [128, 4096], BF16)
    cbc4 = push("cbc4", [128, 1024], BF16)
    crow = push("crow", [1, 4 * T], BF16)
    if True:

        groups = [[0, 1, 2, 3], [4, 5, 6, 7], [8, 9, 10, 11], [12, 13, 14, 15],
                  [16, 17, 18, 19], [20, 21, 22, 23], [24, 25, 26, 27],
                  [28, 29, 30, 31], [32, 33, 34, 35]]

        def pf_expert(g, e, slot):
            S.inc(g.dma_start(
                wtu[:, slot * 4096:(slot + 1) * 4096]
                    .rearrange("p (k i) -> p k i", k=8),
                d["upT"][e].rearrange("(k p) i -> p k i", p=128)), "GPD", 16)
            S.inc(g.dma_start(
                wtd[:, slot * 4 * D_:(slot + 1) * 4 * D_]
                    .rearrange("p (k dd) -> p k dd", k=4),
                d["dnT"][e].rearrange("(k p) dd -> p k dd", p=128)), "GPD", 16)

        def pf0(g):
            for slot, e in enumerate(groups[0]):
                pf_expert(g, e, slot)
        GG(pf0)


        gpd_exp = {}
        for slot, e in enumerate(groups[0]):
            gpd_exp[(0, slot)] = None
        gpd_pf0 = S.n["GPD"]
        vc_ep = {}
        tc_dn = {}
        sc_sq = {}
        tc_up_map = {}
        vc_mev = {}
        tc_cbm = {}

        for gi, grp in enumerate(groups):
            def crl(y, grp=grp, gi=gi):
                S.wait(y, "TC")
                S.wait(y, "VC")
                for si, e in enumerate(grp):
                    S.inc(y.dma_start(crow[0:1, si * T:(si + 1) * T],
                                      combTb[e:e + 1, :]), "SYD", 16)
            SY(crl)
            syd_crow = S.n["SYD"]

            for tt in range(8):
                def cbm4(t, gi=gi, tt=tt, grp=grp, syd_crow=syd_crow):
                    S.wait(t, "SYD", syd_crow)
                    if (gi, tt - 1, 3) in vc_ep:
                        S.wait(t, "VC", vc_ep[(gi, tt - 1, 3)])
                    elif gi > 0 and (gi - 1, 7, 3) in vc_ep:
                        S.wait(t, "VC", vc_ep[(gi - 1, 7, 3)])
                    ins = None
                    for si in range(len(grp)):
                        ins = t.matmul(psB[:, si * 256:(si + 1) * 256],
                                       ones1b[0:1, :],
                                       crow[0:1, si * T + tt * 256:
                                            si * T + (tt + 1) * 256],
                                       start=True, stop=True)
                    S.inc(ins, "TC")
                TT(cbm4)
                tc_cbm[(gi, tt)] = S.n["TC"]

                def cbe4(v, gi=gi, tt=tt):
                    S.wait(v, "TC", tc_cbm[(gi, tt)])
                    S.inc(v.tensor_copy(cbc4[:, :], psB[:, 0:1024]), "VC")
                VV(cbe4)
                vc_cbe = S.n["VC"]

                vc_r = {}
                for si, e in enumerate(grp):
                    half = si % 2
                    def upm(t, e=e, tt=tt, si=si, gi=gi, half=half,
                            vc_cbe=vc_cbe):
                        g_ = gpd_exp.get((gi, si))
                        S.wait(t, "GPD", gpd_pf0 if g_ is None else g_)
                        if si < 2:
                            S.wait(t, "VC", vc_cbe)
                        for key in [(gi, tt, si - 2), (gi, tt - 1, si),
                                    (gi - 1, 7, si)]:
                            if key in sc_sq:
                                S.wait(t, "SC", sc_sq[key])
                                break
                        ins = None
                        for ic in range(4):
                            for kd in range(8):
                                ins = t.matmul(
                                    psB[:, half * 1024 + ic * 256:
                                        half * 1024 + (ic + 1) * 256],
                                    wtu[:, si * 4096 + kd * 512 + ic * 128:
                                        si * 4096 + kd * 512 + (ic + 1) * 128],
                                    x2bf[:, kd * T + tt * 256:
                                         kd * T + (tt + 1) * 256],
                                    start=(kd == 0), stop=(kd == 7))
                        S.inc(ins, "TC")
                    TT(upm)
                    tc_up = S.n["TC"]

                    tc_up_map[(gi, tt, si)] = tc_up

                def sqbody(sc, gi=gi, tt=tt, grp=grp):
                    for si in range(len(grp)):
                        half = si % 2
                        S.wait(sc, "TC", tc_up_map[(gi, tt, si)])
                        for key in [(gi, tt - 1, si), (gi - 1, 7, si)]:
                            if key in tc_dn:
                                S.wait(sc, "TC", tc_dn[key])
                                break
                        sc.activation(heb[:, si * 1024:(si + 1) * 1024],
                                      psB[:, half * 1024:(half + 1) * 1024],
                                      ACT.Relu)
                        S.inc(sc.activation(heb[:, si * 1024:(si + 1) * 1024],
                                            heb[:, si * 1024:(si + 1) * 1024],
                                            ACT.Square), "SC")
                        sc_sq[(gi, tt, si)] = S.n["SC"]
                SS(sqbody)

                for si, e in enumerate(grp):
                    def ep3v(v, si=si, gi=gi, tt=tt):
                        S.wait(v, "SC", sc_sq[(gi, tt, si)])
                        S.inc(v.tensor_tensor(
                            heb[:, si * 1024:(si + 1) * 1024]
                                .rearrange("p (ic q) -> p ic q", q=256),
                            heb[:, si * 1024:(si + 1) * 1024]
                                .rearrange("p (ic q) -> p ic q", q=256),
                            cbc4[:, si * 256:(si + 1) * 256].unsqueeze(1)
                                .broadcast_to([128, 4, 256]), ALU.mult), "VC")
                    VV(ep3v)
                    vc_ep[(gi, tt, si)] = S.n["VC"]

                    first = (si == 0)
                    last = (si == len(grp) - 1)
                    def dnm(t, e=e, tt=tt, si=si, gi=gi, first=first, last=last):
                        S.wait(t, "VC", vc_ep[(gi, tt, si)])
                        if first:
                            for key in [(gi, tt - 1), (gi - 1, 7)]:
                                if key in vc_mev:
                                    S.wait(t, "VC", vc_mev[key])
                                    break
                        ins = None
                        for tc2 in range(2):
                            for hh in range(2):
                                for ik in range(4):
                                    ins = t.matmul(
                                        psA[:, tc2 * 1024 + hh * 512:
                                            tc2 * 1024 + (hh + 1) * 512],
                                        heb[:, si * 1024 + ik * 256 + tc2 * 128:
                                            si * 1024 + ik * 256 + tc2 * 128 + 128],
                                        wtd[:, si * 4 * D_ + ik * D_ + hh * 512:
                                            si * 4 * D_ + ik * D_ + (hh + 1) * 512],
                                        start=(first and ik == 0),
                                        stop=(last and ik == 3))
                        S.inc(ins, "TC")
                    TT(dnm)
                    tc_dn[(gi, tt, si)] = S.n["TC"]

                    if tt == 7 and gi + 1 < len(groups) and si < len(groups[gi + 1]):
                        def pfn(g, gi=gi, si=si):
                            S.wait(g, "TC", tc_dn[(gi, 7, si)])
                            pf_expert(g, groups[gi + 1][si], si)
                        GG(pfn)
                        gpd_exp[(gi + 1, si)] = S.n["GPD"]

                def mev(v, tt=tt, gi=gi):
                    S.wait(v, "TC", tc_dn[(gi, tt, len(grp) - 1)])
                    ins = None
                    for tc2 in range(2):
                        tcg = tt * 2 + tc2
                        ins = v.tensor_tensor(
                            MACC[:, tcg * D_:(tcg + 1) * D_],
                            MACC[:, tcg * D_:(tcg + 1) * D_],
                            psA[:, tc2 * 1024:(tc2 + 1) * 1024], ALU.add)
                    S.inc(ins, "VC")
                VV(mev)
                vc_mev[(gi, tt)] = S.n["VC"]
        fence()
        if "macc" in dbg:
            def dm(y):
                S.inc(y.dma_start(dbg["macc"], MACC[:, :]), "ST", 16)
            SY(dm)

        for tc in range(NTC):
            ln_chunk(MACC[:, tc * D_:(tc + 1) * D_], MACC[:, tc * D_:(tc + 1) * D_])

            def out_dma(y, tc=tc):
                S.wait(y, "VC")
                S.inc(y.dma_start(yout[tc * 128:(tc + 1) * 128, :],
                                  MACC[:, tc * D_:(tc + 1) * D_]), "ST", 16)
            SY(out_dma)
            st_now = S.n["ST"]

            def stw(sc, st_now=st_now):
                S.wait(sc, "ST", st_now)
            SS(stw)

            def vtw(v, st_now=st_now):
                S.wait(v, "ST", st_now)
            VV(vtw)

        def final_wait(y):
            S.wait(y, "ST")
        SY(final_wait)

    blk.__exit__(None, None, None)
    pop_to(0)
    for g in reversed(guards):
        g.__exit__(None, None, None)
    return nc, dbg


def _prep_core_inputs(inputs, core):
    b, half = core >> 1, core & 1
    f32 = lambda a: np.ascontiguousarray(np.asarray(a, dtype=np.float32))
    x = f32(inputs["L0_attn_x"])
    W_in = f32(inputs["L0_attn_W_in"])
    conv_w = f32(inputs["L0_attn_conv_w"])
    conv_b = f32(inputs["L0_attn_conv_b"])
    lnw = f32(inputs["L0_attn_ln_w"]); lnb = f32(inputs["L0_attn_ln_b"])
    dtb = f32(inputs["L0_attn_dt_bias"]); alog = f32(inputs["L0_attn_A_log"])
    Dv = f32(inputs["L0_attn_D"]); nw = f32(inputs["L0_attn_norm_w"])
    W_out = f32(inputs["L0_attn_W_out"])
    gw = f32(inputs["L0_ffn_gate_w"]); eb = f32(inputs["L0_ffn_e_bias"])
    up = f32(inputs["L0_ffn_up_w"]); dn = f32(inputs["L0_ffn_down_w"])
    sup = f32(inputs["L0_ffn_shared_up_w"]); sdn = f32(inputs["L0_ffn_shared_down_w"])

    def convlay(w):
        return np.ascontiguousarray(w.reshape(NEC, 128, 4).transpose(1, 0, 2)
                                    .reshape(128, NEC * 4))

    def convblay(w):
        return np.ascontiguousarray(w.reshape(NEC, 128).T)

    upT = np.ascontiguousarray(np.concatenate(
        [up.transpose(0, 2, 1),
         np.ascontiguousarray(sup.T).reshape(D_, 4, 512).transpose(1, 0, 2)],
        axis=0))
    dnT = np.ascontiguousarray(np.concatenate(
        [dn.transpose(0, 2, 1),
         np.ascontiguousarray(sdn.T).reshape(4, 512, D_)], axis=0))

    if half == 1:
        xaux = np.ascontiguousarray(x[b, 0:T])
        winTa = np.ascontiguousarray(W_in[E:].T)
        lnwa, lnba = lnw, lnb
        cwa, cba = convlay(conv_w), convblay(conv_b)
        dtba, aloga = dtb, alog
    else:
        xaux = np.zeros((T, D_), np.float32)
        winTa = np.zeros((D_, CD + H), np.float32)
        lnwa = np.zeros(D_, np.float32); lnba = np.zeros(D_, np.float32)
        cwa = np.zeros((128, NEC * 4), np.float32)
        cba = np.zeros((128, NEC), np.float32)
        dtba = np.zeros(H, np.float32); aloga = np.zeros(H, np.float32)

    return {
        "xin": np.ascontiguousarray(x[b, half * T:(half + 1) * T]),
        "xaux": xaux,
        "winT": np.ascontiguousarray(W_in.T),
        "winTa": winTa,
        "lnrows": np.ascontiguousarray(
            np.concatenate([lnw, lnb, lnwa, lnba]).reshape(1, 4 * D_)),
        "cw": convlay(conv_w), "cb": convblay(conv_b),
        "cwa": cwa, "cba": cba,
        "dtb": dtb.reshape(H, 1), "alog": alog.reshape(H, 1),
        "dtba": dtba.reshape(H, 1), "aloga": aloga.reshape(H, 1),
        "dexp": np.ascontiguousarray(np.repeat(Dv, P).reshape(16, 128).T),
        "nwp": np.ascontiguousarray(nw.reshape(16, 128).T),
        "woutT": np.ascontiguousarray(W_out.T),
        "gwT": np.ascontiguousarray(gw.T),
        "ebias": eb.reshape(1, H),
        "upT": upT, "dnT": dnT,
    }


_CACHE = {}


def kernel(**inputs):
    if "prog" not in _CACHE:
        _CACHE["prog"] = build_program()
    nc, _ = _CACHE["prog"]
    in_maps = [_prep_core_inputs(inputs, c) for c in range(8)]
    res = run_bass_kernel_spmd(nc, in_maps, core_ids=list(range(8)))
    out = np.zeros((B_, L_, D_), np.float32)
    for c in range(8):
        b, half = c >> 1, c & 1
        out[b, half * T:(half + 1) * T] = np.asarray(res.results[c]["yout"])
    return out


if __name__ == "__main__":
    nc, _ = build_program()
    print("build ok, instructions:", nc.n_instructions())


# version 13
# speedup vs baseline: 1.3612x; 1.0447x over previous
import numpy as np
import ml_dtypes
import concourse.bass as bass
import concourse.mybir as mybir
from concourse.bass_utils import run_bass_kernel_spmd

F32 = mybir.dt.float32
BF16 = mybir.dt.bfloat16
ALU = mybir.AluOpType
ACT = mybir.ActivationFunctionType
AXX = mybir.AxisListType.X

B_, L_, D_ = 4, 4096, 1024
T = 2048
NTC = 16
Q = 128
H, P, NS = 32, 64, 64
E = 2048
CD = 2176
DIP = 4256
NEX = 36
EPS = 1e-5
EPSI = 1e-6
TILE = 256
NTILE = T // TILE
NEC = 17


class Sems:
    def __init__(self, nc, names):
        self.h, self.n, self._g = {}, {}, []
        for nm in names:
            g = nc.semaphore(nm)
            self.h[nm] = g.__enter__()
            self._g.append(g)
            self.n[nm] = 0

    def inc(self, instr, nm, k=1):
        instr.then_inc(self.h[nm], k)
        self.n[nm] += k
        return instr

    def wait(self, eng, nm, val=None):
        eng.wait_ge(self.h[nm], self.n[nm] if val is None else val)


def build_program(debug=()):
    nc = bass.Bass("TRN2", target_bir_lowering=False)
    d = {}

    def din(name, shape):
        d[name] = nc.dram_tensor(name, list(shape), F32, kind="ExternalInput").ap()

    din("xin", (T, D_)); din("xaux", (T, D_))
    din("winT", (D_, DIP)); din("winTa", (D_, CD + H))
    din("lnrows", (1, 4 * D_))
    din("cw", (128, NEC * 4)); din("cb", (128, NEC))
    din("cwa", (128, NEC * 4)); din("cba", (128, NEC))
    din("dtb", (H, 1)); din("alog", (H, 1)); din("dtba", (H, 1)); din("aloga", (H, 1))
    din("dexp", (128, 16)); din("nwp", (128, 16))
    din("woutT", (E, D_)); din("gwT", (D_, H)); din("ebias", (1, H))
    din("upT", (NEX, D_, 512)); din("dnT", (NEX, 512, D_))
    yout = nc.dram_tensor("yout", [T, D_], F32, kind="ExternalOutput").ap()

    x_sp = nc.dram_tensor("x_sp", [16, 128, T], BF16, kind="Internal").ap()
    xa_sp = nc.dram_tensor("xa_sp", [16, 128, T], BF16, kind="Internal").ap()
    sz_sp = nc.dram_tensor("sz_sp", [16, 128, T], BF16, kind="Internal").ap()
    x1_sp = nc.dram_tensor("x1_sp", [T, D_], F32, kind="Internal").ap()

    dbg = {}
    dbg_shapes = {"u": (128, 8 * T), "dtv": (H, T), "acinc": (H, T),
                  "Hinit": (NS, E), "Hend": (NS, E), "y0": (128, E),
                  "x1": (128, NTC * D_), "scores": (128, NTC * H),
                  "comb": (128, NTC * H), "macc": (128, NTC * D_)}
    for nm in debug:
        dbg[nm] = nc.dram_tensor("dbg_" + nm, list(dbg_shapes[nm]), F32,
                                 kind="ExternalOutput").ap()

    S = Sems(nc, ["TC", "VC", "SC", "GC", "SYD", "SCD", "GPD", "ST"])
    guards = []

    def sbuf(name, shape, dt, **kw):
        g = nc.sbuf_tensor(name, shape, dt, **kw)
        t = g.__enter__()
        guards.append(g)
        return t

    c_strict = sbuf("c_strict", [128, 128], BF16)
    c_umask = sbuf("c_umask", [128, 128], BF16)
    c_idf = sbuf("c_idf", [128, 128], F32)
    identb = sbuf("identb", [128, 128], BF16)
    c_ind = sbuf("c_ind", [32, 2048], BF16)
    ones1b = sbuf("ones1b", [1, 128], BF16)
    ones1f = sbuf("ones1f", [1, 128], F32)
    onescol = sbuf("onescol", [128, 1], BF16)
    c_eb = sbuf("c_eb", [128, H], F32)
    cw_s = sbuf("cw_s", [128, NEC * 4], F32)
    cb_s = sbuf("cb_s", [128, NEC], F32)
    cwa_s = sbuf("cwa_s", [128, NEC * 4], F32)
    cba_s = sbuf("cba_s", [128, NEC], F32)
    dexp_s = sbuf("dexp_s", [128, 16], F32)
    nwp_s = sbuf("nwp_s", [128, 16], F32)
    ebrow = sbuf("ebrow", [1, H], F32)
    gwh = sbuf("gwh", [128, 8 * H], BF16)
    gwl = sbuf("gwl", [128, 8 * H], BF16)
    scorestk = sbuf("scorestk", [128, NTC * H], F32)
    combB = sbuf("combB", [128, NTC * H], F32)
    combTb = sbuf("combTb", [NEX, T], BF16)
    stat = sbuf("stat", [128, 8], F32)
    sqscr = sbuf("sqscr", [128, D_], F32)
    rmsrow = sbuf("rmsrow", [1, 256], F32)
    rscr = sbuf("rscr", [128, 40], F32)
    rscr2 = sbuf("rscr2", [128, 40], F32)
    cvals = sbuf("cvals", [128, 2], F32)

    psA = nc.psum_tensor("psA", [128, 2048], F32).__enter__()
    psB = nc.psum_tensor("psB", [128, 2048], F32).__enter__()


    arena = []

    def push(name, shape, dt, **kw):
        g = nc.sbuf_tensor(name, shape, dt, **kw)
        t = g.__enter__()
        arena.append(g)
        return t

    def mark():
        return len(arena)

    def pop_to(m):
        while len(arena) > m:
            arena.pop().__exit__(None, None, None)

    m_ab = mark()
    pk1 = push("pk1", [64, T], F32)
    pk2 = push("pk2", [64, T], F32)
    pk3 = push("pk3", [64, T], F32)
    pk4 = push("pk4", [64, T], F32)
    pk5 = push("pk5", [64, T], F32)
    dtraw, dtrawA = pk1[0:32, :], pk1[32:64, :]
    dtv, dtvA = pk2[0:32, :], pk2[32:64, :]
    acinc, acincA = pk3[0:32, :], pk3[32:64, :]
    w2m, w2a = pk4[0:32, :], pk4[32:64, :]
    scr32, scr32A = pk5[0:32, :], pk5[32:64, :]
    apack = push("apack", [64, 1], F32)
    iotaE = push("iotaE", [128, H], F32)
    gwf = push("gwf", [128, 8 * H], F32)
    dtpack = push("dtpack", [64, 1], F32)
    acol, acola = apack[0:32, :], apack[32:64, :]
    dtb_s, dtba_s = dtpack[0:32, :], dtpack[32:64, :]
    erow = push("erow", [H, T], BF16)
    lam64 = push("lam64", [64, NTC * H], F32)
    lam64A = push("lam64A", [64, NTC * H], F32)
    dtvT = push("dtvT", [128, NTC * H], F32)
    dtvTA = push("dtvTA", [128, NTC * H], F32)
    w2T = push("w2T", [128, NTC * H], F32)
    w2TA = push("w2TA", [128, NTC * H], F32)
    bcm = push("bcm", [128, T], BF16)
    bcma = push("bcma", [128, T], BF16)
    h0 = push("h0", [NS, E], F32)
    h1 = push("h1", [NS, E], F32)
    HB = [h0, h1]

    blk = nc.Block().__enter__()
    TT, VV, SS, GG, SY = blk.tensor, blk.vector, blk.scalar, blk.gpsimd, blk.sync

    def consts_g(g):
        for nm, dst in [("cw", cw_s), ("cb", cb_s), ("cwa", cwa_s), ("cba", cba_s),
                        ("dexp", dexp_s), ("nwp", nwp_s), ("dtb", dtb_s),
                        ("dtba", dtba_s), ("ebias", ebrow), ("alog", acol),
                        ("aloga", acola)]:
            S.inc(g.dma_start(dst[:, :], d[nm]), "GPD", 16)
        S.inc(g.dma_start(gwf[:, :].rearrange("p (k h) -> p k h", k=8),
                          d["gwT"].rearrange("(k p) h -> p k h", p=128)), "GPD", 16)
        g.memset(ones1b[:, :], 1.0)
        g.memset(ones1f[:, :], 1.0)
        g.memset(onescol[:, :], 1.0)
        g.memset(c_strict[:, :], 1.0)
        g.memset(c_umask[:, :], 1.0)
        g.memset(c_idf[:, :], 1.0)
        g.memset(identb[:, :], 1.0)
        g.memset(combTb[32:36, :], 1.0)
        g.drain()
        g.affine_select(c_strict[:, :], c_strict[:, :], [[-1, 128]], ALU.is_ge, 0.0,
                        base=-1, channel_multiplier=1)
        g.affine_select(c_umask[:, :], c_umask[:, :], [[1, 128]], ALU.is_ge, 0.0,
                        channel_multiplier=-1)
        g.affine_select(c_idf[:, :], c_idf[:, :], [[-1, 128]], ALU.is_equal, 0.0,
                        channel_multiplier=1)
        g.drain()
        g.affine_select(identb[:, :], identb[:, :], [[-1, 128]], ALU.is_equal, 0.0,
                        channel_multiplier=1)
        g.memset(cvals[:, 0:1], EPS)
        g.memset(cvals[:, 1:2], 1.0)
        g.memset(c_ind[:, :], 1.0)
        g.drain()
        g.affine_select(c_ind[:, :], c_ind[:, :], [[1, 2048]], ALU.is_ge, 0.0,
                        channel_multiplier=-64)
        g.drain()
        g.affine_select(c_ind[:, :], c_ind[:, :], [[-1, 2048]], ALU.is_ge, 0.0,
                        base=63, channel_multiplier=64)
        g.iota(iotaE[:, :], [[1, H]], base=0, channel_multiplier=0,
               allow_small_or_imprecise_dtypes=True)
        S.inc(g.memset(h0[:, :], 0.0), "GC")
    GG(consts_g)

    def consts_v(v):
        S.wait(v, "GC")
        S.wait(v, "GPD")
        v.tensor_copy(gwh[:, :], gwf[:, :])
        v.drain()
        S.inc(v.tensor_tensor(gwl[:, :], gwf[:, :], gwh[:, :], ALU.subtract), "VC")
    VV(consts_v)

    def consts_s(sc):
        S.wait(sc, "GPD")
        sc.activation(acol[:, :], acol[:, :], ACT.Exp)
        S.inc(sc.activation(acola[:, :], acola[:, :], ACT.Exp), "SC")
    SS(consts_s)

    def consts_v2(v):
        S.wait(v, "SC")
        v.tensor_scalar_mul(acol[:, :], acol[:, :], -1.0)
        S.inc(v.tensor_scalar_mul(acola[:, :], acola[:, :], -1.0), "VC")
    VV(consts_v2)

    def eb_t(t):
        S.wait(t, "VC")
        S.inc(t.matmul(psA[:, 0:H], ones1f[0:1, :], ebrow[0:1, :], start=True,
                       stop=True), "TC")
    TT(eb_t)

    def eb_v(v):
        S.wait(v, "TC")
        S.inc(v.scalar_tensor_tensor(c_eb[:, :], iotaE[:, :], -EPSI, psA[:, 0:H],
                                     ALU.mult, ALU.add), "VC")
    VV(eb_v)

    def fence():
        for body in [TT, VV, SS, GG, SY]:
            def f(e):
                for nm in ["TC", "VC", "SC", "GC", "SYD", "SCD", "GPD"]:
                    S.wait(e, nm)
            body(f)

    def ln_chunk(src_ap, dst_ap):
        def v1(v):
            S.wait(v, "SC")
            v.reduce_sum(stat[:, 1:2], src_ap, axis=AXX)
            v.drain()
            S.inc(v.tensor_scalar_mul(stat[:, 2:3], stat[:, 1:2], 1.0 / D_), "VC")
        VV(v1)

        def s1(sc):
            S.wait(sc, "VC")
            S.inc(sc.activation(sqscr[:, :], src_ap, ACT.Square,
                                accum_out=stat[:, 0:1]), "SC")
        SS(s1)

        def v2(v):
            S.wait(v, "SC")
            v.tensor_scalar_mul(stat[:, 3:4], stat[:, 0:1], 1.0 / D_)
            v.drain()
            v.tensor_tensor(stat[:, 4:5], stat[:, 2:3], stat[:, 2:3], ALU.mult)
            v.drain()
            S.inc(v.tensor_tensor(stat[:, 4:5], stat[:, 3:4], stat[:, 4:5],
                                  ALU.subtract), "VC")
        VV(v2)

        def s2(sc):
            S.wait(sc, "VC")
            sc.activation(stat[:, 5:6], stat[:, 4:5], ACT.Ln, bias=cvals[:, 0:1])
            S.inc(sc.activation(stat[:, 6:7], stat[:, 5:6], ACT.Exp, scale=-0.5),
                  "SC")
        SS(s2)

        def v3(v):
            S.wait(v, "SC")
            S.inc(v.tensor_scalar(dst_ap, src_ap, stat[:, 2:3], stat[:, 6:7],
                                  ALU.subtract, ALU.mult), "VC")
        VV(v3)

    m_a = mark()
    ufm = push("ufm", [128, 8 * T], BF16, is_transpose_dest=True)
    wins = push("wins", [128, 3 * 8 * 128], BF16)
    gbB = push("gbB", [128, 4 * D_], F32)
    gbrow = push("gbrow", [1, D_], F32)
    xtile = push("xtile", [128, D_], F32)
    utile = push("utile", [128, D_], F32)
    ubtile = push("ubtile", [128, D_], BF16)
    fbuf0 = push("fbuf0", [128, T + 3], BF16)
    fbuf1 = push("fbuf1", [128, T + 3], BF16)
    cacc0 = push("cacc0", [128, T], BF16)
    ubz0 = push("ubz0", [128, T], BF16)
    ubz1 = push("ubz1", [128, T], BF16)
    if True:

        for i in range(4):
            def gb_load(g, i=i):
                S.wait(g, "TC")
                S.inc(g.dma_start(gbrow[:, :], d["lnrows"][0:1, i * D_:(i + 1) * D_]),
                      "GPD", 16)
            GG(gb_load)
            gpd_r = S.n["GPD"]

            def bc_t(t, i=i, gpd_r=gpd_r):
                S.wait(t, "GPD", gpd_r)
                S.wait(t, "VC")
                t.matmul(psA[:, 0:512], ones1f[0:1, :],
                         gbrow[0:1, 0:512], start=True, stop=True)
                S.inc(t.matmul(psA[:, 512:1024], ones1f[0:1, :],
                               gbrow[0:1, 512:1024], start=True,
                               stop=True), "TC")
            TT(bc_t)

            def bc_v(v, i=i):
                S.wait(v, "TC")
                S.inc(v.tensor_copy(gbB[:, i * D_:(i + 1) * D_], psA[:, 0:1024]),
                      "VC")
            VV(bc_v)

        syd_u = {}

        def phase_ln1_fixed(src_dram, goff):
            syd_u.clear()
            for tc in range(NTC):
                def ld(y, tc=tc):
                    S.wait(y, "VC")
                    S.wait(y, "SC")
                    S.inc(y.dma_start(xtile[:, :],
                                      src_dram[tc * 128:(tc + 1) * 128, :]),
                          "SYD", 16)
                SY(ld)
                syd_x = S.n["SYD"]

                def s1(sc, syd_x=syd_x):
                    S.wait(sc, "SYD", syd_x)
                    S.inc(sc.activation(sqscr[:, :], xtile[:, :], ACT.Square,
                                        accum_out=stat[:, 0:1]), "SC")
                SS(s1)

                def v1(v, syd_x=syd_x):
                    S.wait(v, "SYD", syd_x)
                    v.reduce_sum(stat[:, 1:2], xtile[:, :], axis=AXX)
                    v.drain()
                    S.wait(v, "SC")
                    v.tensor_scalar_mul(stat[:, 2:3], stat[:, 1:2], 1.0 / D_)
                    v.tensor_scalar_mul(stat[:, 3:4], stat[:, 0:1], 1.0 / D_)
                    v.drain()
                    v.tensor_tensor(stat[:, 4:5], stat[:, 2:3], stat[:, 2:3], ALU.mult)
                    v.drain()
                    S.inc(v.tensor_tensor(stat[:, 4:5], stat[:, 3:4], stat[:, 4:5],
                                          ALU.subtract), "VC")
                VV(v1)

                def s2(sc):
                    S.wait(sc, "VC")
                    sc.activation(stat[:, 5:6], stat[:, 4:5], ACT.Ln, bias=cvals[:, 0:1])
                    S.inc(sc.activation(stat[:, 6:7], stat[:, 5:6], ACT.Exp,
                                        scale=-0.5), "SC")
                SS(s2)

                def v2(v, goff=goff):
                    S.wait(v, "SC")
                    S.wait(v, "SYD")
                    S.wait(v, "SCD")
                    v.scalar_tensor_tensor(utile[:, :], xtile[:, :], stat[:, 2:3],
                                           gbB[:, goff * D_:(goff + 1) * D_],
                                           ALU.subtract, ALU.mult)
                    v.drain()
                    S.inc(v.scalar_tensor_tensor(
                        ubtile[:, :], utile[:, :], stat[:, 6:7],
                        gbB[:, (goff + 1) * D_:(goff + 2) * D_],
                        ALU.mult, ALU.add), "VC")
                VV(v2)

                def tr(y, tc=tc):
                    S.wait(y, "VC")
                    for kd in range(4):
                        S.inc(y.dma_start(
                            ufm[:, kd * T + tc * 128: kd * T + (tc + 1) * 128],
                            ubtile[:, kd * 128:(kd + 1) * 128], transpose=True),
                            "SYD", 16)
                SY(tr)

                def trb(sc, tc=tc):
                    S.wait(sc, "VC")
                    for kd in range(4, 8):
                        S.inc(sc.dma_start(
                            ufm[:, kd * T + tc * 128: kd * T + (tc + 1) * 128],
                            ubtile[:, kd * 128:(kd + 1) * 128], transpose=True),
                            "SCD", 16)
                SS(trb)
                syd_u[tc] = (S.n["SYD"], S.n["SCD"])

        def phase_a2(winT_dram, necols, spill, bc_dst, dtraw_dst, zch, cw_t, cb_t):
            dt_bp = dtraw_dst.base_partition()
            nec_all = necols // 128
            rem = necols - nec_all * 128
            n_ec = nec_all + (1 if rem else 0)
            src3 = winT_dram.rearrange("(k p) e -> p k e", p=128)

            def zt(v):
                v.memset(fbuf0[:, 0:3], 0.0)
                S.inc(v.memset(fbuf1[:, 0:3], 0.0), "VC")
            VV(zt)
            ufm_tile = [syd_u[2 * ti + 1] for ti in range(NTILE)]

            wslab = [wins[:, s * 1024:(s + 1) * 1024]
                         .rearrange("p (k e) -> p k e", e=128) for s in range(3)]
            slab_gpd = {}

            def wl0(g):
                for s in range(min(3, n_ec)):
                    rows = 128 if s < nec_all else rem
                    S.inc(g.dma_start(wslab[s][:, :, 0:rows],
                                      src3[:, :, s * 128:s * 128 + rows]),
                          "GPD", 16)
                    slab_gpd[s] = S.n["GPD"]
            GG(wl0)

            last_evac = {psl: ("VC", S.n["VC"]) for psl in range(4)}
            fb_free = [None, None]
            cc_free = [None]
            ub_free = [None, None]

            for ec in range(n_ec):
                mm_rows = 128 if ec < nec_all else rem
                slot = ec % 3
                gpd_here = slab_gpd[ec]
                par = ec % 2
                fb = [fbuf0, fbuf1][par]
                cc = cacc0
                ub = [ubz0, ubz1][par]
                evac_cnts = []

                for ti in range(NTILE):
                    psl = ti % 4
                    if ec < nec_all:
                        sl = psA[0:mm_rows, psl * 512:psl * 512 + TILE]
                    else:
                        sl = psA[dt_bp:dt_bp + mm_rows, psl * 512:psl * 512 + TILE]

                    def mm(t, ec=ec, ti=ti, sl=sl, mm_rows=mm_rows, slot=slot,
                           gpd_here=gpd_here, psl=psl):
                        S.wait(t, "GPD", gpd_here)
                        S.wait(t, "SYD", ufm_tile[ti][0])
                        S.wait(t, "SCD", ufm_tile[ti][1])
                        if psl in last_evac and last_evac[psl] is not None:
                            S.wait(t, last_evac[psl][0], last_evac[psl][1])
                        ins = None
                        for kd in range(8):
                            ins = t.matmul(
                                sl, wslab[slot][:, kd, 0:mm_rows],
                                ufm[:, kd * T + ti * TILE: kd * T + (ti + 1) * TILE],
                                start=(kd == 0), stop=(kd == 7))
                        S.inc(ins, "TC")
                    TT(mm)
                    tc_here = S.n["TC"]
                    if ti == NTILE - 1 and ec + 3 < n_ec:
                        def wln(g, ec=ec, slot=slot, tc_here=tc_here):
                            S.wait(g, "TC", tc_here)
                            rows = 128 if ec + 3 < nec_all else rem
                            S.inc(g.dma_start(
                                wslab[slot][:, :, 0:rows],
                                src3[:, :, (ec + 3) * 128:(ec + 3) * 128 + rows]),
                                "GPD", 16)
                        GG(wln)
                        slab_gpd[ec + 3] = S.n["GPD"]

                    if ec < nec_all:
                        def ev(v, ti=ti, sl=sl, tc_here=tc_here, fb=fb, par=par):
                            S.wait(v, "TC", tc_here)
                            if ti == 0 and fb_free[par] is not None:
                                S.wait(v, fb_free[par][0], fb_free[par][1])
                            S.inc(v.tensor_copy(
                                fb[:, 3 + ti * TILE: 3 + (ti + 1) * TILE], sl),
                                "VC")
                        VV(ev)
                    else:
                        def ev(v, ti=ti, sl=sl, tc_here=tc_here,
                               dtraw_dst=dtraw_dst):
                            S.wait(v, "TC", tc_here)
                            S.inc(v.tensor_copy(
                                dtraw_dst[:, ti * TILE:(ti + 1) * TILE], sl), "VC")
                        VV(ev)
                    last_evac[psl] = ("VC", S.n["VC"])

                if ec >= nec_all:
                    continue

                if ec < zch:
                    def zsil(sc, fb=fb, ub=ub, par=par):
                        S.wait(sc, "VC")
                        if ub_free[par] is not None:
                            S.wait(sc, ub_free[par][0], ub_free[par][1])
                        S.inc(sc.activation(ub[:, :], fb[:, 3:3 + T], ACT.Silu),
                              "SC")
                    SS(zsil)
                    fb_free[par] = ("SC", S.n["SC"])

                    def zsp(y, ec=ec, ub=ub):
                        S.wait(y, "SC")
                        S.inc(y.dma_start(sz_sp[ec, :, :], ub[:, :]), "SYD", 16)
                    SY(zsp)
                    ub_free[par] = ("SYD", S.n["SYD"])
                else:
                    xc = ec - zch
                    def cv(v, fb=fb, cc=cc, xc=xc, par=par, cw_t=cw_t):
                        if cc_free[0] is not None:
                            S.wait(v, cc_free[0][0], cc_free[0][1])
                        v.tensor_scalar(cc[:, :], fb[:, 0:T],
                                        cw_t[:, xc * 4:xc * 4 + 1], None, ALU.mult)
                        ins = None
                        for j in range(1, 4):
                            v.drain()
                            ins = v.scalar_tensor_tensor(
                                cc[:, :], fb[:, j:j + T],
                                cw_t[:, xc * 4 + j:xc * 4 + j + 1], cc[:, :],
                                ALU.mult, ALU.add)
                        S.inc(ins, "VC")
                    VV(cv)
                    fb_free[par] = ("VC", S.n["VC"])

                    def csil(sc, cc=cc, ub=ub, par=par, xc=xc, cb_t=cb_t):
                        S.wait(sc, "VC")
                        if ub_free[par] is not None:
                            S.wait(sc, ub_free[par][0], ub_free[par][1])
                        S.inc(sc.activation(ub[:, :], cc[:, :], ACT.Silu,
                                            bias=cb_t[:, xc:xc + 1]), "SC")
                    SS(csil)
                    cc_free[0] = ("SC", S.n["SC"])

                    if xc < 16:
                        def xsp(y, xc=xc, ub=ub, spill=spill):
                            S.wait(y, "SC")
                            S.inc(y.dma_start(spill[xc, :, :], ub[:, :]),
                                  "SYD", 16)
                        SY(xsp)
                        ub_free[par] = ("SYD", S.n["SYD"])
                    else:
                        def bcp(v, ub=ub, bc_dst=bc_dst):
                            S.wait(v, "SC")
                            S.inc(v.tensor_copy(bc_dst[:, :], ub[:, :]), "VC")
                        VV(bcp)
                        ub_free[par] = ("VC", S.n["VC"])

        def phase_a3(dtr, dtb_t, a_t, dv, ac, w2_, l64, dvT, w2T_, er, scr):
            bp3 = dtr.base_partition()

            def sp1(sc):
                S.wait(sc, "VC")
                sc.activation(scr[:, :], dtr[:, :], ACT.Abs, bias=dtb_t[:, 0:1])
                sc.activation(scr[:, :], scr[:, :], ACT.Exp, scale=-1.0)
                S.inc(sc.activation(scr[:, :], scr[:, :], ACT.Ln,
                                    bias=cvals[bp3:bp3 + 32, 1:2]), "SC")
            SS(sp1)

            def sp2(v):
                S.wait(v, "SC")
                v.tensor_scalar(dv[:, :], dtr[:, :], dtb_t[:, 0:1], 0.0,
                                ALU.add, ALU.max)
                v.drain()
                v.tensor_tensor(dv[:, :], dv[:, :], scr[:, :], ALU.add)
                v.drain()
                ins = None
                v.tensor_scalar(scr[:, :], dv[:, :], a_t[:, 0:1], None, ALU.mult)
                v.drain()
                for c in range(NTC):
                    ins = v.tensor_tensor_scan(ac[:, c * Q:(c + 1) * Q],
                                               scr[:, c * Q:(c + 1) * Q],
                                               scr[:, c * Q:(c + 1) * Q], 0.0,
                                               ALU.add, ALU.bypass)
                S.inc(ins, "VC")
            VV(sp2)

            if er is not None:
                def sp4(sc):
                    S.wait(sc, "VC")
                    S.inc(sc.activation(er[:, :], ac[:, :], ACT.Exp), "SC")
                SS(sp4)

            for c in range(NTC):
                def trp(t, c=c):
                    S.wait(t, "VC")
                    S.wait(t, "SC")
                    bp = ac.base_partition()
                    t.transpose(psB[0:1, 64:96], ac[:, (c + 1) * Q - 1:(c + 1) * Q],
                                c_idf[bp:bp + 32, bp:bp + 32])
                    bpd = dv.base_partition()
                    S.inc(t.transpose(psB[:, 32:64], dv[:, c * Q:(c + 1) * Q],
                                      c_idf[bpd:bpd + 32, bpd:bpd + 32]), "TC")
                TT(trp)

                def tev(v, c=c, dvT=dvT):
                    S.wait(v, "TC")
                    v.tensor_copy(dvT[:, c * H:(c + 1) * H], psB[:, 32:64])
                    S.inc(v.tensor_copy(rmsrow[0:1, 0:32], psB[0:1, 64:96]), "VC")
                VV(tev)

                def lse(sc, c=c):
                    S.wait(sc, "VC")
                    S.inc(sc.activation(rmsrow[0:1, 32:64], rmsrow[0:1, 0:32],
                                        ACT.Exp), "SC")
                SS(lse)

                def lmm(t, c=c):
                    S.wait(t, "SC")
                    S.inc(t.matmul(psB[0:64, 64:96], ones1f[0:1, 0:64],
                                   rmsrow[0:1, 32:64], start=True, stop=True), "TC")
                TT(lmm)

                def lev(v, c=c, l64=l64):
                    S.wait(v, "TC")
                    S.inc(v.tensor_copy(l64[:, c * H:(c + 1) * H], psB[0:64, 64:96]),
                          "VC")
                VV(lev)

                def w2v(v, c=c, ac=ac):
                    S.inc(v.tensor_scalar(scr[:, c * Q:(c + 1) * Q],
                                          ac[:, c * Q:(c + 1) * Q],
                                          ac[:, (c + 1) * Q - 1:(c + 1) * Q], None,
                                          ALU.subtract), "VC")
                VV(w2v)

                def w2e(sc, c=c):
                    S.wait(sc, "VC")
                    S.inc(sc.activation(scr[:, c * Q:(c + 1) * Q],
                                        scr[:, c * Q:(c + 1) * Q], ACT.Exp,
                                        scale=-1.0), "SC")
                SS(w2e)

                def w2f(v, c=c, dv=dv, w2_=w2_):
                    S.wait(v, "SC")
                    S.inc(v.tensor_tensor(w2_[:, c * Q:(c + 1) * Q],
                                          dv[:, c * Q:(c + 1) * Q],
                                          scr[:, c * Q:(c + 1) * Q], ALU.mult),
                          "VC")
                VV(w2f)

                def w2t(t, c=c, w2_=w2_):
                    S.wait(t, "VC")
                    bpw = w2_.base_partition()
                    S.inc(t.transpose(psB[:, 96:128], w2_[:, c * Q:(c + 1) * Q],
                                      c_idf[bpw:bpw + 32, bpw:bpw + 32]), "TC")
                TT(w2t)

                def w2tv(v, c=c, w2T_=w2T_):
                    S.wait(v, "TC")
                    S.inc(v.tensor_copy(w2T_[:, c * H:(c + 1) * H], psB[:, 96:128]),
                          "VC")
                VV(w2tv)

        phase_ln1_fixed(d["xaux"], 2)
        phase_a2(d["winTa"], CD + H, xa_sp, bcma, dtrawA, 0, cwa_s, cba_s)
        fence()
        phase_ln1_fixed(d["xin"], 0)
        phase_a2(d["winT"], DIP, x_sp, bcm, dtraw, 16, cw_s, cb_s)
        fence()
        phase_a3(dtrawA, dtba_s, acola, dtvA, acincA, w2a, lam64A, dtvTA, w2TA,
                 None, scr32A)
        phase_a3(dtraw, dtb_s, acol, dtv, acinc, w2m, lam64, dtvT, w2T, erow, scr32)
        fence()
        if "u" in dbg:
            def du(g):
                S.inc(g.dma_start(dbg["u"], ufm[:, :]), "ST", 16)
            GG(du)
        if "dtv" in dbg:
            def dd(y):
                S.inc(y.dma_start(dbg["dtv"], dtv[:, :]), "ST", 16)
            SY(dd)
        if "acinc" in dbg:
            def da(y):
                S.inc(y.dma_start(dbg["acinc"], acinc[:, :]), "ST", 16)
            SY(da)

    pop_to(m_a)
    m_bc = mark()
    wout2 = push("wout2", [128, 16 * D_], BF16)
    xt0 = push("xt0", [128, E], BF16, is_transpose_dest=True)
    xt1 = push("xt1", [128, E], BF16, is_transpose_dest=True)
    wq = push("wq", [128, 2 * E], BF16)
    e2sb = push("e2sb", [128, E], BF16)
    gpmb = push("gpmb", [128, 128], BF16)
    zal = push("zal", [128, E], BF16)
    btp = push("btp", [128, 64], BF16)
    ybuf = push("ybuf", [128, E], F32)
    gbuf = push("gbuf", [128, E], F32)
    g2b = push("g2b", [128, E], BF16)
    xck = push("xck", [128, E], BF16)
    szt = push("szt", [128, E], BF16)
    ynb = push("ynb", [128, E], BF16)
    rib = push("rib", [128, 128], F32)
    dab = push("dab", [128, 32], F32)
    vbuf = push("vbuf", [128, D_], F32)
    xres = push("xres", [128, D_], F32)
    x1t = push("x1t", [128, D_], F32)
    cmt = push("cmt", [64, Q], BF16)
    if True:

        XT = [xt0, xt1]

        def w2l(g):
            S.inc(g.dma_start(wout2[:, :].rearrange("p (k dd) -> p k dd", k=16),
                              d["woutT"].rearrange("(k p) dd -> p k dd", p=128)),
                  "GPD", 16)
        GG(w2l)
        gpd_wout = S.n["GPD"]

        def w2sc(v):
            S.wait(v, "GPD", gpd_wout)
            ins = None
            for ec in range(16):
                ins = v.tensor_scalar(wout2[:, ec * D_:(ec + 1) * D_],
                                      wout2[:, ec * D_:(ec + 1) * D_],
                                      nwp_s[:, ec:ec + 1], None, ALU.mult)
            S.inc(ins, "VC")
        VV(w2sc)

        for c in range(NTC):
            par = c % 2
            def xtl(y, c=c, par=par):
                S.wait(y, "VC")
                for ec in range(8):
                    S.inc(y.dma_start(XT[par][:, ec * 128:(ec + 1) * 128],
                                      xa_sp[ec, :, c * Q:(c + 1) * Q],
                                      transpose=True), "SYD", 16)
            SY(xtl)

            def xtlb(sc, c=c, par=par):
                S.wait(sc, "VC")
                for ec in range(8, 16):
                    S.inc(sc.dma_start(XT[par][:, ec * 128:(ec + 1) * 128],
                                       xa_sp[ec, :, c * Q:(c + 1) * Q],
                                       transpose=True), "SCD", 16)
            SS(xtlb)
            syd_xt = (S.n["SYD"], S.n["SCD"])

            def btr(t, c=c):
                S.wait(t, "VC")
                S.inc(t.transpose(psB[:, 0:32].bitcast(BF16),
                                  bcma[0:64, c * Q:(c + 1) * Q],
                                  identb[0:64, 0:64]), "TC")
            TT(btr)

            def bev(v, c=c):
                S.wait(v, "TC")
                S.inc(v.tensor_copy(btp[:, :], psB[:, 0:32].bitcast(BF16)), "VC")
            VV(bev)

            def zb(v, c=c, par=par, syd_xt=syd_xt):
                S.wait(v, "SYD", syd_xt[0])
                S.wait(v, "SCD", syd_xt[1])
                S.inc(v.tensor_tensor(
                    zal[:, :].rearrange("p (h q) -> p h q", q=P),
                    XT[par][:, :].rearrange("p (h q) -> p h q", q=P),
                    w2TA[:, c * H:(c + 1) * H].unsqueeze(2)
                        .broadcast_to([128, H, P]), ALU.mult), "VC")
            VV(zb)

            def scm(t, c=c):
                S.wait(t, "VC")
                ins = None
                for i in range(4):
                    ins = t.matmul(psA[0:64, i * 512:(i + 1) * 512], btp[:, :],
                                   zal[:, i * 512:(i + 1) * 512], start=True,
                                   stop=True)
                S.inc(ins, "TC")
            TT(scm)

            def hup(v, c=c):
                S.wait(v, "TC")
                src, dst = HB[c % 2], HB[1 - c % 2]
                v.tensor_tensor(dst[:, :].rearrange("n (h q) -> n h q", q=P),
                                src[:, :].rearrange("n (h q) -> n h q", q=P),
                                lam64A[:, c * H:(c + 1) * H].unsqueeze(2)
                                    .broadcast_to([64, H, P]), ALU.mult)
                v.drain()
                S.inc(v.tensor_tensor(dst[:, :], dst[:, :], psA[0:64, :], ALU.add),
                      "VC")
            VV(hup)
        fence()
        if "Hinit" in dbg:
            def dh(y):
                S.inc(y.dma_start(dbg["Hinit"], HB[0][:, :]), "ST", 16)
            SY(dh)

        vc_hup = {}
        vc_resv = {}
        vc_yc = {}
        vc_bev = {}
        tc_ysm_map = {}
        sc_a3 = S.n["SC"]
        for c in range(NTC):
            par = c % 2
            hsrc, hdst = HB[c % 2], HB[1 - c % 2]

            def xtl(y, c=c, par=par):
                if c - 1 in vc_resv:
                    S.wait(y, "VC", vc_resv[c - 1])
                if c - 1 in tc_ysm_map:
                    S.wait(y, "TC", tc_ysm_map[c - 1])
                for ec in range(8):
                    S.inc(y.dma_start(XT[par][:, ec * 128:(ec + 1) * 128],
                                      x_sp[ec, :, c * Q:(c + 1) * Q],
                                      transpose=True), "SYD", 16)
                S.inc(y.dma_start(xck[:, :].rearrange("p (ec q) -> p ec q", q=Q),
                                  x_sp[:, :, c * Q:(c + 1) * Q].transpose([1, 0, 2])),
                      "SYD", 16)
                S.inc(y.dma_start(szt[:, :].rearrange("p (ec q) -> p ec q", q=Q),
                                  sz_sp[:, :, c * Q:(c + 1) * Q].transpose([1, 0, 2])),
                      "SYD", 16)
                S.inc(y.dma_start(xres[:, :], d["xin"][c * 128:(c + 1) * 128, :]),
                      "SYD", 16)
                S.inc(y.dma_start(cmt[:, :], bcm[64:128, c * Q:(c + 1) * Q]),
                      "SYD", 16)
            SY(xtl)

            def xtl2(sc, c=c, par=par):
                if c - 1 in vc_resv:
                    S.wait(sc, "VC", vc_resv[c - 1])
                if c - 1 in tc_ysm_map:
                    S.wait(sc, "TC", tc_ysm_map[c - 1])
                for ec in range(8, 16):
                    S.inc(sc.dma_start(XT[par][:, ec * 128:(ec + 1) * 128],
                                       x_sp[ec, :, c * Q:(c + 1) * Q],
                                       transpose=True), "SCD", 16)
            SS(xtl2)
            syd_xt = (S.n["SYD"], S.n["SCD"])

            def e2t(t, c=c):
                if c - 1 in vc_bev:
                    S.wait(t, "VC", vc_bev[c - 1])
                S.wait(t, "SC", sc_a3)
                ins = None
                for fc in range(16):
                    ins = t.matmul(psB[:, fc * 128:(fc + 1) * 128],
                                   c_ind[:, fc * 128:(fc + 1) * 128],
                                   erow[:, c * Q:(c + 1) * Q], start=True, stop=True)
                S.inc(ins, "TC")
            TT(e2t)

            def e2v(v, c=c):
                S.wait(v, "TC")
                if c - 1 in vc_yc:
                    S.wait(v, "VC", vc_yc[c - 1])
                S.inc(v.tensor_copy(e2sb[:, :], psB[:, :]), "VC")
            VV(e2v)
            vc_e2 = S.n["VC"]

            def dtar(v, c=c):
                S.inc(v.tensor_scalar(scr32[:, c * Q:(c + 1) * Q],
                                      dtv[:, c * Q:(c + 1) * Q], acol[:, 0:1], None,
                                      ALU.mult), "VC")
            VV(dtar)

            vc_dtar = S.n["VC"]

            def dtat(t, c=c, vc_dtar=vc_dtar):
                S.wait(t, "VC", vc_dtar)
                if c - 1 in vc_hup:
                    S.wait(t, "VC", vc_hup[c - 1])
                S.inc(t.transpose(psA[:, 0:32], scr32[:, c * Q:(c + 1) * Q],
                                  c_idf[0:32, 0:32]), "TC")
            TT(dtat)

            def dtae(v, c=c):
                S.wait(v, "TC")
                S.inc(v.tensor_copy(dab[:, :], psA[:, 0:32]), "VC")
            VV(dtae)

            def rhb(v, c=c):
                S.inc(v.tensor_tensor(
                    wq[:, :].rearrange("p (h q) -> p h q", q=Q),
                    c_umask[:, :].unsqueeze(1).broadcast_to([128, H, Q]),
                    dab[:, :].unsqueeze(2).broadcast_to([128, H, Q]),
                    ALU.mult), "VC")
            VV(rhb)
            vc_rh = S.n["VC"]

            sc_mde = [None]
            for hf in range(4):
                def cdf(t, c=c, hf=hf, vc_rh=vc_rh):
                    S.wait(t, "VC", vc_rh)
                    if hf >= 1:
                        S.wait(t, "SC", sc_mde[0])
                    elif c - 1 in vc_hup:
                        S.wait(t, "VC", vc_hup[c - 1])
                    ins = None
                    for hh in range(8):
                        h = hf * 8 + hh
                        ins = t.matmul(psA[:, hh * 128:(hh + 1) * 128],
                                       c_strict[:, :], wq[:, h * 128:(h + 1) * 128],
                                       start=True, stop=True)
                    S.inc(ins, "TC")
                TT(cdf)

                def mde(sc, c=c, hf=hf):
                    S.wait(sc, "TC")
                    S.inc(sc.activation(wq[:, hf * 1024:(hf + 1) * 1024],
                                        psA[:, 0:1024], ACT.Exp), "SC")
                SS(mde)
                sc_mde[0] = S.n["SC"]

            def gpm(t, c=c, vc_e2=vc_e2, syd_xt=syd_xt):
                S.wait(t, "VC", vc_e2)
                S.wait(t, "SYD", syd_xt[0])
                S.wait(t, "SCD", syd_xt[1])
                S.inc(t.matmul(psB[:, 0:128], bcm[0:64, c * Q:(c + 1) * Q],
                               cmt[:, :], start=True, stop=True), "TC")
            TT(gpm)

            def gpe(v, c=c):
                S.wait(v, "TC")
                S.inc(v.tensor_tensor(gpmb[:, :], psB[:, 0:128], c_umask[:, :],
                                      ALU.mult), "VC")
            VV(gpe)

            def wpb(v, c=c):
                S.wait(v, "SC", sc_mde[0])
                ins = None
                for hf in range(2):
                    ins = v.tensor_tensor(
                        wq[:, hf * E:(hf + 1) * E].rearrange("p (h q) -> p h q", q=Q),
                        wq[:, hf * E:(hf + 1) * E].rearrange("p (h q) -> p h q", q=Q),
                        gpmb[:, :].unsqueeze(1).broadcast_to([128, 16, Q]),
                        ALU.mult)
                S.inc(ins, "VC")
            VV(wpb)

            def xdt(v, c=c, par=par, syd_xt=syd_xt):
                S.wait(v, "SYD", syd_xt[0])
                S.wait(v, "SCD", syd_xt[1])
                S.inc(v.tensor_tensor(
                    zal[:, :].rearrange("p (h q) -> p h q", q=P),
                    XT[par][:, :].rearrange("p (h q) -> p h q", q=P),
                    dtvT[:, c * H:(c + 1) * H].unsqueeze(2)
                        .broadcast_to([128, H, P]), ALU.mult), "VC")
            VV(xdt)

            def hcast(v, c=c, hsrc=hsrc):
                S.inc(v.tensor_copy(g2b[0:64, :], hsrc[:, :]), "VC")
            VV(hcast)
            vc_xdt = S.n["VC"]

            def yim(t, c=c, vc_xdt=vc_xdt):
                S.wait(t, "VC", vc_xdt)
                ins = None
                for h in range(H):
                    fc, odd = h // 2, h % 2
                    ins = t.matmul(psB[odd * 64:odd * 64 + 64,
                                       fc * 128:(fc + 1) * 128],
                                   zal[:, h * P:(h + 1) * P],
                                   wq[:, h * 128:(h + 1) * 128],
                                   start=True, stop=True)
                S.inc(ins, "TC")
            TT(yim)

            def ysm(t, c=c):
                S.wait(t, "SC", sc_mde[0])
                if c - 1 in vc_resv:
                    S.wait(t, "VC", vc_resv[c - 1])
                ins = None
                for fc in range(16):
                    ins = t.matmul(psA[:, fc * 128:(fc + 1) * 128],
                                   g2b[0:64, fc * 128:(fc + 1) * 128],
                                   cmt[:, :], start=True, stop=True)
                S.inc(ins, "TC")
            TT(ysm)
            tc_y = S.n["TC"]
            tc_ysm_map[c] = tc_y

            def yc(v, c=c, tc_y=tc_y, syd_xt=syd_xt):
                S.wait(v, "TC", tc_y)
                S.wait(v, "SYD", syd_xt[0])
                S.wait(v, "SCD", syd_xt[1])
                v.tensor_tensor(ybuf[:, :], psA[:, :], e2sb[:, :], ALU.mult)
                v.drain()
                v.tensor_tensor(ybuf[:, :], ybuf[:, :], psB[:, :], ALU.add)
                v.tensor_tensor(gbuf[:, :].rearrange("p (fc q) -> p fc q", q=Q),
                                xck[:, :].rearrange("p (fc q) -> p fc q", q=Q),
                                dexp_s[:, :].unsqueeze(2)
                                    .broadcast_to([128, 16, Q]), ALU.mult)
                v.drain()
                S.inc(v.tensor_tensor(ybuf[:, :], ybuf[:, :], gbuf[:, :], ALU.add),
                      "VC")
            VV(yc)
            vc_y = S.n["VC"]
            vc_yc[c] = vc_y
            vc_yc[c] = vc_y
            if c == 0 and "y0" in dbg:
                def dy(y, vc_y=vc_y):
                    S.wait(y, "VC", vc_y)
                    S.inc(y.dma_start(dbg["y0"], ybuf[:, :]), "ST", 16)
                SY(dy)

            def zb2(v, c=c, par=par):
                S.inc(v.tensor_tensor(
                    zal[:, :].rearrange("p (h q) -> p h q", q=P),
                    XT[par][:, :].rearrange("p (h q) -> p h q", q=P),
                    w2T[:, c * H:(c + 1) * H].unsqueeze(2)
                        .broadcast_to([128, H, P]), ALU.mult), "VC")
            VV(zb2)

            def btr2(t, c=c):
                S.wait(t, "VC")
                S.inc(t.transpose(psB[:, 0:32].bitcast(BF16),
                                  bcm[0:64, c * Q:(c + 1) * Q],
                                  identb[0:64, 0:64]), "TC")
            TT(btr2)

            def bev2(v, c=c):
                S.wait(v, "TC")
                S.inc(v.tensor_copy(btp[:, :], psB[:, 0:32].bitcast(BF16)), "VC")
            VV(bev2)
            vc_bev[c] = S.n["VC"]

            def scm2(t, c=c):
                S.wait(t, "VC")
                if c - 1 in vc_resv:
                    S.wait(t, "VC", vc_resv[c - 1])
                ins = None
                for i in range(4):
                    ins = t.matmul(psA[0:64, i * 512:(i + 1) * 512], btp[:, :],
                                   zal[:, i * 512:(i + 1) * 512], start=True,
                                   stop=True)
                S.inc(ins, "TC")
            TT(scm2)

            def hup2(v, c=c, hsrc=hsrc, hdst=hdst):
                S.wait(v, "TC")
                v.tensor_tensor(hdst[:, :].rearrange("n (h q) -> n h q", q=P),
                                hsrc[:, :].rearrange("n (h q) -> n h q", q=P),
                                lam64[:, c * H:(c + 1) * H].unsqueeze(2)
                                    .broadcast_to([64, H, P]), ALU.mult)
                v.drain()
                S.inc(v.tensor_tensor(hdst[:, :], hdst[:, :], psA[0:64, :], ALU.add),
                      "VC")
            VV(hup2)
            vc_hup[c] = S.n["VC"]

            def gat(v, c=c):
                v.tensor_tensor(gbuf[:, :], ybuf[:, :], szt[:, :], ALU.mult)
                v.drain()
                S.inc(v.tensor_tensor(g2b[:, :], gbuf[:, :], gbuf[:, :], ALU.mult),
                      "VC")
            VV(gat)

            def rmsm(t, c=c):
                S.wait(t, "VC")
                S.wait(t, "VC", vc_hup[c])
                ins = None
                for fc in range(16):
                    ins = t.matmul(psA[0:1, 1024:1152], onescol[:, 0:1],
                                   g2b[:, fc * 128:(fc + 1) * 128],
                                   start=(fc == 0), stop=(fc == 15))
                S.inc(ins, "TC")
            TT(rmsm)

            def rmss(sc, c=c):
                S.wait(sc, "TC")
                sc.activation(rmsrow[0:1, 0:128], psA[0:1, 1024:1152], ACT.Ln,
                              bias=cvals[0:1, 0:1], scale=1.0 / E)
                S.inc(sc.activation(rmsrow[0:1, 128:256], rmsrow[0:1, 0:128],
                                    ACT.Exp, scale=-0.5), "SC")
            SS(rmss)

            def rim(t, c=c):
                S.wait(t, "SC")
                S.inc(t.matmul(psA[:, 1152:1280], ones1f[0:1, :],
                               rmsrow[0:1, 128:256], start=True, stop=True), "TC")
            TT(rim)

            def rie(v, c=c):
                S.wait(v, "TC")
                S.inc(v.tensor_copy(rib[:, :], psA[:, 1152:1280]), "VC")
            VV(rie)

            def ynbv(v, c=c):
                S.inc(v.tensor_tensor(
                    ynb[:, :].rearrange("p (fc q) -> p fc q", q=Q),
                    gbuf[:, :].rearrange("p (fc q) -> p fc q", q=Q),
                    rib[:, :].unsqueeze(1).broadcast_to([128, 16, Q]),
                    ALU.mult), "VC")
            VV(ynbv)
            vc_ynb = S.n["VC"]

            def wom(t, c=c, vc_ynb=vc_ynb):
                S.wait(t, "VC", vc_ynb)
                S.wait(t, "VC", vc_hup[c])
                ins = None
                for ek in range(16):
                    for hh in range(2):
                        ins = t.matmul(
                            psA[:, 1024 + hh * 512:1024 + (hh + 1) * 512],
                            ynb[:, ek * 128:(ek + 1) * 128],
                            wout2[:, ek * D_ + hh * 512: ek * D_ + (hh + 1) * 512],
                            start=(ek == 0), stop=(ek == 15))
                S.inc(ins, "TC")
            TT(wom)

            def resv(v, c=c):
                S.wait(v, "TC")
                S.wait(v, "SYD")
                S.inc(v.scalar_tensor_tensor(vbuf[:, :], xres[:, :], 2.0,
                                             psA[:, 1024:2048], ALU.mult, ALU.add),
                      "VC")
            VV(resv)
            vc_resv[c] = S.n["VC"]

            ln_chunk(vbuf[:, :], x1t[:, :])

            def x1s(y, c=c):
                S.wait(y, "VC")
                S.inc(y.dma_start(x1_sp[c * 128:(c + 1) * 128, :], x1t[:, :]),
                      "SYD", 16)
            SY(x1s)

        fence()
        if "Hend" in dbg:
            def dh2(y):
                S.inc(y.dma_start(dbg["Hend"], HB[0][:, :]), "ST", 16)
            SY(dh2)

    pop_to(m_bc)
    pop_to(m_ab)
    MACC = push("maccb", [128, NTC * D_], F32)
    x2bf = push("x2bf", [128, 8 * T], BF16, is_transpose_dest=True)
    m_d = mark()
    x2lo = push("x2lo", [128, 8 * T], BF16, is_transpose_dest=True)
    hib = push("hib", [128, D_], BF16)
    lob = push("lob", [128, D_], BF16)
    scoresF = push("scoresF", [32, T], F32)
    if True:
        def x1l(y):
            for tc in range(NTC):
                S.inc(y.dma_start(MACC[:, tc * D_:(tc + 1) * D_],
                                  x1_sp[tc * 128:(tc + 1) * 128, :]), "SYD", 16)
        SY(x1l)
        if "x1" in dbg:
            def dx1(y):
                S.wait(y, "SYD")
                S.inc(y.dma_start(dbg["x1"], MACC[:, :]), "ST", 16)
            SY(dx1)

        syd_tc = {}
        for tc in range(NTC):
            def hilo(v, tc=tc):
                S.wait(v, "SYD")
                S.wait(v, "SCD")
                v.tensor_copy(hib[:, :], MACC[:, tc * D_:(tc + 1) * D_])
                v.drain()
                S.inc(v.tensor_tensor(lob[:, :], MACC[:, tc * D_:(tc + 1) * D_],
                                      hib[:, :], ALU.subtract), "VC")
            VV(hilo)

            def htr(y, tc=tc):
                S.wait(y, "VC")
                for kd in range(4):
                    S.inc(y.dma_start(
                        x2bf[:, kd * T + tc * 128: kd * T + (tc + 1) * 128],
                        hib[:, kd * 128:(kd + 1) * 128], transpose=True),
                        "SYD", 16)
                    S.inc(y.dma_start(
                        x2lo[:, kd * T + tc * 128: kd * T + (tc + 1) * 128],
                        lob[:, kd * 128:(kd + 1) * 128], transpose=True),
                        "SYD", 16)
            SY(htr)

            def htr2(sc, tc=tc):
                S.wait(sc, "VC")
                for kd in range(4, 8):
                    S.inc(sc.dma_start(
                        x2bf[:, kd * T + tc * 128: kd * T + (tc + 1) * 128],
                        hib[:, kd * 128:(kd + 1) * 128], transpose=True),
                        "SCD", 16)
                    S.inc(sc.dma_start(
                        x2lo[:, kd * T + tc * 128: kd * T + (tc + 1) * 128],
                        lob[:, kd * 128:(kd + 1) * 128], transpose=True),
                        "SCD", 16)
            SS(htr2)
            syd_tc[tc] = (S.n["SYD"], S.n["SCD"])
        syd_hilo = S.n["SYD"]

        for tg in range(4):
            def rlm(t, tg=tg):
                S.wait(t, "SYD", syd_tc[4 * tg + 3][0])
                S.wait(t, "SCD", syd_tc[4 * tg + 3][1])
                if tg > 0:
                    S.wait(t, "SC")
                mms = []
                for kd in range(8):
                    for a_, b_ in [(gwh, x2bf), (gwl, x2bf), (gwh, x2lo)]:
                        mms.append((a_, b_, kd))
                ins = None
                for i, (a_, b_, kd) in enumerate(mms):
                    ins = t.matmul(psB[0:32, 0:512],
                                   a_[:, kd * H:(kd + 1) * H],
                                   b_[:, kd * T + tg * 512: kd * T + (tg + 1) * 512],
                                   start=(i == 0), stop=(i == len(mms) - 1))
                S.inc(ins, "TC")
            TT(rlm)

            def sig(sc, tg=tg):
                S.wait(sc, "TC")
                S.inc(sc.activation(scoresF[:, tg * 512:(tg + 1) * 512],
                                    psB[0:32, 0:512], ACT.Sigmoid), "SC")
            SS(sig)

        for tc in range(NTC):
            def str_(t, tc=tc):
                S.wait(t, "SC")
                S.wait(t, "VC")
                S.inc(t.transpose(psA[:, 0:32],
                                  scoresF[:, tc * 128:(tc + 1) * 128],
                                  c_idf[0:32, 0:32]), "TC")
            TT(str_)

            def rmath(v, tc=tc):
                S.wait(v, "TC")
                sco = scorestk[:, tc * H:(tc + 1) * H]
                v.tensor_copy(sco, psA[:, 0:32])
                sc_ = rscr[:, 0:32]
                t2s = rscr[:, 32:40]
                t2b = rscr2[:, 32:40]
                msk = rscr2[:, 0:32]
                v.drain()
                v.tensor_tensor(sc_, sco, c_eb[:, :], ALU.add)
                v.drain()
                pairs = [(0, 1), (0, 2), (0, 3), (1, 2), (1, 3), (2, 3)]
                for k_, (i_, j_) in enumerate(pairs):
                    dstp = t2s if k_ == 0 else t2b
                    v.tensor_tensor(dstp, sc_[:, i_:32:4], sc_[:, j_:32:4],
                                    ALU.add)
                    v.drain()
                    if k_ > 0:
                        v.tensor_tensor(t2s, t2s, t2b, ALU.max)
                        v.drain()
                v.tensor_copy(t2b, t2s)
                v.drain()
                for _ in range(3):
                    v.reduce_max(stat[:, 0:1], t2b, axis=AXX)
                    v.drain()
                    v.tensor_scalar(sqscr[:, 0:8], t2b, stat[:, 0:1], -1e30,
                                    ALU.is_ge, ALU.mult)
                    v.drain()
                    v.tensor_tensor(t2b, t2b, sqscr[:, 0:8], ALU.add)
                    v.drain()
                v.reduce_max(stat[:, 0:1], t2b, axis=AXX)
                v.drain()
                v.tensor_scalar(t2b, t2s, stat[:, 0:1], None, ALU.is_ge)
                v.drain()
                v.tensor_tensor(msk.rearrange("p (g j) -> p g j", j=4),
                                sc_.rearrange("p (g j) -> p g j", j=4),
                                t2b.unsqueeze(2)
                                   .broadcast_to([128, 8, 4]), ALU.mult)
                v.drain()
                v.tensor_copy(sqscr[:, 0:32], msk)
                v.drain()
                for _ in range(5):
                    v.reduce_max(stat[:, 0:1], sqscr[:, 0:32], axis=AXX)
                    v.drain()
                    v.tensor_scalar(sqscr[:, 32:64], sqscr[:, 0:32],
                                    stat[:, 0:1], -1e30, ALU.is_ge, ALU.mult)
                    v.drain()
                    v.tensor_tensor(sqscr[:, 0:32], sqscr[:, 0:32],
                                    sqscr[:, 32:64], ALU.add)
                    v.drain()
                v.reduce_max(stat[:, 0:1], sqscr[:, 0:32], axis=AXX)
                v.drain()
                v.tensor_scalar(msk, msk, stat[:, 0:1], None, ALU.is_ge)
                v.drain()
                v.tensor_tensor(msk, msk, sco, ALU.mult)
                v.drain()
                v.reduce_sum(stat[:, 1:2], msk, axis=AXX)
                v.drain()
                v.tensor_scalar(stat[:, 1:2], stat[:, 1:2], 1e-20, None, ALU.add)
                v.drain()
                v.reciprocal(stat[:, 2:3], stat[:, 1:2])
                v.drain()
                S.inc(v.tensor_scalar(combB[:, tc * H:(tc + 1) * H], msk,
                                      stat[:, 2:3], 2.5, ALU.mult, ALU.mult),
                      "VC")
            VV(rmath)

            def ctr(t, tc=tc):
                S.wait(t, "VC")
                S.inc(t.transpose(psA[0:32, 128:256], combB[:, tc * H:(tc + 1) * H],
                                  c_idf[:, :]), "TC")
            TT(ctr)

            def cte(v, tc=tc):
                S.wait(v, "TC")
                S.inc(v.tensor_copy(combTb[0:32, tc * 128:(tc + 1) * 128],
                                    psA[0:32, 128:256]), "VC")
            VV(cte)
        fence()
        if "scores" in dbg:
            def ds(y):
                S.inc(y.dma_start(dbg["scores"], scorestk[:, :]), "ST", 16)
            SY(ds)
        if "comb" in dbg:
            def dc(y):
                S.inc(y.dma_start(dbg["comb"], combB[:, :]), "ST", 16)
            SY(dc)

    pop_to(m_d)
    m_e = mark()
    wtu = push("wtu", [128, 4 * 8 * 512], BF16)
    wtd = push("wtd", [128, 4 * 4 * D_], BF16)
    heb = push("heb", [128, 4096], BF16)
    cbc4 = push("cbc4", [128, 1024], BF16)
    crow = push("crow", [1, 4 * T], BF16)
    if True:

        groups = [[0, 1, 2, 3], [4, 5, 6, 7], [8, 9, 10, 11], [12, 13, 14, 15],
                  [16, 17, 18, 19], [20, 21, 22, 23], [24, 25, 26, 27],
                  [28, 29, 30, 31], [32, 33, 34, 35]]

        def pf_expert(g, e, slot):
            S.inc(g.dma_start(
                wtu[:, slot * 4096:(slot + 1) * 4096]
                    .rearrange("p (k i) -> p k i", k=8),
                d["upT"][e].rearrange("(k p) i -> p k i", p=128)), "GPD", 16)
            S.inc(g.dma_start(
                wtd[:, slot * 4 * D_:(slot + 1) * 4 * D_]
                    .rearrange("p (k dd) -> p k dd", k=4),
                d["dnT"][e].rearrange("(k p) dd -> p k dd", p=128)), "GPD", 16)

        def pf0(g):
            for slot, e in enumerate(groups[0]):
                pf_expert(g, e, slot)
        GG(pf0)


        gpd_exp = {}
        for slot, e in enumerate(groups[0]):
            gpd_exp[(0, slot)] = None
        gpd_pf0 = S.n["GPD"]
        vc_ep = {}
        tc_dn = {}
        sc_sq = {}
        tc_up_map = {}
        vc_mev = {}
        tc_cbm = {}

        for gi, grp in enumerate(groups):
            def crl(y, grp=grp, gi=gi):
                S.wait(y, "TC")
                S.wait(y, "VC")
                for si, e in enumerate(grp):
                    S.inc(y.dma_start(crow[0:1, si * T:(si + 1) * T],
                                      combTb[e:e + 1, :]), "SYD", 16)
            SY(crl)
            syd_crow = S.n["SYD"]

            for tt in range(8):
                def cbm4(t, gi=gi, tt=tt, grp=grp, syd_crow=syd_crow):
                    S.wait(t, "SYD", syd_crow)
                    if (gi, tt - 1, 3) in vc_ep:
                        S.wait(t, "VC", vc_ep[(gi, tt - 1, 3)])
                    elif gi > 0 and (gi - 1, 7, 3) in vc_ep:
                        S.wait(t, "VC", vc_ep[(gi - 1, 7, 3)])
                    ins = None
                    for si in range(len(grp)):
                        ins = t.matmul(psB[:, si * 256:(si + 1) * 256],
                                       ones1b[0:1, :],
                                       crow[0:1, si * T + tt * 256:
                                            si * T + (tt + 1) * 256],
                                       start=True, stop=True)
                    S.inc(ins, "TC")
                TT(cbm4)
                tc_cbm[(gi, tt)] = S.n["TC"]

                def cbe4(v, gi=gi, tt=tt):
                    S.wait(v, "TC", tc_cbm[(gi, tt)])
                    S.inc(v.tensor_copy(cbc4[:, :], psB[:, 0:1024]), "VC")
                VV(cbe4)
                vc_cbe = S.n["VC"]

                vc_r = {}
                for si, e in enumerate(grp):
                    half = si % 2
                    def upm(t, e=e, tt=tt, si=si, gi=gi, half=half,
                            vc_cbe=vc_cbe):
                        g_ = gpd_exp.get((gi, si))
                        S.wait(t, "GPD", gpd_pf0 if g_ is None else g_)
                        if si < 2:
                            S.wait(t, "VC", vc_cbe)
                        for key in [(gi, tt, si - 2), (gi, tt - 1, si),
                                    (gi - 1, 7, si)]:
                            if key in sc_sq:
                                S.wait(t, "SC", sc_sq[key])
                                break
                        ins = None
                        for ic in range(4):
                            for kd in range(8):
                                ins = t.matmul(
                                    psB[:, half * 1024 + ic * 256:
                                        half * 1024 + (ic + 1) * 256],
                                    wtu[:, si * 4096 + kd * 512 + ic * 128:
                                        si * 4096 + kd * 512 + (ic + 1) * 128],
                                    x2bf[:, kd * T + tt * 256:
                                         kd * T + (tt + 1) * 256],
                                    start=(kd == 0), stop=(kd == 7))
                        S.inc(ins, "TC")
                    TT(upm)
                    tc_up = S.n["TC"]

                    tc_up_map[(gi, tt, si)] = tc_up

                def sqbody(sc, gi=gi, tt=tt, grp=grp):
                    for si in range(len(grp)):
                        half = si % 2
                        S.wait(sc, "TC", tc_up_map[(gi, tt, si)])
                        for key in [(gi, tt - 1, si), (gi - 1, 7, si)]:
                            if key in tc_dn:
                                S.wait(sc, "TC", tc_dn[key])
                                break
                        sc.activation(heb[:, si * 1024:(si + 1) * 1024],
                                      psB[:, half * 1024:(half + 1) * 1024],
                                      ACT.Relu)
                        S.inc(sc.activation(heb[:, si * 1024:(si + 1) * 1024],
                                            heb[:, si * 1024:(si + 1) * 1024],
                                            ACT.Square), "SC")
                        sc_sq[(gi, tt, si)] = S.n["SC"]
                SS(sqbody)

                for si, e in enumerate(grp):
                    def ep3v(v, si=si, gi=gi, tt=tt):
                        S.wait(v, "SC", sc_sq[(gi, tt, si)])
                        S.inc(v.tensor_tensor(
                            heb[:, si * 1024:(si + 1) * 1024]
                                .rearrange("p (ic q) -> p ic q", q=256),
                            heb[:, si * 1024:(si + 1) * 1024]
                                .rearrange("p (ic q) -> p ic q", q=256),
                            cbc4[:, si * 256:(si + 1) * 256].unsqueeze(1)
                                .broadcast_to([128, 4, 256]), ALU.mult), "VC")
                    VV(ep3v)
                    vc_ep[(gi, tt, si)] = S.n["VC"]

                    first = (si == 0)
                    last = (si == len(grp) - 1)
                    def dnm(t, e=e, tt=tt, si=si, gi=gi, first=first, last=last):
                        S.wait(t, "VC", vc_ep[(gi, tt, si)])
                        if first:
                            for key in [(gi, tt - 1), (gi - 1, 7)]:
                                if key in vc_mev:
                                    S.wait(t, "VC", vc_mev[key])
                                    break
                        ins = None
                        for tc2 in range(2):
                            for hh in range(2):
                                for ik in range(4):
                                    ins = t.matmul(
                                        psA[:, tc2 * 1024 + hh * 512:
                                            tc2 * 1024 + (hh + 1) * 512],
                                        heb[:, si * 1024 + ik * 256 + tc2 * 128:
                                            si * 1024 + ik * 256 + tc2 * 128 + 128],
                                        wtd[:, si * 4 * D_ + ik * D_ + hh * 512:
                                            si * 4 * D_ + ik * D_ + (hh + 1) * 512],
                                        start=(first and ik == 0),
                                        stop=(last and ik == 3))
                        S.inc(ins, "TC")
                    TT(dnm)
                    tc_dn[(gi, tt, si)] = S.n["TC"]

                    if tt == 7 and gi + 1 < len(groups) and si < len(groups[gi + 1]):
                        def pfn(g, gi=gi, si=si):
                            S.wait(g, "TC", tc_dn[(gi, 7, si)])
                            pf_expert(g, groups[gi + 1][si], si)
                        GG(pfn)
                        gpd_exp[(gi + 1, si)] = S.n["GPD"]

                def mev(v, tt=tt, gi=gi):
                    S.wait(v, "TC", tc_dn[(gi, tt, len(grp) - 1)])
                    ins = None
                    for tc2 in range(2):
                        tcg = tt * 2 + tc2
                        ins = v.tensor_tensor(
                            MACC[:, tcg * D_:(tcg + 1) * D_],
                            MACC[:, tcg * D_:(tcg + 1) * D_],
                            psA[:, tc2 * 1024:(tc2 + 1) * 1024], ALU.add)
                    S.inc(ins, "VC")
                VV(mev)
                vc_mev[(gi, tt)] = S.n["VC"]
        fence()
        if "macc" in dbg:
            def dm(y):
                S.inc(y.dma_start(dbg["macc"], MACC[:, :]), "ST", 16)
            SY(dm)

        for tc in range(NTC):
            ln_chunk(MACC[:, tc * D_:(tc + 1) * D_], MACC[:, tc * D_:(tc + 1) * D_])

            def out_dma(y, tc=tc):
                S.wait(y, "VC")
                S.inc(y.dma_start(yout[tc * 128:(tc + 1) * 128, :],
                                  MACC[:, tc * D_:(tc + 1) * D_]), "ST", 16)
            SY(out_dma)
            st_now = S.n["ST"]

            def stw(sc, st_now=st_now):
                S.wait(sc, "ST", st_now)
            SS(stw)

            def vtw(v, st_now=st_now):
                S.wait(v, "ST", st_now)
            VV(vtw)

        def final_wait(y):
            S.wait(y, "ST")
        SY(final_wait)

    blk.__exit__(None, None, None)
    pop_to(0)
    for g in reversed(guards):
        g.__exit__(None, None, None)
    return nc, dbg


def _prep_core_inputs(inputs, core):
    b, half = core >> 1, core & 1
    f32 = lambda a: np.ascontiguousarray(np.asarray(a, dtype=np.float32))
    x = f32(inputs["L0_attn_x"])
    W_in = f32(inputs["L0_attn_W_in"])
    conv_w = f32(inputs["L0_attn_conv_w"])
    conv_b = f32(inputs["L0_attn_conv_b"])
    lnw = f32(inputs["L0_attn_ln_w"]); lnb = f32(inputs["L0_attn_ln_b"])
    dtb = f32(inputs["L0_attn_dt_bias"]); alog = f32(inputs["L0_attn_A_log"])
    Dv = f32(inputs["L0_attn_D"]); nw = f32(inputs["L0_attn_norm_w"])
    W_out = f32(inputs["L0_attn_W_out"])
    gw = f32(inputs["L0_ffn_gate_w"]); eb = f32(inputs["L0_ffn_e_bias"])
    up = f32(inputs["L0_ffn_up_w"]); dn = f32(inputs["L0_ffn_down_w"])
    sup = f32(inputs["L0_ffn_shared_up_w"]); sdn = f32(inputs["L0_ffn_shared_down_w"])

    def convlay(w):
        return np.ascontiguousarray(w.reshape(NEC, 128, 4).transpose(1, 0, 2)
                                    .reshape(128, NEC * 4))

    def convblay(w):
        return np.ascontiguousarray(w.reshape(NEC, 128).T)

    upT = np.ascontiguousarray(np.concatenate(
        [up.transpose(0, 2, 1),
         np.ascontiguousarray(sup.T).reshape(D_, 4, 512).transpose(1, 0, 2)],
        axis=0))
    dnT = np.ascontiguousarray(np.concatenate(
        [dn.transpose(0, 2, 1),
         np.ascontiguousarray(sdn.T).reshape(4, 512, D_)], axis=0))

    if half == 1:
        xaux = np.ascontiguousarray(x[b, 0:T])
        winTa = np.ascontiguousarray(W_in[E:].T)
        lnwa, lnba = lnw, lnb
        cwa, cba = convlay(conv_w), convblay(conv_b)
        dtba, aloga = dtb, alog
    else:
        xaux = np.zeros((T, D_), np.float32)
        winTa = np.zeros((D_, CD + H), np.float32)
        lnwa = np.zeros(D_, np.float32); lnba = np.zeros(D_, np.float32)
        cwa = np.zeros((128, NEC * 4), np.float32)
        cba = np.zeros((128, NEC), np.float32)
        dtba = np.zeros(H, np.float32); aloga = np.zeros(H, np.float32)

    return {
        "xin": np.ascontiguousarray(x[b, half * T:(half + 1) * T]),
        "xaux": xaux,
        "winT": np.ascontiguousarray(W_in.T),
        "winTa": winTa,
        "lnrows": np.ascontiguousarray(
            np.concatenate([lnw, lnb, lnwa, lnba]).reshape(1, 4 * D_)),
        "cw": convlay(conv_w), "cb": convblay(conv_b),
        "cwa": cwa, "cba": cba,
        "dtb": dtb.reshape(H, 1), "alog": alog.reshape(H, 1),
        "dtba": dtba.reshape(H, 1), "aloga": aloga.reshape(H, 1),
        "dexp": np.ascontiguousarray(np.repeat(Dv, P).reshape(16, 128).T),
        "nwp": np.ascontiguousarray(nw.reshape(16, 128).T),
        "woutT": np.ascontiguousarray(W_out.T),
        "gwT": np.ascontiguousarray(gw.T),
        "ebias": eb.reshape(1, H),
        "upT": upT, "dnT": dnT,
    }


_CACHE = {}


def kernel(**inputs):
    if "prog" not in _CACHE:
        _CACHE["prog"] = build_program()
    nc, _ = _CACHE["prog"]
    in_maps = [_prep_core_inputs(inputs, c) for c in range(8)]
    res = run_bass_kernel_spmd(nc, in_maps, core_ids=list(range(8)))
    out = np.zeros((B_, L_, D_), np.float32)
    for c in range(8):
        b, half = c >> 1, c & 1
        out[b, half * T:(half + 1) * T] = np.asarray(res.results[c]["yout"])
    return out


if __name__ == "__main__":
    nc, _ = build_program()
    print("build ok, instructions:", nc.n_instructions())
